# version 02
# baseline (speedup 1.0000x reference)
import math
import numpy as np
import ml_dtypes
import concourse.bass as bass
import concourse.mybir as mybir
from concourse.alu_op_type import AluOpType as OP
from concourse.bass_utils import run_bass_kernel_spmd
from contextlib import ExitStack

F32 = mybir.dt.float32
BF16 = mybir.dt.bfloat16
I32 = mybir.dt.int32
AF = mybir.ActivationFunctionType
AX = mybir.AxisListType
bfd = ml_dtypes.bfloat16

D = 1024
T = 4096
NCH = 8
NT = 8
TT = 512
H = 16
DH = 64
NEG = -1.0e30



def _host_consts():
    c = {}
    c["ident"] = np.eye(128).astype(bfd)
    half = DH // 2
    theta = 1.0 / (10000.0 ** (np.arange(half, dtype=np.float64) * 2.0 / DH))
    ang = np.arange(T, dtype=np.float64)[None, :] * theta[:, None]
    c["cos64"] = np.repeat(np.cos(ang), 2, axis=0).astype(bfd)
    c["sin64"] = np.repeat(np.sin(ang), 2, axis=0).astype(bfd)
    P = np.zeros((128, 128), dtype=np.float32)
    for blk in range(2):
        o = 64 * blk
        for i in range(half):
            P[o + 2 * i + 1, o + 2 * i] = -1.0
            P[o + 2 * i, o + 2 * i + 1] = 1.0
    c["rope_perm"] = P.astype(bfd)
    NP_ = 220
    groups = []
    masks = []
    g = 0
    q0 = 0
    while q0 < T:
        q1 = min(q0 + 2 * NP_, T)
        f_lo = q0 // NP_
        kvisf = min(NP_ * (f_lo + 1), T)
        kvise = min(NP_ * (f_lo + 2), T)
        if q1 - q0 <= NP_:
            kvise = kvisf
        mlist = []
        for kt in range(kvisf // 128, (kvise + 127) // 128):
            m = np.zeros((128, 512), dtype=np.float32)
            for r in range(128):
                key = 128 * kt + r
                if key >= kvisf:
                    qsplit = NP_ * (f_lo + 1) - q0
                    m[r, :qsplit] = NEG
                    if key >= kvise:
                        m[r, :] = NEG
            mlist.append((kt, len(masks)))
            masks.append(m)
        groups.append((q0, q1, kvisf, kvise, mlist))
        q0 = q1
        g += 1
    c["l0_groups"] = groups
    c["l0_masks"] = (np.stack(masks, axis=0).astype(bfd) if masks
                     else np.zeros((1, 128, 512), dtype=bfd))
    kk, qq = np.meshgrid(np.arange(128), np.arange(512), indexing="ij")
    l1m = np.stack([np.where(128 * kd + kk > qq, NEG, 0.0) for kd in range(4)], axis=0)
    c["l1_masks"] = l1m.astype(bfd)
    PI = [0, 0, 0, 1, 1, 2]
    PJ = [1, 2, 3, 2, 3, 3]
    selI = np.zeros((64, 96), dtype=np.float32)
    selJ = np.zeros((64, 96), dtype=np.float32)
    G96 = np.zeros((96, 16), dtype=np.float32)
    E96 = np.zeros((16, 96), dtype=np.float32)
    for h in range(16):
        for cc in range(6):
            selI[4 * h + PI[cc], 6 * h + cc] = 1.0
            selJ[4 * h + PJ[cc], 6 * h + cc] = 1.0
            G96[6 * h + cc, h] = 1.0
            E96[h, 6 * h + cc] = 1.0
    c["selI"] = selI.astype(bfd)
    c["selJ"] = selJ.astype(bfd)
    c["G96"] = G96.astype(bfd)
    c["E96"] = E96.astype(bfd)
    J6 = np.array([[0, 0, 0, 0, 0, 1], [0, 0, 0, 0, -1, 0], [0, 0, 0, 1, 0, 0],
                   [0, 0, 1, 0, 0, 0], [0, -1, 0, 0, 0, 0], [1, 0, 0, 0, 0, 0]],
                  dtype=np.float32)
    Jb = np.zeros((96, 96), dtype=np.float32)
    for h in range(16):
        Jb[6 * h:6 * h + 6, 6 * h:6 * h + 6] = J6
    c["J96"] = Jb.astype(bfd)
    c["sel15"] = np.kron(np.eye(15), np.ones((1, 128))).astype(bfd)
    c["sel8"] = np.kron(np.eye(8), np.ones((1, 128))).astype(bfd)
    return c


HOST_CONSTS = _host_consts()
_DBG = {}


def _fchunks(v):
    v = np.asarray(v, dtype=np.float32)
    return np.ascontiguousarray(v.reshape(-1, 128).T)




def layer_inputs(l, inp):
    w = {}
    f32c = lambda a: np.ascontiguousarray(np.asarray(a), dtype=np.float32)
    if l == 0:
        w["wqkv"] = f32c(inp["L0_attn_Wqkv"])
        w["wproj"] = f32c(inp["L0_attn_Wproj"])
        w["bproj"] = _fchunks(inp["L0_attn_bproj"])
        w["qn_w"] = f32c(np.asarray(inp["L0_attn_qn_w"]).reshape(1, 64))
        w["kn_w"] = f32c(np.asarray(inp["L0_attn_kn_w"]).reshape(1, 64))
        w["router_W"] = f32c(inp["L0_ffn_router_W"])
        w["router_b"] = f32c(np.asarray(inp["L0_ffn_router_b"]).reshape(1, 15))
        w["sW1"] = f32c(inp["L0_ffn_sW1"])
        w["sb1"] = _fchunks(inp["L0_ffn_sb1"])
        w["sW2"] = f32c(inp["L0_ffn_sW2"])
        w["sb2"] = _fchunks(inp["L0_ffn_sb2"])
        w["eW1"] = f32c(inp["L0_ffn_eW1"])
        w["eb1"] = np.concatenate([_fchunks(b) for b in np.asarray(inp["L0_ffn_eb1"])], axis=1)
        w["eW2"] = f32c(inp["L0_ffn_eW2"])
        w["eb2"] = f32c(inp["L0_ffn_eb2"])
        w["c_cos"] = HOST_CONSTS["cos64"]
        w["c_sin"] = HOST_CONSTS["sin64"]
        w["c_rope"] = HOST_CONSTS["rope_perm"]
        w["c_l0m"] = HOST_CONSTS["l0_masks"]
        w["c_sel15"] = HOST_CONSTS["sel15"]
        w["c_ident"] = HOST_CONSTS["ident"]
    elif l == 1:
        w["wqkv"] = f32c(inp["L1_attn_Wqkv"])
        w["bqkv"] = _fchunks(np.asarray(inp["L1_attn_bqkv"])[:2048])
        w["bqkv_v"] = f32c(np.asarray(inp["L1_attn_bqkv"])[2048:].reshape(1, 1024))
        w["w1p"] = f32c(inp["L1_attn_W1p"])
        w["w2p"] = f32c(inp["L1_attn_W2p"])
        w["pl_scale"] = f32c(np.asarray(inp["L1_attn_plucker_scale"]).reshape(16, 1))
        w["wproj"] = f32c(inp["L1_attn_Wout"])
        w["bproj"] = _fchunks(inp["L1_attn_bout"])
        for i in range(1, 5):
            w[f"cn_w{i}"] = f32c(inp[f"L1_ffn_cn_w{i}"])
            w[f"cn_b{i}"] = (_fchunks(inp[f"L1_ffn_cn_b{i}"]) if i < 4 else
                             f32c(np.asarray(inp["L1_ffn_cn_b4"]).reshape(1, 1)))
        for i in range(1, 4):
            w[f"gc_w{i}"] = f32c(inp[f"L1_ffn_gc_w{i}"])
            w[f"gc_b{i}"] = (_fchunks(inp[f"L1_ffn_gc_b{i}"]) if i < 3 else
                             f32c(np.asarray(inp["L1_ffn_gc_b3"]).reshape(1, 1)))
        w["adrange"] = f32c(np.asarray(inp["L1_ffn_adaptive_range"]).reshape(1, 1))
        w["c_selI"] = HOST_CONSTS["selI"]
        w["c_selJ"] = HOST_CONSTS["selJ"]
        w["c_G96"] = HOST_CONSTS["G96"]
        w["c_E96"] = HOST_CONSTS["E96"]
        w["c_J96"] = HOST_CONSTS["J96"]
        w["c_l1m"] = HOST_CONSTS["l1_masks"]
    elif l == 2:
        w["wq"] = f32c(inp["L2_attn_Wq"])
        w["wk"] = f32c(inp["L2_attn_Wk"])
        w["wv"] = f32c(inp["L2_attn_Wv"])
        w["wproj"] = f32c(inp["L2_attn_Wo"])
        w["lq1"] = f32c(np.asarray(inp["L2_attn_lambda_q1"]).reshape(1, 64))
        w["lk1"] = f32c(np.asarray(inp["L2_attn_lambda_k1"]).reshape(1, 64))
        w["lq2"] = f32c(np.asarray(inp["L2_attn_lambda_q2"]).reshape(1, 64))
        w["lk2"] = f32c(np.asarray(inp["L2_attn_lambda_k2"]).reshape(1, 64))
        w["gamma"] = _fchunks(inp["L2_attn_gamma"])
        w["beta"] = _fchunks(inp["L2_attn_beta"])
        w["wg_t"] = f32c(np.asarray(inp["L2_ffn_w_gate"]).T)
        w["wu_t"] = f32c(np.asarray(inp["L2_ffn_w_up"]).T)
        w["wd_t"] = f32c(np.asarray(inp["L2_ffn_w_down"]).T)
    elif l == 3:
        w["wqkv"] = f32c(inp["L3_attn_w_qkv"])
        w["wproj"] = f32c(inp["L3_attn_w_out"])
        w["bproj"] = _fchunks(inp["L3_attn_b_out"])
        w["proj_f"] = f32c(inp["L3_attn_proj"])
        w["router_W"] = f32c(inp["L3_ffn_Wr"])
        w["router_b"] = f32c(np.asarray(inp["L3_ffn_br"]).reshape(1, 8))
        w["eW1"] = f32c(inp["L3_ffn_W1"])
        w["eb1"] = np.concatenate([_fchunks(b) for b in np.asarray(inp["L3_ffn_b1"])], axis=1)
        w["eW2"] = f32c(inp["L3_ffn_W2"])
        w["eb2"] = f32c(inp["L3_ffn_b2"])
        w["c_ident"] = HOST_CONSTS["ident"]
        w["c_sel8"] = HOST_CONSTS["sel8"]
    return w


DT_OVERRIDE = {}




class Sem:
    def __init__(self, h):
        self.h = h
        self.n = 0

    def inc(self, inst, k=1):
        inst.then_inc(self.h, k)
        self.n += k
        return inst


class Prog:
    def __init__(self, nc, stack):
        self.nc = nc
        self.stack = stack
        self._sems = {}
        self.block = None

    def sem(self, name):
        if name not in self._sems:
            self._sems[name] = Sem(self.stack.enter_context(self.nc.semaphore(name)))
        return self._sems[name]

    def ve(self, v, inst):
        s = self.sem("vchain")
        s.inc(inst)
        v.wait_ge(s.h, s.n)
        return inst

    def pair(self, name):
        if name + "__p" not in self._sems:
            self._sems[name + "__p"] = DPair(self.sem(name + "a"), self.sem(name + "b"))
        return self._sems[name + "__p"]


class DPair:

    def __init__(self, a, b):
        self.s = [a, b]

    def base(self):
        return (self.s[0].n, self.s[1].n)

    def inc(self, inst, k):
        self.s[k % 2].inc(inst, 16)
        return inst

    def wait_done(self, q, b, k):
        q.wait_ge(self.s[k % 2].h, b[k % 2] + 16 * (k // 2 + 1))

    def wait_all(self, q, n=None):
        q.wait_ge(self.s[0].h, self.s[0].n)
        q.wait_ge(self.s[1].h, self.s[1].n)

    def total(self):
        return (self.s[0].n, self.s[1].n)




def build_program(layers, host_shapes):
    nc = bass.Bass("TRN2", target_bir_lowering=False)
    stack = ExitStack()
    P = Prog(nc, stack)

    x_in = nc.dram_tensor("x", [D, T], F32, kind="ExternalInput").ap()
    x_out = nc.dram_tensor("xo", [D, T], F32, kind="ExternalOutput").ap()
    import os as _os
    if _os.environ.get("KDBG"):
        _DBG["xo2"] = nc.dram_tensor("xo2", [64, T], BF16, kind="ExternalOutput").ap()
    xmid = [nc.dram_tensor(f"xmid{i}", [D, T], F32, kind="Internal").ap()
            for i in range(2)]

    W = {}
    for l in layers:
        for name, shape in host_shapes[l].items():
            dt = DT_OVERRIDE.get(name, BF16 if name.startswith("c_") else F32)
            W[(l, name)] = nc.dram_tensor(f"L{l}_{name}", list(shape), dt,
                                          kind="ExternalInput").ap()

    B = {}
    B["qpack"] = nc.dram_tensor("qpack", [2048, T], BF16, kind="Internal").ap()
    B["kpack"] = nc.dram_tensor("kpack", [2048, T], BF16, kind="Internal").ap()
    B["vpack"] = nc.dram_tensor("vpack", [128, 32 * 1040], BF16, kind="Internal").ap()
    B["gbuf"] = nc.dram_tensor("gbuf", [D, T], F32, kind="Internal").ap()
    B["cwpack"] = nc.dram_tensor("cwpack", [16, T], BF16, kind="Internal").ap()
    B["sbb"] = nc.dram_tensor("sbb", [D, T], F32, kind="Internal").ap()
    B["hbuf"] = nc.dram_tensor("hbuf", [3 * D, T], BF16, kind="Internal").ap()
    B["jqpack"] = nc.dram_tensor("jqpack", [96, T], BF16, kind="Internal").ap()
    B["jkpack"] = nc.dram_tensor("jkpack", [96, T], BF16, kind="Internal").ap()

    sbt = lambda n, shp, dt=F32: stack.enter_context(nc.sbuf_tensor(n, shp, dt))
    B["xnb"] = sbt("xnb", [128, NCH * T], BF16)
    B["otb"] = sbt("otb", [128, NCH * T], BF16)
    B["ar3"] = sbt("ar3", [128, 6144], F32)
    B["rwb"] = sbt("rwb", [128, 128], BF16)
    B["cwtb"] = sbt("cwtb", [128, 16], BF16)
    B["identb"] = sbt("identb", [128, 128], BF16)
    B["sel15b"] = sbt("sel15b", [16, 1920], BF16)
    B["eb2bb"] = sbt("eb2bb", [16, 1024], BF16)
    B["cstb"] = sbt("cstb", [128, 8192], BF16)
    B["evb"] = sbt("evb", [128, 512], F32)
    B["evb2"] = sbt("evb2", [128, 512], F32)
    B["smallb"] = sbt("smallb", [128, 512], BF16)
    B["smallb2"] = sbt("smallb2", [128, 512], BF16)
    B["cwsb"] = sbt("cwsb", [16, T], BF16)
    B["onec"] = sbt("onec", [128, 1], BF16)
    B["oner"] = sbt("oner", [1, 128], BF16)
    B["rows"] = {n: sbt("row_" + n, [1, 512], F32) for n in
                 ["r1", "r2", "r3", "r4", "r5"]}
    B["rowsb"] = {}
    for n in ["b1", "b2", "b3", "b4"]:
        B["rowsb"][n] = sbt("rowb_" + n, [1, 512], BF16)
    for n in ["b5", "b6"]:
        B["rowsb"][n] = sbt("rowb_" + n, [1, 1024], BF16)
    B["bcol"] = [sbt(f"bcol{i}", [128, 1], F32) for i in range(4)]
    B["constc"] = sbt("constc", [128, 4], F32)
    ar3_ = B["ar3"]
    B["hr"] = {f"h{i+1}": ar3_[0:1, 1024 + 64 * i:1088 + 64 * i] for i in range(6)}
    B["rtr"] = ar3_[:, 2048:2176]
    B["r15"] = ar3_[0:1, 2112:2144]
    B["rbc"] = ar3_[:, 2144:2160]
    B["eb1c"] = ar3_[:, 2176:2304]
    B["biasA"] = ar3_[:, 2304:2336]
    otb_ = B["otb"]
    B["hrb"] = {f"b{i+1}": otb_[0:1, 7840 + 64 * i:7904 + 64 * i] for i in range(4)}
    B["r15b"] = otb_[0:1, 8096:8128]
    B["ps"] = [stack.enter_context(nc.psum_tensor(f"ps{i}", [128, 512], F32))
               for i in range(7)]
    B["psbf"] = stack.enter_context(nc.psum_tensor("psbf", [128, 512], BF16))

    P.block = stack.enter_context(nc.Block())
    blk = P.block
    with stack:
        s_init = P.sem("init")

        @blk.vector
        def _(v):
            v.memset(B["onec"][:], 1.0)
            v.memset(B["oner"][:], 1.0)
            v.memset(B["constc"][:, 0:1], 1e-5)
            v.memset(B["constc"][:, 1:2], 1e-6)
            s_init.inc(v.memset(B["constc"][:, 2:4], 0.0))

        n_layers = len(layers)
        for li, l in enumerate(layers):
            xi = x_in if li == 0 else xmid[li % 2]
            xo = x_out if li == n_layers - 1 else xmid[(li + 1) % 2]
            emit_layer(P, l, li, xi, xo, W, B)
    return nc


def emit_layer(P, l, li, xi, xo, W, B):
    emit_ln(P, l, li, xi, B)
    if l == 0:
        emit_ffn_l0(P, l, li, B, W)
        emit_qkv_l0(P, l, li, B, W)
        emit_attn_l0(P, l, li, B, W)
        emit_proj_update(P, l, li, xi, xo, B, W)
    elif l == 1:
        emit_ffn_l1(P, l, li, B, W)
        emit_qkv_l1(P, l, li, B, W)
        emit_attn_l1(P, l, li, B, W)
        emit_proj_update(P, l, li, xi, xo, B, W)
    elif l == 2:
        emit_qkv_l2(P, l, li, B, W)
        emit_ffn_l2(P, l, li, B, W)
        emit_attn_l2(P, l, li, B, W)
        emit_proj_update_l2(P, l, li, xi, xo, B, W)
    else:
        emit_ffn_l3(P, l, li, B, W)
        emit_qkv_l3(P, l, li, B, W)
        emit_attn_l3(P, l, li, B, W)
        emit_proj_update(P, l, li, xi, xo, B, W)



def emit_ln(P, l, li, xi, B):
    blk = P.block
    xnb, otb, ps = B["xnb"], B["otb"], B["ps"]
    rows, rowsb = B["rows"], B["rowsb"]
    onec, oner = B["onec"], B["oner"]
    smallb = B["smallb"]
    ssq, smm, srow, ssqr, shl, sbc, sap = (P.sem(f"ln_{n}_{li}") for n in
        ["sq", "mm", "row", "sqr", "hl", "bc", "ap"])
    sx = P.pair(f"ln_x_{li}")
    b = {s: s.n for s in [ssq, smm, srow, ssqr, shl, sbc, sap]}
    bx = sx.base()
    supd = P.pair(f"upd_done_{li-1}") if li > 0 else None
    supd_tot = supd.total() if supd else None

    @blk.gpsimd
    def _(g):
        if supd is not None:
            g.wait_ge(supd.s[0].h, supd_tot[0])
            g.wait_ge(supd.s[1].h, supd_tot[1])
        for tt in range(NT):
            if tt >= 2:
                g.wait_ge(ssq.h, b[ssq] + tt - 1)
            sx.inc(g.dma_start(
                out=otb[:, tt * 4096:(tt + 1) * 4096].rearrange("p (c t) -> p c t", c=8),
                in_=xi[:, tt * TT:(tt + 1) * TT].rearrange("(c p) t -> p c t", p=128)), tt)

    @blk.vector
    def _(v):
        for tt in range(NT):
            sx.wait_done(v, bx, tt)
            for c in range(NCH):
                inst = v.tensor_mul(out=xnb[:, c * T + tt * TT:c * T + tt * TT + TT],
                                    in0=otb[:, tt * 4096 + c * 512:tt * 4096 + c * 512 + 512],
                                    in1=otb[:, tt * 4096 + c * 512:tt * 4096 + c * 512 + 512])
            ssq.inc(inst)

    @blk.tensor
    def _(q):
        q.wait_ge(P.sem("init").h, 1)
        for tt in range(NT):
            q.wait_ge(ssq.h, b[ssq] + tt + 1)
            if tt >= 1:
                q.wait_ge(srow.h, b[srow] + tt)
            for c in range(NCH):
                q.matmul(out=ps[0][0:1, 0:TT], lhsT=onec[:, 0:1],
                         rhs=otb[:, tt * 4096 + c * 512:tt * 4096 + c * 512 + 512],
                         start=(c == 0), stop=(c == NCH - 1))
            for c in range(NCH):
                inst = q.matmul(out=ps[1][0:1, 0:TT], lhsT=onec[:, 0:1],
                                rhs=xnb[:, c * T + tt * TT:c * T + tt * TT + TT],
                                start=(c == 0), stop=(c == NCH - 1))
            smm.inc(inst)
            q.wait_ge(shl.h, b[shl] + tt + 1)
            if tt >= 1:
                q.wait_ge(sap.h, b[sap] + tt)
            q.matmul(out=ps[2][:, 0:TT], lhsT=oner[0:1, :],
                     rhs=rowsb["b1"][0:1, 0:TT], start=True, stop=False)
            q.matmul(out=ps[2][:, 0:TT], lhsT=oner[0:1, :],
                     rhs=rowsb["b2"][0:1, 0:TT], start=False, stop=True)
            q.matmul(out=ps[3][:, 0:TT], lhsT=oner[0:1, :],
                     rhs=rowsb["b3"][0:1, 0:TT], start=True, stop=False)
            inst = q.matmul(out=ps[3][:, 0:TT], lhsT=oner[0:1, :],
                            rhs=rowsb["b4"][0:1, 0:TT], start=False, stop=True)
            sbc.inc(inst)

    @blk.scalar
    def _(s):
        for tt in range(NT):
            s.wait_ge(srow.h, b[srow] + tt + 1)
            ssqr.inc(s.activation(out=rows["r2"][0:1, 0:TT], in_=rows["r1"][0:1, 0:TT],
                                  func=AF.Sqrt, bias=B["constc"][0:1, 0:1], scale=1.0))

    @blk.vector
    def _(v):
        for tt in range(NT):
            v.wait_ge(smm.h, b[smm] + tt + 1)
            P.ve(v, v.tensor_scalar(out=rows["r3"][0:1, 0:TT], in0=ps[0][0:1, 0:TT],
                            scalar1=1.0 / D, scalar2=None, op0=OP.mult))
            v.tensor_scalar(out=rows["r1"][0:1, 0:TT], in0=ps[1][0:1, 0:TT],
                            scalar1=1.0 / D, scalar2=None, op0=OP.mult)
            P.ve(v, v.tensor_mul(out=rows["r4"][0:1, 0:TT], in0=rows["r3"][0:1, 0:TT],
                         in1=rows["r3"][0:1, 0:TT]))
            srow.inc(v.tensor_sub(out=rows["r1"][0:1, 0:TT], in0=rows["r1"][0:1, 0:TT],
                                  in1=rows["r4"][0:1, 0:TT]))
            v.wait_ge(ssqr.h, b[ssqr] + tt + 1)
            P.ve(v, v.reciprocal(out=rows["r4"][0:1, 0:TT], in_=rows["r2"][0:1, 0:TT]))
            P.ve(v, v.tensor_mul(out=rows["r5"][0:1, 0:TT], in0=rows["r3"][0:1, 0:TT],
                         in1=rows["r4"][0:1, 0:TT]))
            P.ve(v, v.tensor_copy(out=rowsb["b1"][0:1, 0:TT], in_=rows["r4"][0:1, 0:TT]))
            P.ve(v, v.tensor_sub(out=rows["r3"][0:1, 0:TT], in0=rows["r4"][0:1, 0:TT],
                         in1=rowsb["b1"][0:1, 0:TT]))
            P.ve(v, v.tensor_copy(out=rowsb["b2"][0:1, 0:TT], in_=rows["r3"][0:1, 0:TT]))
            P.ve(v, v.tensor_copy(out=rowsb["b3"][0:1, 0:TT], in_=rows["r5"][0:1, 0:TT]))
            P.ve(v, v.tensor_sub(out=rows["r3"][0:1, 0:TT], in0=rows["r5"][0:1, 0:TT],
                         in1=rowsb["b3"][0:1, 0:TT]))
            shl.inc(v.tensor_copy(out=rowsb["b4"][0:1, 0:TT], in_=rows["r3"][0:1, 0:TT]))
            v.wait_ge(sbc.h, b[sbc] + tt + 1)
            t0 = tt * TT
            for c in range(NCH):
                P.ve(v, v.tensor_mul(out=smallb[:, 0:TT],
                             in0=otb[:, tt * 4096 + c * 512:tt * 4096 + c * 512 + 512],
                             in1=ps[2][:, 0:TT]))
                inst = v.tensor_sub(out=xnb[:, c * T + t0:c * T + t0 + TT],
                                    in0=smallb[:, 0:TT], in1=ps[3][:, 0:TT])
            sap.inc(inst)



def emit_ffn_l0(P, l, li, B, W):
    _emit_ffn_moe(P, l, li, B, W, E=15, shared=True)


def _emit_ffn_moe(P, l, li, B, W, E, shared, HID_CH=8):
    blk = P.block
    xnb, otb, ps = B["xnb"], B["otb"], B["ps"]
    evb, evb2, smallb = B["evb"], B["evb2"], B["smallb"]
    cwsb, oner, psbf = B["cwsb"], B["oner"], B["psbf"]
    rtr = B["rtr"]
    gbuf = B["gbuf"]
    heb = B["cstb"]
    hsb_off = 16384
    sap = P.sem(f"ln_ap_{li}")
    sap_done = sap.n
    selb = B["sel15b"]
    selname = "c_sel15" if E == 15 else "c_sel8"

    swr = P.sem(f"f_wr_{li}")
    sbb = P.sem(f"f_bb_{li}")
    srbc = P.sem(f"f_rbc_{li}")
    srl = P.sem(f"f_rl_{li}")
    srx = P.sem(f"f_rx_{li}")
    srexp = P.sem(f"f_rexp_{li}")
    srd = P.sem(f"f_rd_{li}")
    srt = P.sem(f"f_rt_{li}")
    sre = P.sem(f"f_re_{li}")
    sws = P.sem(f"f_ws_{li}")
    shm = P.sem(f"f_hm_{li}")
    she = P.sem(f"f_he_{li}")
    shd = P.sem(f"f_hd_{li}")
    sym = P.sem(f"f_ym_{li}")
    sye = P.sem(f"f_ye_{li}")
    sgw = P.pair(f"f_gw_{li}")
    sad = P.pair(f"f_ad_{li}")
    swe1 = P.sem(f"f_we1_{li}")
    swe2 = P.sem(f"f_we2_{li}")
    scb = P.sem(f"f_cb_{li}")
    sfb = P.sem(f"f_fb_{li}")
    sfl = P.pair(f"f_fl_{li}")
    sfd = P.sem(f"f_fd_{li}")
    sfe = P.sem(f"f_fe_{li}")
    sfw = P.pair(f"f_fw_{li}")
    sems = [swr, sbb, srbc, srl, srx, srexp, srd, srt, sre, sws, shm, she, shd,
            sym, sye, swe1, swe2, scb, sfb, sfd, sfe]
    b = {s: s.n for s in sems}
    bgw, bad, bfl, bfw = sgw.base(), sad.base(), sfl.base(), sfw.base()
    NTC = 32
    NSH = NCH * NT if shared else 0
    HSTEPS = HID_CH
    wqcols = 1024 * (HID_CH // 8)

    @blk.gpsimd
    def _(g):
        g.wait_ge(sap.h, sap_done)
        swr.inc(g.dma_start(out=B["eb1c"][:, 0:E * HID_CH], in_=W[(l, "eb1")]), 16)
        swr.inc(g.dma_start(out=B["identb"][:], in_=W[(l, "c_ident")]), 16)
        swr.inc(g.dma_start(out=B["eb2bb"][0:E, 0:1024],
                            in_=W[(l, "eb2")]), 16)
        swr.inc(g.dma_start(out=B["rwb"][:, 0:8 * E].rearrange("p (k e) -> p k e", k=8),
                            in_=W[(l, "router_W")].rearrange("(k p) e -> p k e", p=128)), 16)
        if shared:
            swr.inc(g.dma_start(out=B["biasA"][:, 0:8], in_=W[(l, "sb1")]), 16)
            swr.inc(g.dma_start(out=B["biasA"][:, 8:16], in_=W[(l, "sb2")]), 16)
        swr.inc(g.dma_start(out=B["sel15b"][0:E, 0:E * 128], in_=W[(l, selname)]), 16)
        swr.inc(g.dma_start(out=B["r15"][0:1, 0:E], in_=W[(l, "router_b")]), 16)
        NPRE = 8 if shared else 6
        if shared:
            sws.inc(g.dma_start(out=otb[:, 0:8192].rearrange("p (k m) -> p k m", k=8),
                                in_=W[(l, "sW1")].rearrange("(k p) m -> p k m", p=128)), 16)
            sws.inc(g.dma_start(out=otb[:, 8192:16384].rearrange("p (k m) -> p k m", k=8),
                                in_=W[(l, "sW2")].rearrange("(k p) m -> p k m", p=128)), 16)
        NYS = NCH * NT

        def emit_accum(ea, step):
            gstep = ea * NYS + step
            tt, oc = divmod(step, NCH)
            g.wait_ge(sye.h, b[sye] + NSH + gstep + 1)
            if ea == 0 and step == 0 and shared:
                g.wait_ge(sgw.s[0].h, bgw[0] + 16 * (NSH // 2))
                g.wait_ge(sgw.s[1].h, bgw[1] + 16 * (NSH // 2))
            kw = {} if (not shared and ea == 0) else {"accum_op": OP.add}
            sad.inc(g.dma_start(
                out=gbuf[oc * 128:(oc + 1) * 128, tt * TT:(tt + 1) * TT],
                in_=evb[:, 0:TT] if (NSH + gstep) % 2 == 0 else evb2[:, 0:TT],
                **kw), gstep)

        def emit_w1(e):
            eW1 = W[(l, "eW1")][e]
            if e >= 1:
                g.wait_ge(shm.h, b[shm] + NSH + e * NT * HSTEPS)
            elif shared:
                g.wait_ge(sym.h, b[sym] + NSH)
            if HID_CH == 8:
                swe1.inc(g.dma_start(
                    out=otb[:, 16384:24576].rearrange("p (k m) -> p k m", k=8),
                    in_=eW1.rearrange("(k p) m -> p k m", p=128)), 16)
            else:
                swe1.inc(g.dma_start(
                    out=otb[:, 0:16384].rearrange("p (k m) -> p k m", k=8),
                    in_=eW1.rearrange("(k p) m -> p k m", p=128)), 16)

        def emit_w2(e):
            eW2 = W[(l, "eW2")][e]
            g.wait_ge(sym.h, b[sym] + NSH + e * NYS)
            if HID_CH == 8:
                swe2.inc(g.dma_start(
                    out=otb[:, 24576:32768].rearrange("p (k m) -> p k m", k=8),
                    in_=eW2.rearrange("(k p) m -> p k m", p=128)), 16)
            else:
                swe2.inc(g.dma_start(
                    out=otb[:, 16384:32768].rearrange("p (k m) -> p k m", k=16),
                    in_=eW2.rearrange("(k p) m -> p k m", p=128)), 16)

        emit_w1(0)
        emit_w2(0)
        for e in range(E):
            for step in range(NYS - 10):
                emit_accum(e, step)
            if e + 1 < E:
                emit_w1(e + 1)
            for step in range(NYS - 10, NYS - 2):
                emit_accum(e, step)
            if e + 1 < E:
                emit_w2(e + 1)
            for step in range(NYS - 2, NYS):
                emit_accum(e, step)

    NPRE_W = 8 if shared else 6

    @blk.vector
    def _(v):
        v.wait_ge(swr.h, b[swr] + 16 * NPRE_W)
        P.ve(v, v.tensor_copy(out=B["r15b"][0:1, 0:E], in_=B["r15"][0:1, 0:E]))
        P.ve(v, v.tensor_sub(out=B["r15"][0:1, 16:16 + E], in0=B["r15"][0:1, 0:E],
                     in1=B["r15b"][0:1, 0:E]))
        srd.inc(v.tensor_copy(out=B["r15b"][0:1, 16:16 + E], in_=B["r15"][0:1, 16:16 + E]))

    @blk.tensor
    def _(q):
        q.wait_ge(srd.h, b[srd] + 1)
        q.matmul(out=ps[5][:, 0:E], lhsT=oner[0:1, :], rhs=B["r15b"][0:1, 0:E],
                 start=True, stop=False)
        sbb.inc(q.matmul(out=ps[5][:, 0:E], lhsT=oner[0:1, :], rhs=B["r15b"][0:1, 16:16 + E],
                         start=False, stop=True))

    @blk.vector
    def _(v):
        v.wait_ge(sbb.h, b[sbb] + 1)
        srbc.inc(v.tensor_copy(out=B["rbc"][:, 0:E], in_=ps[5][:, 0:E]))

    @blk.tensor
    def _(q):
        q.wait_ge(srbc.h, b[srbc] + 1)
        for tc in range(NTC):
            q.wait_ge(sap.h, sap_done - NT + tc // 4 + 1)
            if tc >= 1:
                q.wait_ge(srx.h, b[srx] + tc)
            for c in range(NCH):
                inst = q.matmul(out=ps[4][:, 0:E],
                                lhsT=xnb[:, c * T + tc * 128:c * T + tc * 128 + 128],
                                rhs=B["rwb"][:, c * E:(c + 1) * E],
                                start=(c == 0), stop=(c == NCH - 1))
            srl.inc(inst)
            q.wait_ge(srd.h, b[srd] + 1 + tc + 1)
            if tc >= 1:
                q.wait_ge(sre.h, b[sre] + tc)
            srt.inc(q.matmul(out=ps[5][0:E, 0:128], lhsT=B["cwtb"][:, 0:E],
                             rhs=B["identb"][:, :], start=True, stop=True))

    @blk.vector
    def _(v):
        K = 3 if E == 15 else 2
        for tc in range(NTC):
            v.wait_ge(srl.h, b[srl] + tc + 1)
            if tc >= 1:
                v.wait_ge(srexp.h, b[srexp] + tc)
            P.ve(v, v.tensor_add(out=rtr[:, 0:E], in0=ps[4][:, 0:E], in1=B["rbc"][:, 0:E]))
            srx.inc(v.reduce_max(out=rtr[:, 16:17], in_=rtr[:, 0:E], axis=AX.X,
                                 negate=True))
            v.wait_ge(srexp.h, b[srexp] + tc + 1)
            P.ve(v, v.reduce_sum(out=rtr[:, 17:18], in_=rtr[:, 22:22 + E], axis=AX.X))
            P.ve(v, v.reciprocal(out=rtr[:, 18:19], in_=rtr[:, 17:18]))
            P.ve(v, v.tensor_scalar(out=rtr[:, 38:38 + E], in0=rtr[:, 22:22 + E],
                            scalar1=rtr[:, 18:19], scalar2=None, op0=OP.mult))
            P.ve(v, v.reduce_max(out=rtr[:, 19:20], in_=rtr[:, 38:38 + E], axis=AX.X))
            P.ve(v, v.scalar_tensor_tensor(out=rtr[:, 22:22 + E], in0=rtr[:, 38:38 + E],
                                   scalar=rtr[:, 19:20], in1=rtr[:, 38:38 + E],
                                   op0=OP.is_lt, op1=OP.mult))
            P.ve(v, v.reduce_max(out=rtr[:, 20:21], in_=rtr[:, 22:22 + E], axis=AX.X))
            if K == 3:
                P.ve(v, v.scalar_tensor_tensor(out=rtr[:, 54:54 + E], in0=rtr[:, 22:22 + E],
                                       scalar=rtr[:, 20:21], in1=rtr[:, 22:22 + E],
                                       op0=OP.is_lt, op1=OP.mult))
                P.ve(v, v.reduce_max(out=rtr[:, 21:22], in_=rtr[:, 54:54 + E], axis=AX.X))
            else:
                P.ve(v, v.tensor_copy(out=rtr[:, 21:22], in_=rtr[:, 20:21]))
            P.ve(v, v.scalar_tensor_tensor(out=rtr[:, 70:70 + E], in0=rtr[:, 38:38 + E],
                                   scalar=rtr[:, 21:22], in1=rtr[:, 38:38 + E],
                                   op0=OP.is_ge, op1=OP.mult))
            srd.inc(v.tensor_copy(out=B["cwtb"][:, 0:E], in_=rtr[:, 70:70 + E]))
            v.wait_ge(srt.h, b[srt] + tc + 1)
            sre.inc(v.tensor_copy(out=cwsb[0:E, tc * 128:(tc + 1) * 128],
                                  in_=ps[5][0:E, 0:128]))

    @blk.scalar
    def _(s):
        for tc in range(NTC):
            s.wait_ge(srx.h, b[srx] + tc + 1)
            srexp.inc(s.activation(out=rtr[:, 22:22 + E], in_=rtr[:, 0:E], func=AF.Exp,
                                   bias=rtr[:, 16:17], scale=1.0))


    @blk.tensor
    def _(q):
        if shared:
            q.wait_ge(sws.h, b[sws] + 32)
            for tt in range(NT):
                t0 = tt * TT
                for hc in range(NCH):
                    step = tt * NCH + hc
                    pb = step % 2
                    if step >= 2:
                        q.wait_ge(she.h, b[she] + step - 1)
                    for c in range(NCH):
                        inst = q.matmul(
                            out=ps[pb][:, 0:TT],
                            lhsT=otb[:, c * 1024 + hc * 128:c * 1024 + hc * 128 + 128],
                            rhs=xnb[:, c * T + t0:c * T + t0 + TT],
                            start=(c == 0), stop=(c == NCH - 1))
                    shm.inc(inst)
                for oc in range(NCH):
                    step = tt * NCH + oc
                    pb = 2 + step % 2
                    q.wait_ge(she.h, b[she] + tt * NCH + NCH)
                    if step >= 2:
                        q.wait_ge(sye.h, b[sye] + step - 1)
                    for hc in range(NCH):
                        inst = q.matmul(
                            out=ps[pb][:, 0:TT],
                            lhsT=otb[:, 8192 + hc * 1024 + oc * 128:8192 + hc * 1024 + oc * 128 + 128],
                            rhs=otb[:, hsb_off + hc * TT:hsb_off + (hc + 1) * TT],
                            start=(hc == 0), stop=(hc == NCH - 1))
                    sym.inc(inst)
        NYS = NCH * NT
        w1_0 = 16384 if HID_CH == 8 else 0
        w2_0 = 24576 if HID_CH == 8 else 16384
        for e in range(E):
            q.wait_ge(swe1.h, b[swe1] + 16 * (e + 1))
            for tt in range(NT):
                t0 = tt * TT
                et = e * NT + tt
                q.wait_ge(sre.h, b[sre] + min(NTC, 4 * (tt + 1)))
                if et >= 1:
                    q.wait_ge(shd.h, b[shd] + (et - 1) * HSTEPS + HSTEPS)
                scb.inc(q.matmul(out=ps[6][:, 0:TT], lhsT=selb[0:E, e * 128:(e + 1) * 128],
                                 rhs=cwsb[0:E, t0:t0 + TT], start=True, stop=True))
                for hc in range(HID_CH):
                    step = et * HSTEPS + hc
                    pb = step % 2
                    if NSH + step >= 2:
                        q.wait_ge(she.h, b[she] + NSH + step - 1)
                    for c in range(NCH):
                        inst = q.matmul(
                            out=ps[pb][:, 0:TT],
                            lhsT=otb[:, w1_0 + c * (HID_CH * 128) + hc * 128:
                                     w1_0 + c * (HID_CH * 128) + hc * 128 + 128],
                            rhs=xnb[:, c * T + t0:c * T + t0 + TT],
                            start=(c == 0), stop=(c == NCH - 1))
                    shm.inc(inst)
                if tt == 0:
                    q.wait_ge(swe2.h, b[swe2] + 16 * (e + 1))
                for oc in range(NCH):
                    step = et * NCH + oc
                    pb = 2 + step % 2
                    q.wait_ge(shd.h, b[shd] + et * HSTEPS + HSTEPS)
                    if NSH + step >= 2:
                        q.wait_ge(sye.h, b[sye] + NSH + step - 1)
                    for hc in range(HID_CH):
                        inst = q.matmul(
                            out=ps[pb][:, 0:TT],
                            lhsT=otb[:, w2_0 + hc * 1024 + oc * 128:
                                     w2_0 + hc * 1024 + oc * 128 + 128],
                            rhs=heb[:, hc * TT:(hc + 1) * TT] if HID_CH == 8 else
                                heb[:, hc * TT:(hc + 1) * TT],
                            start=(hc == 0), stop=(hc == HID_CH - 1))
                    sym.inc(inst)

    @blk.scalar
    def _(s):
        if shared:
            for tt in range(NT):
                for hc in range(NCH):
                    step = tt * NCH + hc
                    pb = step % 2
                    s.wait_ge(shm.h, b[shm] + step + 1)
                    she.inc(s.activation(out=otb[:, hsb_off + hc * TT:hsb_off + (hc + 1) * TT],
                                         in_=ps[pb][:, 0:TT], func=AF.Relu,
                                         bias=B["biasA"][:, hc:hc + 1], scale=1.0))
                for oc in range(NCH):
                    step = tt * NCH + oc
                    pb = 2 + step % 2
                    s.wait_ge(sym.h, b[sym] + step + 1)
                    if step >= 2:
                        sgw.wait_done(s, bgw, step - 2)
                    sye.inc(s.copy(out=evb[:, 0:TT] if step % 2 == 0 else evb2[:, 0:TT],
                                   in_=ps[pb][:, 0:TT]))
        for e in range(E):
            for tt in range(NT):
                et = e * NT + tt
                for hc in range(HID_CH):
                    step = et * HSTEPS + hc
                    pb = step % 2
                    s.wait_ge(shm.h, b[shm] + NSH + step + 1)
                    if step >= 1:
                        s.wait_ge(shd.h, b[shd] + step)
                    she.inc(s.activation(out=smallb[:, 0:TT], in_=ps[pb][:, 0:TT],
                                         func=AF.Relu,
                                         bias=B["eb1c"][:, e * HID_CH + hc:e * HID_CH + hc + 1],
                                         scale=1.0))
                for oc in range(NCH):
                    step = et * NCH + oc
                    pb = 2 + step % 2
                    s.wait_ge(sym.h, b[sym] + NSH + step + 1)
                    if step >= 2:
                        sad.wait_done(s, bad, step - 2)
                    elif shared:
                        s.wait_ge(sgw.s[0].h, bgw[0] + 16 * (NSH // 2))
                        s.wait_ge(sgw.s[1].h, bgw[1] + 16 * (NSH // 2))
                    sye.inc(s.copy(out=evb[:, 0:TT] if (NSH + step) % 2 == 0 else evb2[:, 0:TT],
                                   in_=ps[pb][:, 0:TT]))

    @blk.vector
    def _(v):
        for e in range(E):
            for tt in range(NT):
                et = e * NT + tt
                v.wait_ge(scb.h, b[scb] + et + 1)
                if et >= 1:
                    v.wait_ge(sym.h, b[sym] + NSH + et * NCH)
                for hc in range(HID_CH):
                    step = et * HSTEPS + hc
                    v.wait_ge(she.h, b[she] + NSH + step + 1)
                    shd.inc(v.tensor_mul(out=heb[:, hc * TT:(hc + 1) * TT],
                                         in0=smallb[:, 0:TT], in1=ps[6][:, 0:TT]))

    @blk.sync
    def _(sy):
        if shared:
            for tt in range(NT):
                for oc in range(NCH):
                    step = tt * NCH + oc
                    sy.wait_ge(sye.h, b[sye] + step + 1)
                    sgw.inc(sy.dma_start(
                        out=gbuf[oc * 128:(oc + 1) * 128, tt * TT:(tt + 1) * TT],
                        in_=evb[:, 0:TT] if step % 2 == 0 else evb2[:, 0:TT]), step)

    NYS = NCH * NT
    n_acc = E * NYS
    all_accum = (bad[0] + 16 * ((n_acc + 1) // 2), bad[1] + 16 * (n_acc // 2))

    @blk.tensor
    def _(q):
        for tt in range(NT):
            for oc in range(NCH):
                step = tt * NCH + oc
                pb = step % 2
                if step >= 2:
                    q.wait_ge(sfd.h, b[sfd] + step - 1)
                sfb.inc(q.matmul(out=ps[4 + pb][:, 0:TT],
                                 lhsT=B["eb2bb"][0:E, oc * 128:(oc + 1) * 128],
                                 rhs=cwsb[0:E, tt * TT:(tt + 1) * TT],
                                 start=True, stop=True))

    @blk.sync
    def _(sy):
        sy.wait_ge(sad.s[0].h, all_accum[0])
        sy.wait_ge(sad.s[1].h, all_accum[1])
        sy.wait_ge(scb.h, b[scb] + E * NT)
        for tt in range(NT):
            for oc in range(NCH):
                step = tt * NCH + oc
                pb = step % 2
                if step >= 2:
                    sy.wait_ge(sfe.h, b[sfe] + step - 1)
                sfl.inc(sy.dma_start(
                    out=B["ar3"][:, 0:512] if pb == 0 else B["ar3"][:, 512:1024],
                    in_=gbuf[oc * 128:(oc + 1) * 128, tt * TT:(tt + 1) * TT]), step)

    @blk.vector
    def _(v):
        for tt in range(NT):
            for oc in range(NCH):
                step = tt * NCH + oc
                pb = step % 2
                sfl.wait_done(v, bfl, step)
                v.wait_ge(sfb.h, b[sfb] + step + 1)
                sfd.inc(v.tensor_add(
                    out=B["ar3"][:, 0:512] if pb == 0 else B["ar3"][:, 512:1024],
                    in0=B["ar3"][:, 0:512] if pb == 0 else B["ar3"][:, 512:1024],
                    in1=ps[4 + pb][:, 0:TT]))

    @blk.scalar
    def _(s):
        for tt in range(NT):
            for oc in range(NCH):
                step = tt * NCH + oc
                pb = step % 2
                s.wait_ge(sfd.h, b[sfd] + step + 1)
                if step >= 2:
                    sfw.wait_done(s, bfw, step - 2)
                kwb = {"bias": B["biasA"][:, 8 + oc:8 + oc + 1]} if shared else {}
                sfe.inc(s.activation(
                    out=evb[:, 0:TT] if pb == 0 else evb2[:, 0:TT],
                    in_=B["ar3"][:, 0:512] if pb == 0 else B["ar3"][:, 512:1024],
                    func=AF.Silu, scale=1.0, **kwb))

    @blk.gpsimd
    def _(g):
        for tt in range(NT):
            for oc in range(NCH):
                step = tt * NCH + oc
                g.wait_ge(sfe.h, b[sfe] + step + 1)
                sfw.inc(g.dma_start(out=gbuf[oc * 128:(oc + 1) * 128, tt * TT:(tt + 1) * TT],
                                    in_=evb[:, 0:TT] if step % 2 == 0 else evb2[:, 0:TT]), step)




def emit_qkv_l0(P, l, li, B, W):
    blk = P.block
    xnb, otb, ps, cstb = B["xnb"], B["otb"], B["ps"], B["cstb"]
    rows, rowsb, hr, hrb = B["rows"], B["rowsb"], B["hr"], B["hrb"]
    onec, oner, smallb2 = B["onec"], B["oner"], B["smallb2"]
    qpack, kpack, vpack = B["qpack"], B["kpack"], B["vpack"]
    sap = P.sem(f"ln_ap_{li}")
    sap_done = sap.n
    p2_sym = P.sem(f"f_ym_{li}")
    p2_done_mm = p2_sym.n
    p2_fw = P.pair(f"f_fw_{li}")
    p2_fw_tot = p2_fw.total()
    p2_hd = P.sem(f"f_hd_{li}")
    p2_done_hd = p2_hd.n

    sw = P.sem(f"a0_w_{li}")
    swq = P.pair(f"a0_wq_{li}")
    svw = P.sem(f"a0_vw_{li}")
    smm = P.sem(f"a0_mm_{li}")
    sev = P.sem(f"a0_ev_{li}")
    ssq = P.sem(f"a0_sq_{li}")
    srs = P.sem(f"a0_rs_{li}")
    srw = P.sem(f"a0_rw_{li}")
    src_ = P.sem(f"a0_rc_{li}")
    ssc = P.sem(f"a0_sc_{li}")
    st1 = P.sem(f"a0_t1_{li}")
    ssw = P.sem(f"a0_sw_{li}")
    sro = P.sem(f"a0_ro_{li}")
    sst = P.pair(f"a0_st_{li}")
    svm = P.sem(f"a0_vm_{li}")
    sve = P.sem(f"a0_ve_{li}")
    svs = P.pair(f"a0_vs_{li}")
    srow1 = P.sem(f"a0_row1_{li}")
    b = {s: s.n for s in [sw, svw, smm, sev, ssq, srs, srw, src_, ssc, st1, ssw,
                          sro, svm, sve, srow1]}
    bwq, bst, bvs = swq.base(), sst.base(), svs.base()

    @blk.sync
    def _(sy):
        sy.wait_ge(p2_sym.h, p2_done_mm)
        sy.wait_ge(p2_hd.h, p2_done_hd)
        sy.wait_ge(p2_fw.s[0].h, p2_fw_tot[0])
        sy.wait_ge(p2_fw.s[1].h, p2_fw_tot[1])
        sw.inc(sy.dma_start(out=cstb[0:64, 0:4096], in_=W[(l, "c_cos")]), 16)
        sw.inc(sy.dma_start(out=cstb[64:128, 0:4096], in_=W[(l, "c_cos")]), 16)
        sw.inc(sy.dma_start(out=cstb[0:64, 4096:8192], in_=W[(l, "c_sin")]), 16)
        sw.inc(sy.dma_start(out=cstb[64:128, 4096:8192], in_=W[(l, "c_sin")]), 16)
        sw.inc(sy.dma_start(out=smallb2[:, 0:128], in_=W[(l, "c_rope")]), 16)
        sw.inc(sy.dma_start(out=hr["h1"][0:1, 0:64], in_=W[(l, "qn_w")]), 16)
        sw.inc(sy.dma_start(out=hr["h2"][0:1, 0:64], in_=W[(l, "kn_w")]), 16)

    @blk.vector
    def _(v):
        v.wait_ge(sw.h, b[sw] + 16 * 7)
        P.ve(v, v.tensor_scalar(out=hr["h3"][0:1, 0:64], in0=hr["h1"][0:1, 0:64],
                        scalar1=0.125, scalar2=None, op0=OP.mult))
        P.ve(v, v.tensor_copy(out=hrb["b1"][0:1, 0:64], in_=hr["h3"][0:1, 0:64]))
        P.ve(v, v.tensor_copy(out=hrb["b2"][0:1, 0:64], in_=hr["h2"][0:1, 0:64]))
        P.ve(v, v.reduce_max(out=hr["h4"][0:1, 0:1], in_=hr["h1"][0:1, 0:64], axis=AX.X,
                     apply_absolute_value=True))
        P.ve(v, v.reduce_max(out=hr["h4"][0:1, 1:2], in_=hr["h2"][0:1, 0:64], axis=AX.X,
                     apply_absolute_value=True))
        P.ve(v, v.tensor_mul(out=hr["h4"][0:1, 2:3], in0=hr["h4"][0:1, 0:1],
                     in1=hr["h4"][0:1, 1:2]))
        P.ve(v, v.tensor_scalar(out=hr["h4"][0:1, 3:4], in0=hr["h4"][0:1, 2:3],
                        scalar1=-8.0, scalar2=None, op0=OP.mult))
        srow1.inc(v.tensor_copy(out=hrb["b3"][0:1, 0:1], in_=hr["h4"][0:1, 3:4]))
        v.wait_ge(p2_sym.h, p2_done_mm)
        v.wait_ge(p2_hd.h, p2_done_hd)
        for i in range(2):
            srow1.inc(v.memset(otb[:, 3584 + i * 1040 + 64:3584 + i * 1040 + 1040:65], 1.0))

    @blk.tensor
    def _(q):
        q.wait_ge(srow1.h, b[srow1] + 1)
        ssc.inc(q.matmul(out=ps[6][:, 0:1], lhsT=oner[0:1, :], rhs=hrb["b3"][0:1, 0:1],
                         start=True, stop=True))

    @blk.vector
    def _(v):
        v.wait_ge(ssc.h, b[ssc] + 1)
        srow1.inc(v.tensor_copy(out=B["bcol"][0][:, 0:1], in_=ps[6][:, 0:1]))

    @blk.gpsimd
    def _(g):
        g.wait_ge(p2_sym.h, p2_done_mm)
        g.wait_ge(p2_hd.h, p2_done_hd)
        g.wait_ge(p2_fw.s[0].h, p2_fw_tot[0])
        g.wait_ge(p2_fw.s[1].h, p2_fw_tot[1])
        for oc in range(16):
            col0 = oc * 128
            if oc >= 2:
                g.wait_ge(smm.h, b[smm] + (oc - 1) * NT)
            swq.inc(g.dma_start(
                out=otb[:, 5664 + (oc % 2) * 1024:5664 + (oc % 2) * 1024 + 1024].rearrange(
                    "p (k m) -> p k m", k=8),
                in_=W[(l, "wqkv")][:, col0:col0 + 128].rearrange("(k p) m -> p k m", p=128)),
                oc)
        for vc in range(2):
            svw.inc(g.dma_start(
                out=otb[:, 8192 + vc * 4096:8192 + (vc + 1) * 4096].rearrange(
                    "p (k m) -> p k m", k=8),
                in_=W[(l, "wqkv")][:, 2048 + vc * 512:2048 + (vc + 1) * 512].rearrange(
                    "(k p) m -> p k m", p=128)), 16)

    @blk.tensor
    def _(q):
        for oc in range(16):
            swq.wait_done(q, bwq, oc)
            for tt in range(NT):
                step = oc * NT + tt
                pb = step % 2
                q.wait_ge(sap.h, sap_done - NT + tt + 1)
                if step >= 2:
                    q.wait_ge(sev.h, b[sev] + step - 1)
                for c in range(NCH):
                    wb0 = 5664 + (oc % 2) * 1024
                    inst = q.matmul(out=ps[pb][:, 0:TT],
                                    lhsT=otb[:, wb0 + c * 128:wb0 + (c + 1) * 128],
                                    rhs=xnb[:, c * T + tt * TT:c * T + tt * TT + TT],
                                    start=(c == 0), stop=(c == NCH - 1))
                smm.inc(inst)
                q.wait_ge(ssq.h, b[ssq] + step + 1)
                if step >= 1:
                    q.wait_ge(srw.h, b[srw] + 2 * step)
                q.matmul(out=ps[2][0:1, 0:TT], lhsT=onec[0:64, 0:1],
                         rhs=otb[0:64, 512:512 + TT], start=True, stop=True,
                         tile_position=(0, 0))
                srs.inc(q.matmul(out=ps[3][0:1, 0:TT], lhsT=onec[64:128, 0:1],
                                 rhs=otb[64:128, 512:512 + TT], start=True, stop=True,
                                 tile_position=(64, 0)))
                q.wait_ge(src_.h, b[src_] + step + 1)
                qn = hrb["b1"] if oc < 8 else hrb["b2"]
                q.matmul(out=ps[4][0:64, 0:TT], lhsT=qn[0:1, 0:64],
                         rhs=rowsb["b5"][0:1, 0:TT], start=True, stop=True)
                ssc.inc(q.matmul(out=ps[4][64:128, 0:TT], lhsT=qn[0:1, 0:64],
                                 rhs=rowsb["b6"][0:1, 0:TT], start=True, stop=True))
                q.wait_ge(st1.h, b[st1] + step + 1)
                ssw.inc(q.matmul(out=ps[5][:, 0:TT], lhsT=smallb2[:, 0:128],
                                 rhs=otb[:, 0:TT], start=True, stop=True))

    @blk.scalar
    def _(s):
        for oc in range(16):
            for tt in range(NT):
                step = oc * NT + tt
                pb = step % 2
                s.wait_ge(smm.h, b[smm] + step + 1)
                if step >= 1:
                    s.wait_ge(sro.h, b[sro] + step)
                sev.inc(s.copy(out=otb[:, 0:TT], in_=ps[pb][:, 0:TT]))
                s.wait_ge(srs.h, b[srs] + step + 1)
                if step >= 1:
                    s.wait_ge(src_.h, b[src_] + step)
                s.activation(out=rows["r1"][0:1, 0:TT], in_=ps[2][0:1, 0:TT],
                             func=AF.Sqrt, bias=B["constc"][0:1, 1:2], scale=1.0 / 64.0)
                srw.inc(s.activation(out=rows["r2"][0:1, 0:TT], in_=ps[3][0:1, 0:TT],
                                     func=AF.Sqrt, bias=B["constc"][0:1, 1:2], scale=1.0 / 64.0), 2)

    @blk.vector
    def _(v):
        for oc in range(16):
            for tt in range(NT):
                step = oc * NT + tt
                v.wait_ge(sev.h, b[sev] + step + 1)
                ssq.inc(v.tensor_mul(out=otb[:, 512:512 + TT],
                                     in0=otb[:, 0:TT], in1=otb[:, 0:TT]))
                v.wait_ge(srw.h, b[srw] + 2 * (step + 1))
                P.ve(v, v.reciprocal(out=rows["r3"][0:1, 0:TT], in_=rows["r1"][0:1, 0:TT]))
                P.ve(v, v.reciprocal(out=rows["r4"][0:1, 0:TT], in_=rows["r2"][0:1, 0:TT]))
                P.ve(v, v.tensor_copy(out=rowsb["b5"][0:1, 0:TT], in_=rows["r3"][0:1, 0:TT]))
                src_.inc(v.tensor_copy(out=rowsb["b6"][0:1, 0:TT],
                                       in_=rows["r4"][0:1, 0:TT]))
                v.wait_ge(ssc.h, b[ssc] + 1 + step + 1)
                P.ve(v, v.tensor_mul(out=otb[:, 0:TT], in0=otb[:, 0:TT],
                                     in1=ps[4][:, 0:TT]))
                st1.inc(v.engine_nop())
                v.wait_ge(ssw.h, b[ssw] + step + 1)
                t0 = tt * TT
                P.ve(v, v.tensor_mul(out=otb[:, 1536:1536 + TT], in0=otb[:, 0:TT],
                             in1=cstb[:, t0:t0 + TT]))
                P.ve(v, v.tensor_mul(out=otb[:, 2048:2048 + TT], in0=ps[5][:, 0:TT],
                             in1=cstb[:, 4096 + t0:4096 + t0 + TT]))
                if step >= 2:
                    sst.wait_done(v, bst, step - 2)
                ob = 2560 + (step % 2) * 512
                sro.inc(v.tensor_add(out=otb[:, ob:ob + TT],
                                     in0=otb[:, 1536:1536 + TT],
                                     in1=otb[:, 2048:2048 + TT]))

    @blk.sync
    def _(sy):
        for oc in range(16):
            pack = qpack if oc < 8 else kpack
            prow = (oc % 8) * 128
            for tt in range(NT):
                step = oc * NT + tt
                sy.wait_ge(sro.h, b[sro] + step + 1)
                ob = 2560 + (step % 2) * 512
                sst.inc(sy.dma_start(out=pack[prow:prow + 128, tt * TT:tt * TT + TT],
                                     in_=otb[:, ob:ob + TT]), step)

    @blk.tensor
    def _(q):
        q.wait_ge(svw.h, b[svw] + 32)
        q.wait_ge(srow1.h, b[srow1] + 4)
        q.wait_ge(sap.h, sap_done)
        for tk in range(32):
            for vc in range(2):
                step = tk * 2 + vc
                pb = step % 2
                if step >= 2:
                    q.wait_ge(sve.h, b[sve] + step - 1)
                for c in range(NCH):
                    inst = q.matmul(
                        out=ps[pb][:, 0:TT],
                        lhsT=xnb[:, c * T + tk * 128:c * T + tk * 128 + 128],
                        rhs=otb[:, 8192 + vc * 4096 + c * 512:8192 + vc * 4096 + (c + 1) * 512],
                        start=(c == 0), stop=(c == NCH - 1))
                svm.inc(inst)

    @blk.scalar
    def _(s):
        for tk in range(32):
            for vc in range(2):
                step = tk * 2 + vc
                pb = step % 2
                s.wait_ge(svm.h, b[svm] + step + 1)
                if tk >= 2 and vc == 0:
                    svs.wait_done(s, bvs, tk - 2)
                vr0 = 3584 + (tk % 2) * 1040
                sve.inc(s.copy(
                    out=otb[:, vr0 + vc * 520:vr0 + (vc + 1) * 520].rearrange(
                        "p (h d) -> p h d", h=8)[:, :, 0:64],
                    in_=ps[pb][:, 0:TT].rearrange("p (h d) -> p h d", h=8)))

    @blk.sync
    def _(sy):
        for tk in range(32):
            sy.wait_ge(sve.h, b[sve] + 2 * (tk + 1))
            svs.inc(sy.dma_start(out=vpack[:, tk * 1040:(tk + 1) * 1040],
                                 in_=otb[:, 3584 + (tk % 2) * 1040:3584 + (tk % 2) * 1040 + 1040]), tk)



def attn_core(P, li, B, jobs, kload, qload, eb_bias, nheads_branch, pv_lhs,
              norm_out, vg_of_job, mask_ap, tag):
    raise NotImplementedError


def emit_attn_l0(P, l, li, B, W):
    blk = P.block
    xnb, otb, ps = B["xnb"], B["otb"], B["ps"]
    evb = B["evb"]
    rows, rowsb = B["rows"], B["rowsb"]
    oner = B["oner"]
    qpack, kpack, vpack = B["qpack"], B["kpack"], B["vpack"]
    groups = HOST_CONSTS["l0_groups"]
    sst = P.pair(f"a0_st_{li}")
    sst_tot = sst.total()
    svs = P.pair(f"a0_vs_{li}")
    svs_tot = svs.total()
    srow1 = P.sem(f"a0_row1_{li}")
    srow1_done = srow1.n

    svg = P.sem(f"b0_vg_{li}")
    skh = P.sem(f"b0_kh_{li}")
    sqh = P.sem(f"b0_qh_{li}")
    smk = P.pair(f"b0_mk_{li}")
    ssc = P.sem(f"b0_sc_{li}")
    sms = P.sem(f"b0_ms_{li}")
    sex = P.sem(f"b0_ex_{li}")
    spv = P.sem(f"b0_pv_{li}")
    sdc = P.sem(f"b0_dc_{li}")
    sdn = P.sem(f"b0_dn_{li}")
    srb = P.sem(f"b0_rb_{li}")
    snv = P.sem(f"b0_nv_{li}")
    sno = P.sem(f"b0_no_{li}")
    b = {s: s.n for s in [svg, skh, sqh, ssc, sms, sex, spv, sdc, sdn, srb, snv, sno]}
    bmk = smk.base()

    VG0, KH0, QH0, E0, MK0 = 0, 16640, 20736, 24832, 25856

    jobs = []
    for hp in range(8):
        for (q0, q1, kvisf, kvise, mlist) in groups:
            mm = dict(mlist)
            kts = [(kt, min(128, kvise - 128 * kt), mm.get(kt))
                   for kt in range((kvise + 127) // 128)]
            jobs.append((hp, q0, q1 - q0, kts))
    gstep = []
    for ji, (hp, q0, N, kts) in enumerate(jobs):
        for ki, (kt, krows, midx) in enumerate(kts):
            gstep.append((ji, ki, midx is not None))
    nsteps_of_job = [len(kts) for (_, _, _, kts) in jobs]
    step_base = [0]
    for n_ in nsteps_of_job:
        step_base.append(step_base[-1] + n_)
    nmask_upto = []
    m = 0
    for (_, _, masked) in gstep:
        if masked:
            m += 1
        nmask_upto.append(m)
    masks_of_hp = []
    for hp in range(8):
        lst = []
        for (q0, q1, kvisf, kvise, mlist) in groups:
            for (kt, midx) in mlist:
                lst.append(midx)
        masks_of_hp.append(lst)
    per_hp_masks = len(masks_of_hp[0])
    steps_upto_hp = [0] * 9
    for ji, (hp, q0, N, kts) in enumerate(jobs):
        steps_upto_hp[hp + 1] = step_base[ji] + len(kts)

    @blk.gpsimd
    def _(g):
        g.wait_ge(sst.s[0].h, sst_tot[0])
        g.wait_ge(sst.s[1].h, sst_tot[1])
        g.wait_ge(svs.s[0].h, svs_tot[0])
        g.wait_ge(svs.s[1].h, svs_tot[1])
        g.wait_ge(srow1.h, srow1_done)
        nm = 0
        for hp in range(8):
            vg = hp // 4
            if hp % 4 == 0:
                if hp > 0:
                    g.wait_ge(spv.h, b[spv] + 2 * steps_upto_hp[hp])
                svg.inc(g.dma_start(
                    out=xnb[:, VG0:VG0 + 16640].rearrange("p (k w) -> p k w", k=32),
                    in_=vpack[:, :].rearrange("p (k g w) -> p k g w", k=32, g=2)[:, :, vg, :]),
                    16)
            if hp >= 1:
                g.wait_ge(ssc.h, b[ssc] + 2 * steps_upto_hp[hp])
            skh.inc(g.dma_start(out=xnb[:, KH0:KH0 + T],
                                in_=kpack[hp * 128:(hp + 1) * 128, :]), 16)
            sqh.inc(g.dma_start(out=xnb[:, QH0:QH0 + T],
                                in_=qpack[hp * 128:(hp + 1) * 128, :]), 16)
            for i, midx in enumerate(masks_of_hp[hp]):
                if nm >= 2:
                    g.wait_ge(sms.h, b[sms] + nm - 1)
                smk.inc(g.dma_start(out=xnb[:, MK0 + (nm % 2) * 512:MK0 + (nm % 2) * 512 + 512],
                                    in_=W[(l, "c_l0m")][midx]), nm)
                nm += 1

    @blk.tensor
    def _(q):
        for ji, (hp, q0, N, kts) in enumerate(jobs):
            q.wait_ge(svg.h, b[svg] + 16 * (hp // 4 + 1))
            q.wait_ge(skh.h, b[skh] + 16 * (hp + 1))
            q.wait_ge(sqh.h, b[sqh] + 16 * (hp + 1))
            hloc = (2 * hp) % 8

            def emit_pv(ki):
                kt_, krows_, _m = kts[ki]
                s_ = step_base[ji] + ki
                q.wait_ge(sex.h, b[sex] + 2 * (s_ + 1))
                if ki == 0:
                    q.wait_ge(snv.h, b[snv] + ji)
                first = (ki == 0)
                last = (ki == len(kts) - 1)
                q.matmul(out=ps[2][0:65, 0:N],
                         lhsT=xnb[0:krows_, VG0 + kt_ * 520 + 65 * hloc:
                                  VG0 + kt_ * 520 + 65 * hloc + 65],
                         rhs=xnb[0:krows_, E0:E0 + N], start=first, stop=last)
                spv.inc(q.matmul(out=ps[5][0:65, 0:N],
                                 lhsT=xnb[0:krows_, VG0 + kt_ * 520 + 65 * (hloc + 1):
                                          VG0 + kt_ * 520 + 65 * (hloc + 1) + 65],
                                 rhs=xnb[0:krows_, E0 + 512:E0 + 512 + N],
                                 start=first, stop=last), 2)

            for ki, (kt, krows, midx) in enumerate(kts):
                s = step_base[ji] + ki
                bA = ps[0 + 3 * (s % 2)]
                bB = ps[1 + 3 * (s % 2)]
                if s >= 2:
                    q.wait_ge(sex.h, b[sex] + 2 * (s - 1))
                q.matmul(out=bA[0:krows, 0:N],
                         lhsT=xnb[0:64, KH0 + kt * 128:KH0 + kt * 128 + krows],
                         rhs=xnb[0:64, QH0 + q0:QH0 + q0 + N], start=True, stop=True,
                         tile_position=(0, 0))
                ssc.inc(q.matmul(out=bB[0:krows, 0:N],
                                 lhsT=xnb[64:128, KH0 + kt * 128:KH0 + kt * 128 + krows],
                                 rhs=xnb[64:128, QH0 + q0:QH0 + q0 + N], start=True,
                                 stop=True, tile_position=(64, 0)), 2)
                if ki >= 1:
                    emit_pv(ki - 1)
            emit_pv(len(kts) - 1)
            q.wait_ge(sdn.h, b[sdn] + ji + 1)
            q.matmul(out=ps[6][0:64, 0:N], lhsT=oner[0:1, 0:64],
                     rhs=rowsb["b5"][0:1, 0:N], start=True, stop=False)
            q.matmul(out=ps[6][0:64, 0:N], lhsT=oner[0:1, 0:64],
                     rhs=rowsb["b6"][0:1, 0:N], start=False, stop=True)
            q.matmul(out=ps[6][64:128, 0:N], lhsT=oner[0:1, 0:64],
                     rhs=rowsb["b5"][0:1, 512:512 + N], start=True, stop=False)
            srb.inc(q.matmul(out=ps[6][64:128, 0:N], lhsT=oner[0:1, 0:64],
                             rhs=rowsb["b6"][0:1, 512:512 + N], start=False, stop=True))

    @blk.vector
    def _(v):
        nm_global = 0
        for ji, (hp, q0, N, kts) in enumerate(jobs):
            for ki, (kt, krows, midx) in enumerate(kts):
                s = step_base[ji] + ki
                if midx is not None:
                    nmh = nm_global % per_hp_masks
                    slot = (hp * per_hp_masks + nmh) % 2
                    v.wait_ge(ssc.h, b[ssc] + 2 * (s + 1))
                    smk.wait_done(v, bmk, hp * per_hp_masks + nmh)
                    bA = ps[0 + 3 * (s % 2)]
                    bB = ps[1 + 3 * (s % 2)]
                    v.tensor_add(out=bA[0:krows, 0:N], in0=bA[0:krows, 0:N],
                                 in1=xnb[0:krows, MK0 + slot * 512:MK0 + slot * 512 + N])
                    sms.inc(v.tensor_add(out=bB[0:krows, 0:N], in0=bB[0:krows, 0:N],
                                         in1=xnb[0:krows, MK0 + slot * 512:MK0 + slot * 512 + N]))
                    nm_global += 1
            v.wait_ge(spv.h, b[spv] + 2 * step_base[ji + 1])
            P.ve(v, v.tensor_copy(out=rows["r1"][0:1, 0:N], in_=ps[2][64:65, 0:N]))
            sdc.inc(v.tensor_copy(out=rows["r2"][0:1, 0:N], in_=ps[5][64:65, 0:N]))
            P.ve(v, v.reciprocal(out=rows["r3"][0:1, 0:N], in_=rows["r1"][0:1, 0:N]))
            P.ve(v, v.reciprocal(out=rows["r4"][0:1, 0:N], in_=rows["r2"][0:1, 0:N]))
            if ji >= 1:
                v.wait_ge(srb.h, b[srb] + ji)
            P.ve(v, v.tensor_copy(out=rowsb["b5"][0:1, 0:N], in_=rows["r3"][0:1, 0:N]))
            P.ve(v, v.tensor_sub(out=rows["r5"][0:1, 0:N], in0=rows["r3"][0:1, 0:N],
                         in1=rowsb["b5"][0:1, 0:N]))
            P.ve(v, v.tensor_copy(out=rowsb["b6"][0:1, 0:N], in_=rows["r5"][0:1, 0:N]))
            P.ve(v, v.tensor_copy(out=rowsb["b5"][0:1, 512:512 + N], in_=rows["r4"][0:1, 0:N]))
            P.ve(v, v.tensor_sub(out=rows["r5"][0:1, 0:N], in0=rows["r4"][0:1, 0:N],
                         in1=rowsb["b5"][0:1, 512:512 + N]))
            sdn.inc(v.tensor_copy(out=rowsb["b6"][0:1, 512:512 + N],
                                  in_=rows["r5"][0:1, 0:N]))
            v.wait_ge(snv.h, b[snv] + ji + 1)
            v.wait_ge(srb.h, b[srb] + ji + 1)
            sno.inc(v.tensor_mul(out=otb[:, hp * T + q0:hp * T + q0 + N],
                                 in0=evb[:, 0:N], in1=ps[6][:, 0:N]))

    @blk.scalar
    def _(s_):
        for ji, (hp, q0, N, kts) in enumerate(jobs):
            for ki, (kt, krows, midx) in enumerate(kts):
                s = step_base[ji] + ki
                if midx is not None:
                    s_.wait_ge(sms.h, b[sms] + nmask_upto[s])
                else:
                    s_.wait_ge(ssc.h, b[ssc] + 2 * (s + 1))
                if s >= 1:
                    s_.wait_ge(spv.h, b[spv] + 2 * s)
                bA = ps[0 + 3 * (s % 2)]
                bB = ps[1 + 3 * (s % 2)]
                s_.activation(out=xnb[0:krows, E0:E0 + N], in_=bA[0:krows, 0:N],
                              func=AF.Exp, bias=B["bcol"][0][0:krows, 0:1], scale=1.0)
                sex.inc(s_.activation(out=xnb[0:krows, E0 + 512:E0 + 512 + N],
                                      in_=bB[0:krows, 0:N], func=AF.Exp,
                                      bias=B["bcol"][0][0:krows, 0:1], scale=1.0), 2)
            s_.wait_ge(sdc.h, b[sdc] + ji + 1)
            if ji >= 1:
                s_.wait_ge(sno.h, b[sno] + ji)
            s_.copy(out=evb[0:64, 0:N], in_=ps[2][0:64, 0:N])
            snv.inc(s_.copy(out=evb[64:128, 0:N], in_=ps[5][0:64, 0:N]))


def emit_proj_update(P, l, li, xi, xo, B, W):
    blk = P.block
    xnb, otb, ps, ar3 = B["xnb"], B["otb"], B["ps"], B["ar3"]
    evb, evb2 = B["evb"], B["evb2"]
    rows, rowsb = B["rows"], B["rowsb"]
    oner = B["oner"]
    sno = P.sem(f"b{l}_no_{li}")
    sno_done = sno.n
    sg = P.pair(f"f_fw_{li}")
    sg_tot = sg.total()

    sw = P.sem(f"p4_w_{li}")
    sgl = P.pair(f"p4_g_{li}")
    sxl = P.pair(f"p4_x_{li}")
    smm = P.sem(f"p4_mm_{li}")
    sev = P.sem(f"p4_ev_{li}")
    sdv = P.sem(f"p4_dv_{li}")
    supd = P.pair(f"upd_done_{li}")
    sln = P.sem(f"p4_ln_{li}")
    slr = P.sem(f"p4_lr_{li}")
    slb = P.sem(f"p4_lb_{li}")
    b = {s: s.n for s in [sw, smm, sev, sdv, sln, slr, slb]}
    bg, bx2, bupd = sgl.base(), sxl.base(), supd.base()

    @blk.gpsimd
    def _(g):
        g.wait_ge(sno.h, sno_done)
        sw.inc(g.dma_start(
            out=xnb[:, 0:8192].rearrange("p (k m) -> p k m", k=8),
            in_=W[(l, "wproj")].rearrange("(k p) m -> p k m", p=128)), 16)
        sw.inc(g.dma_start(out=B["biasA"][:, 16:24], in_=W[(l, "bproj")]), 16)

    @blk.sync
    def _(sy):
        sy.wait_ge(sg.s[0].h, sg_tot[0])
        sy.wait_ge(sg.s[1].h, sg_tot[1])
        for tt in range(NT):
            for oc in range(NCH):
                step = tt * NCH + oc
                pb = step % 2
                if step >= 2:
                    sy.wait_ge(sdv.h, b[sdv] + 2 * (step - 1))
                sgl.inc(sy.dma_start(out=ar3[:, 4096 + pb * 512:4096 + pb * 512 + 512],
                             in_=B["gbuf"][oc * 128:(oc + 1) * 128, tt * TT:(tt + 1) * TT]), step)
                sxl.inc(sy.dma_start(out=ar3[:, 5120 + pb * 512:5120 + pb * 512 + 512],
                                     in_=xi[oc * 128:(oc + 1) * 128, tt * TT:(tt + 1) * TT]),
                        step)

    @blk.tensor
    def _(q):
        q.wait_ge(sw.h, b[sw] + 32)
        for tt in range(NT):
            for oc in range(NCH):
                step = tt * NCH + oc
                pb = step % 2
                if step >= 2:
                    q.wait_ge(sev.h, b[sev] + step - 1)
                for fc in range(NCH):
                    inst = q.matmul(
                        out=ps[pb][:, 0:TT],
                        lhsT=xnb[:, fc * 1024 + oc * 128:fc * 1024 + oc * 128 + 128],
                        rhs=otb[:, fc * T + tt * TT:fc * T + tt * TT + TT],
                        start=(fc == 0), stop=(fc == NCH - 1))
                smm.inc(inst)

    @blk.scalar
    def _(s):
        for tt in range(NT):
            for oc in range(NCH):
                step = tt * NCH + oc
                pb = step % 2
                s.wait_ge(smm.h, b[smm] + step + 1)
                if step >= 2:
                    s.wait_ge(sdv.h, b[sdv] + 2 * (step - 1))
                    supd.wait_done(s, bupd, step - 2)
                sev.inc(s.activation(out=evb[:, 0:TT] if pb == 0 else evb2[:, 0:TT],
                                     in_=ps[pb][:, 0:TT], func=AF.Identity,
                                     bias=B["biasA"][:, 16 + oc:16 + oc + 1], scale=1.0))

    @blk.vector
    def _(v):
        for tt in range(NT):
            for oc in range(NCH):
                step = tt * NCH + oc
                pb = step % 2
                eb = evb if pb == 0 else evb2
                v.wait_ge(sev.h, b[sev] + step + 1)
                sgl.wait_done(v, bg, step)
                sxl.wait_done(v, bx2, step)
                P.ve(v, v.tensor_mul(out=eb[:, 0:TT], in0=eb[:, 0:TT],
                             in1=ar3[:, 4096 + pb * 512:4096 + pb * 512 + 512]))
                sdv.inc(v.tensor_add(out=eb[:, 0:TT], in0=eb[:, 0:TT],
                                     in1=ar3[:, 5120 + pb * 512:5120 + pb * 512 + 512]), 2)

    @blk.gpsimd
    def _(g):
        for tt in range(NT):
            for oc in range(NCH):
                step = tt * NCH + oc
                pb = step % 2
                g.wait_ge(sdv.h, b[sdv] + 2 * (step + 1))
                supd.inc(g.dma_start(
                    out=xo[oc * 128:(oc + 1) * 128, tt * TT:(tt + 1) * TT],
                    in_=evb[:, 0:TT] if pb == 0 else evb2[:, 0:TT]), step)




def emit_ffn_l3(P, l, li, B, W):
    _emit_ffn_moe(P, l, li, B, W, E=8, shared=False, HID_CH=16)



_PROG_CACHE = {}


def _get_program(layers_key, host_shapes_key):
    if layers_key not in _PROG_CACHE:
        shapes = {l: dict(s) for l, s in host_shapes_key}
        _PROG_CACHE[layers_key] = build_program(list(layers_key), shapes)
    return _PROG_CACHE[layers_key]


def kernel(**inputs):
    x_full = np.asarray(inputs["L0_attn_x"], dtype=np.float32)
    per_layer_w = {l: layer_inputs(l, inputs) for l in range(4)}
    shapes = {l: {k: v.shape for k, v in w.items()} for l, w in per_layer_w.items()}

    xs = [np.ascontiguousarray(x_full[c // 2].T) for c in range(8)]
    for l in range(4):
        key = (l,)
        skey = tuple((ll, tuple(sorted((k, tuple(s)) for k, s in shapes[ll].items())))
                     for ll in key)
        if key not in _PROG_CACHE:
            _PROG_CACHE[key] = build_program([l], shapes)
        nc = _PROG_CACHE[key]
        in_maps = []
        for c in range(8):
            m = {"x": xs[c]}
            for k, v in per_layer_w[l].items():
                m[f"L{l}_{k}"] = v
            in_maps.append(m)
        res = run_bass_kernel_spmd(nc, in_maps, core_ids=list(range(8)))
        xs = [np.ascontiguousarray(np.asarray(res.results[c]["xo"])) for c in range(8)]
    out = np.stack([xs[2 * b].T for b in range(4)], axis=0)
    return np.ascontiguousarray(out.astype(np.float32))



def emit_ffn_l1(P, l, li, B, W):
    blk = P.block
    xnb, otb, ps, ar3 = B["xnb"], B["otb"], B["ps"], B["ar3"]
    evb, evb2 = B["evb"], B["evb2"]
    rows, rowsb = B["rows"], B["rowsb"]
    oner = B["oner"]
    biasA = B["biasA"]
    gbuf = B["gbuf"]
    sap = P.sem(f"ln_ap_{li}")
    sap_done = sap.n
    SC = [0, 512, 1024, 1536, 4096, 4608, 5120, 5632]
    TMPC = 3072
    sgn = {0: 1, 1: 1, 2: -1, 3: -1}

    swr = P.sem(f"c1_wr_{li}")
    sgm = P.sem(f"c1_gm_{li}")
    sgc = P.sem(f"c1_gc_{li}")
    sge = P.sem(f"c1_ge_{li}")
    sgf = P.sem(f"c1_gf_{li}")
    sic = P.sem(f"c1_ic_{li}")
    scc = P.sem(f"c1_cc_{li}")
    h1m = P.sem(f"c1_h1m_{li}")
    h1e = P.sem(f"c1_h1e_{li}")
    h2m = P.sem(f"c1_h2m_{li}")
    h2e = P.sem(f"c1_h2e_{li}")
    h3m = P.sem(f"c1_h3m_{li}")
    h3e = P.sem(f"c1_h3e_{li}")
    h4m = P.sem(f"c1_h4m_{li}")
    str_ = P.sem(f"c1_tf_{li}")
    srw = P.sem(f"c1_rw_{li}")
    scb = P.sem(f"c1_cb_{li}")
    sud = P.sem(f"c1_ud_{li}")
    ssl = P.sem(f"c1_sl_{li}")
    sems = [swr, sgm, sgc, sge, sgf, sic, scc, h1m, h1e, h2m, h2e, h3m, h3e,
            h4m, str_, srw, scb, sud, ssl]
    b = {s: s.n for s in sems}
    sgw = P.pair(f"f_fw_{li}")
    bgw = sgw.base()

    @blk.gpsimd
    def _(g):
        g.wait_ge(sap.h, sap_done)
        loads = [("cn_w1", 0, 8), ("cn_w2", 8192, 8), ("cn_w3", 12288, 4),
                 ("cn_w4", 13312, 2), ("gc_w1", 13440, 8), ("gc_w2", 17536, 4),
                 ("gc_w3", 18560, 2)]
        for name, off, k in loads:
            wshape = W[(l, name)]
            swr.inc(g.dma_start(
                out=otb[:, off:off + k * (wshape.shape[1])].rearrange(
                    "p (k m) -> p k m", k=k),
                in_=wshape.rearrange("(k p) m -> p k m", p=128),
                allow_slow_non_contiguous=(wshape.shape[1] == 1)), 16)
        for name, sl, full in [("cn_b1", (0, 8), True), ("cn_b2", (8, 12), True),
                               ("cn_b3", (12, 14), True), ("cn_b4", (14, 15), False),
                               ("gc_b1", (16, 20), True), ("gc_b2", (20, 22), True),
                               ("gc_b3", (22, 23), False), ("adrange", (23, 24), False)]:
            ap = biasA[:, sl[0]:sl[1]] if full else biasA[0:1, sl[0]:sl[1]]
            swr.inc(g.dma_start(out=ap, in_=W[(l, name)]), 16)

    NW = 15

    @blk.vector
    def _(v):
        v.wait_ge(sap.h, sap_done)
        for c in range(NCH):
            P.ve(v, v.reduce_sum(out=B["rbc"][:, c:c + 1], in_=xnb[:, c * T:(c + 1) * T],
                                 axis=AX.X))
        P.ve(v, v.tensor_scalar(out=B["rbc"][:, 8:16], in0=B["rbc"][:, 0:8],
                                scalar1=1.0 / T, scalar2=None, op0=OP.mult))
        sgm.inc(v.tensor_copy(out=B["smallb"][:, 0:8], in_=B["rbc"][:, 8:16]))

    @blk.tensor
    def _(q):
        q.wait_ge(sgm.h, b[sgm] + 1)
        q.wait_ge(swr.h, b[swr] + 16 * NW)
        for oc in range(4):
            for c in range(NCH):
                inst = q.matmul(out=ps[4][:, oc:oc + 1],
                                lhsT=otb[:, 13440 + c * 512 + oc * 128:
                                         13440 + c * 512 + oc * 128 + 128],
                                rhs=B["smallb"][:, c:c + 1], start=(c == 0),
                                stop=(c == NCH - 1))
            sgc.inc(inst)
        q.wait_ge(sge.h, b[sge] + 4)
        for oc in range(2):
            for c in range(4):
                inst = q.matmul(out=ps[5][:, oc:oc + 1],
                                lhsT=otb[:, 17536 + c * 256 + oc * 128:
                                         17536 + c * 256 + oc * 128 + 128],
                                rhs=B["smallb"][:, 16 + c:16 + c + 1], start=(c == 0),
                                stop=(c == 3))
            sgc.inc(inst)
        q.wait_ge(sge.h, b[sge] + 6)
        for c in range(2):
            inst = q.matmul(out=ps[4][0:1, 8:9],
                            lhsT=otb[:, 18560 + c:18560 + c + 1],
                            rhs=B["smallb"][:, 24 + c:24 + c + 1], start=(c == 0),
                            stop=(c == 1))
        sgc.inc(inst)

    @blk.scalar
    def _(s):
        for oc in range(4):
            s.wait_ge(sgc.h, b[sgc] + oc + 1)
            sge.inc(s.activation(out=B["smallb"][:, 16 + oc:17 + oc],
                                 in_=ps[4][:, oc:oc + 1], func=AF.Gelu,
                                 bias=biasA[:, 16 + oc:17 + oc], scale=1.0))
        for oc in range(2):
            s.wait_ge(sgc.h, b[sgc] + 4 + oc + 1)
            sge.inc(s.activation(out=B["smallb"][:, 24 + oc:25 + oc],
                                 in_=ps[5][:, oc:oc + 1], func=AF.Gelu,
                                 bias=biasA[:, 20 + oc:21 + oc], scale=1.0))
        s.wait_ge(sgc.h, b[sgc] + 7)
        sgf.inc(s.activation(out=rows["r5"][0:1, 4:5], in_=ps[4][0:1, 8:9],
                             func=AF.Sigmoid, bias=biasA[0:1, 22:23], scale=1.0))

    @blk.vector
    def _(v):
        v.wait_ge(sgf.h, b[sgf] + 1)
        sgf.inc(v.tensor_scalar(out=rows["r5"][0:1, 5:6], in0=rows["r5"][0:1, 4:5],
                                scalar1=0.7, scalar2=None, op0=OP.mult))

    @blk.tensor
    def _(q):
        q.wait_ge(sgf.h, b[sgf] + 2)
        for tt in range(NT):
            for cyc in range(3):
                idx = tt * 3 + cyc
                q.wait_ge(scc.h, b[scc] + idx + 1)
                for hc in range(NCH):
                    g1 = idx * 8 + hc
                    pb = hc % 2
                    if hc <= 1:
                        if idx >= 1:
                            q.wait_ge(h3e.h, b[h3e] + 2 * idx)
                    elif g1 >= 2:
                        q.wait_ge(h1e.h, b[h1e] + g1 - 1)
                    for c in range(NCH):
                        inst = q.matmul(out=ps[pb][:, 0:TT],
                                        lhsT=otb[:, c * 1024 + hc * 128:c * 1024 + hc * 128 + 128],
                                        rhs=otb[:, 20480 + c * 512:20480 + c * 512 + 512],
                                        start=(c == 0), stop=(c == NCH - 1))
                    h1m.inc(inst)
                q.wait_ge(h1e.h, b[h1e] + 8 * (idx + 1))
                for hc in range(4):
                    g2 = idx * 4 + hc
                    pb = hc % 2
                    if g2 >= 2:
                        q.wait_ge(h2e.h, b[h2e] + g2 - 1)
                    for c in range(NCH):
                        inst = q.matmul(out=ps[2 + pb][:, 0:TT],
                                        lhsT=otb[:, 8192 + c * 512 + hc * 128:
                                                 8192 + c * 512 + hc * 128 + 128],
                                        rhs=otb[:, 24576 + c * 512:24576 + c * 512 + 512],
                                        start=(c == 0), stop=(c == NCH - 1))
                    h2m.inc(inst)
                q.wait_ge(h2e.h, b[h2e] + 4 * (idx + 1))
                for hc in range(2):
                    g3 = idx * 2 + hc
                    pb = hc % 2
                    for c in range(4):
                        inst = q.matmul(out=ps[pb][:, 0:TT],
                                        lhsT=otb[:, 12288 + c * 256 + hc * 128:
                                                 12288 + c * 256 + hc * 128 + 128],
                                        rhs=otb[:, 28672 + c * 512:28672 + c * 512 + 512],
                                        start=(c == 0), stop=(c == 3))
                    h3m.inc(inst)
                q.wait_ge(h3e.h, b[h3e] + 2 * (idx + 1))
                if idx >= 1:
                    q.wait_ge(str_.h, b[str_] + idx)
                for c in range(2):
                    inst = q.matmul(out=ps[5][0:1, 0:TT],
                                    lhsT=otb[:, 13312 + c:13312 + c + 1],
                                    rhs=otb[:, 30720 + c * 512:30720 + c * 512 + 512],
                                    start=(c == 0), stop=(c == 1))
                h4m.inc(inst)
                q.wait_ge(srw.h, b[srw] + idx + 1)
                if idx >= 1:
                    q.wait_ge(sud.h, b[sud] + idx)
                q.matmul(out=ps[6][:, 0:TT], lhsT=oner[0:1, :],
                         rhs=rowsb["b1"][0:1, 0:TT], start=True, stop=False)
                scb.inc(q.matmul(out=ps[6][:, 0:TT], lhsT=oner[0:1, :],
                                 rhs=rowsb["b2"][0:1, 0:TT], start=False, stop=True))

    @blk.scalar
    def _(s):
        for tt in range(NT):
            for cyc in range(3):
                idx = tt * 3 + cyc
                for hc in range(NCH):
                    s.wait_ge(h1m.h, b[h1m] + idx * 8 + hc + 1)
                    h1e.inc(s.activation(out=otb[:, 24576 + hc * 512:24576 + hc * 512 + 512],
                                         in_=ps[hc % 2][:, 0:TT], func=AF.Gelu,
                                         bias=biasA[:, hc:hc + 1], scale=1.0))
                for hc in range(4):
                    s.wait_ge(h2m.h, b[h2m] + idx * 4 + hc + 1)
                    h2e.inc(s.activation(out=otb[:, 28672 + hc * 512:28672 + hc * 512 + 512],
                                         in_=ps[2 + hc % 2][:, 0:TT], func=AF.Gelu,
                                         bias=biasA[:, 8 + hc:9 + hc], scale=1.0))
                for hc in range(2):
                    s.wait_ge(h3m.h, b[h3m] + idx * 2 + hc + 1)
                    h3e.inc(s.activation(out=otb[:, 30720 + hc * 512:30720 + hc * 512 + 512],
                                         in_=ps[hc % 2][:, 0:TT], func=AF.Gelu,
                                         bias=biasA[:, 12 + hc:13 + hc], scale=1.0))
                s.wait_ge(h4m.h, b[h4m] + idx + 1)
                if idx >= 1:
                    s.wait_ge(srw.h, b[srw] + idx)
                str_.inc(s.activation(out=rows["r1"][0:1, 0:TT], in_=ps[5][0:1, 0:TT],
                                      func=AF.Sigmoid, bias=biasA[0:1, 14:15], scale=1.0))
            for c in range(NCH):
                gsl = tt * NCH + c
                s.wait_ge(sud.h, b[sud] + 3 * (tt + 1))
                if gsl >= 2:
                    sgw.wait_done(s, bgw, gsl - 2)
                ssl.inc(s.activation(out=evb[:, 0:TT] if gsl % 2 == 0 else evb2[:, 0:TT],
                                     in_=ar3[:, SC[c]:SC[c] + TT], func=AF.Silu,
                                     scale=1.0))

    @blk.vector
    def _(v):
        for tt in range(NT):
            t0 = tt * TT
            if tt >= 1:
                v.wait_ge(ssl.h, b[ssl] + tt * NCH)
            for c in range(NCH):
                inst = v.tensor_copy(out=ar3[:, SC[c]:SC[c] + TT],
                                     in_=xnb[:, c * T + t0:c * T + t0 + TT])
            sic.inc(inst)
            for cyc in range(3):
                idx = tt * 3 + cyc
                for c in range(NCH):
                    inst = v.tensor_copy(out=otb[:, 20480 + c * 512:20480 + c * 512 + 512],
                                         in_=ar3[:, SC[c]:SC[c] + TT])
                scc.inc(inst)
                v.wait_ge(str_.h, b[str_] + idx + 1)
                P.ve(v, v.tensor_scalar(out=rows["r2"][0:1, 0:TT], in0=rows["r1"][0:1, 0:TT],
                                        scalar1=0.3, scalar2=None, op0=OP.mult))
                P.ve(v, v.tensor_scalar(out=rows["r2"][0:1, 0:TT], in0=rows["r2"][0:1, 0:TT],
                                        scalar1=rows["r5"][0:1, 5:6], scalar2=None,
                                        op0=OP.add))
                P.ve(v, v.tensor_scalar(out=rows["r2"][0:1, 0:TT], in0=rows["r2"][0:1, 0:TT],
                                        scalar1=2.0, scalar2=-1.0, op0=OP.mult, op1=OP.add))
                P.ve(v, v.tensor_scalar(out=rows["r2"][0:1, 0:TT], in0=rows["r2"][0:1, 0:TT],
                                        scalar1=biasA[0:1, 23:24], scalar2=0.1,
                                        op0=OP.mult, op1=OP.add))
                P.ve(v, v.tensor_copy(out=rowsb["b1"][0:1, 0:TT], in_=rows["r2"][0:1, 0:TT]))
                P.ve(v, v.tensor_sub(out=rows["r3"][0:1, 0:TT], in0=rows["r2"][0:1, 0:TT],
                                     in1=rowsb["b1"][0:1, 0:TT]))
                srw.inc(v.tensor_copy(out=rowsb["b2"][0:1, 0:TT], in_=rows["r3"][0:1, 0:TT]))
                v.wait_ge(scb.h, b[scb] + idx + 1)
                USC = 3584
                for c in range(4):
                    c2 = c + 4
                    P.ve(v, v.tensor_copy(out=ar3[:, TMPC:TMPC + TT],
                                          in_=ar3[:, SC[c]:SC[c] + TT]))
                    P.ve(v, v.tensor_mul(out=ar3[:, USC:USC + TT],
                                         in0=ar3[:, SC[c2]:SC[c2] + TT],
                                         in1=ps[6][:, 0:TT]))
                    op1 = OP.add if sgn[c] > 0 else OP.subtract
                    P.ve(v, v.tensor_tensor(out=ar3[:, SC[c]:SC[c] + TT],
                                            in0=ar3[:, SC[c]:SC[c] + TT],
                                            in1=ar3[:, USC:USC + TT], op=op1))
                    P.ve(v, v.tensor_mul(out=ar3[:, USC:USC + TT],
                                         in0=ar3[:, TMPC:TMPC + TT],
                                         in1=ps[6][:, 0:TT]))
                    op2 = OP.subtract if sgn[c] > 0 else OP.add
                    P.ve(v, v.tensor_tensor(out=ar3[:, SC[c2]:SC[c2] + TT],
                                            in0=ar3[:, SC[c2]:SC[c2] + TT],
                                            in1=ar3[:, USC:USC + TT], op=op2))
                sud.inc(v.engine_nop())

    @blk.sync
    def _(sy):
        for tt in range(NT):
            for c in range(NCH):
                gsl = tt * NCH + c
                sy.wait_ge(ssl.h, b[ssl] + gsl + 1)
                sgw.inc(sy.dma_start(out=gbuf[c * 128:(c + 1) * 128, tt * TT:(tt + 1) * TT],
                                     in_=evb[:, 0:TT] if gsl % 2 == 0 else evb2[:, 0:TT]),
                        gsl)

def emit_qkv_l1(P, l, li, B, W):
    blk = P.block
    xnb, otb, ps, cstb = B["xnb"], B["otb"], B["ps"], B["cstb"]
    rows, rowsb, hr = B["rows"], B["rowsb"], B["hr"]
    onec, oner, smallb2 = B["onec"], B["oner"], B["smallb2"]
    qpack, kpack, vpack = B["qpack"], B["kpack"], B["vpack"]
    biasA = B["biasA"]
    sap = P.sem(f"ln_ap_{li}")
    sap_done = sap.n
    p2_ud = P.sem(f"c1_ud_{li}")
    p2_ud_done = p2_ud.n
    p2_sl = P.sem(f"c1_sl_{li}")
    p2_sl_done = p2_sl.n
    p2_fw = P.pair(f"f_fw_{li}")
    p2_fw_tot = p2_fw.total()

    sw = P.sem(f"a1_w_{li}")
    swq = P.pair(f"a1_wq_{li}")
    svw = P.sem(f"a1_vw_{li}")
    smm = P.sem(f"a1_mm_{li}")
    sev = P.sem(f"a1_ev_{li}")
    ssq = P.sem(f"a1_sq_{li}")
    srs = P.sem(f"a1_rs_{li}")
    snx = P.sem(f"a1_nx_{li}")
    sst = P.pair(f"a1_st_{li}")
    svm = P.sem(f"a1_vm_{li}")
    sve = P.sem(f"a1_ve_{li}")
    svs = P.pair(f"a1_vs_{li}")
    srow1 = P.sem(f"a1_row1_{li}")
    spw = P.sem(f"a1_pw_{li}")
    spm = P.sem(f"a1_pm_{li}")
    spe = P.sem(f"a1_pe_{li}")
    sxm = P.sem(f"a1_xm_{li}")
    sxe = P.sem(f"a1_xe_{li}")
    sld = P.sem(f"a1_ld_{li}")
    snm = P.sem(f"a1_nm_{li}")
    sne = P.sem(f"a1_ne_{li}")
    snr = P.sem(f"a1_nr_{li}")
    snb = P.sem(f"a1_nb_{li}")
    sli = P.sem(f"a1_li_{li}")
    sjm = P.sem(f"a1_jm_{li}")
    sje = P.sem(f"a1_je_{li}")
    sjs = P.sem(f"a1_js_{li}")
    sjc = P.sem(f"a1_jc_{li}")
    sbs = P.sem(f"a1_bs_{li}")
    b = {s: s.n for s in [sw, svw, smm, sev, ssq, srs, snx, svm, sve, srow1,
                          spw, spm, spe, sxm, sxe, sld, snm, sne, snr, snb, sli, sbs,
                          sjm, sje, sjs, sjc]}
    bwq, bst, bvs = swq.base(), sst.base(), svs.base()

    @blk.sync
    def _(sy):
        sy.wait_ge(p2_ud.h, p2_ud_done)
        sy.wait_ge(p2_sl.h, p2_sl_done)
        sy.wait_ge(p2_fw.s[0].h, p2_fw_tot[0])
        sy.wait_ge(p2_fw.s[1].h, p2_fw_tot[1])
        sy.dma_start(out=biasA[:, 0:16], in_=W[(l, "bqkv")]).then_inc(sw.h, 16)
        sw.n += 16
        sy.dma_start(out=rows["r4"][0:1, 0:512],
                     in_=W[(l, "bqkv_v")][0:1, 0:512]).then_inc(sw.h, 16)
        sw.n += 16
        sy.dma_start(out=rows["r3"][0:1, 0:512],
                     in_=W[(l, "bqkv_v")][0:1, 512:1024]).then_inc(sw.h, 16)
        sw.n += 16
        sy.dma_start(out=smallb2[0:64, 0:96], in_=W[(l, "c_selI")]).then_inc(sw.h, 16)
        sw.n += 16
        sy.dma_start(out=smallb2[0:64, 96:192], in_=W[(l, "c_selJ")]).then_inc(sw.h, 16)
        sw.n += 16
        sy.dma_start(out=smallb2[0:96, 192:208], in_=W[(l, "c_G96")]).then_inc(sw.h, 16)
        sw.n += 16
        sy.dma_start(out=smallb2[0:16, 208:304], in_=W[(l, "c_E96")]).then_inc(sw.h, 16)
        sw.n += 16
        sy.dma_start(out=smallb2[0:96, 304:400], in_=W[(l, "c_J96")]).then_inc(sw.h, 16)
        sw.n += 16
        sy.dma_start(out=hr["h5"][0:1, 0:16],
                     in_=W[(l, "pl_scale")].rearrange("e one -> one e")).then_inc(sw.h, 16)
        sw.n += 16
        sy.dma_start(out=B["rbc"][0:16, 1:2], in_=W[(l, "pl_scale")],
                     allow_slow_non_contiguous=True).then_inc(sw.h, 16)
        sw.n += 16

    @blk.gpsimd
    def _(g):
        g.wait_ge(p2_ud.h, p2_ud_done)
        g.wait_ge(p2_sl.h, p2_sl_done)
        g.wait_ge(p2_fw.s[0].h, p2_fw_tot[0])
        g.wait_ge(p2_fw.s[1].h, p2_fw_tot[1])
        for oc in range(16):
            if oc >= 2:
                g.wait_ge(smm.h, b[smm] + (oc - 1) * NT)
            swq.inc(g.dma_start(
                out=otb[:, 5664 + (oc % 2) * 1024:5664 + (oc % 2) * 1024 + 1024].rearrange(
                    "p (k m) -> p k m", k=8),
                in_=W[(l, "wqkv")][:, oc * 128:(oc + 1) * 128].rearrange(
                    "(k p) m -> p k m", p=128)), oc)
        for vc in range(2):
            svw.inc(g.dma_start(
                out=otb[:, 8192 + vc * 4096:8192 + (vc + 1) * 4096].rearrange(
                    "p (k m) -> p k m", k=8),
                in_=W[(l, "wqkv")][:, 2048 + vc * 512:2048 + (vc + 1) * 512].rearrange(
                    "(k p) m -> p k m", p=128)), 16)
        spw.inc(g.dma_start(
            out=otb[:, 16384:16896].rearrange("p (k m) -> p k m", k=8),
            in_=W[(l, "w1p")].rearrange("(k p) m -> p k m", p=128)), 16)
        spw.inc(g.dma_start(
            out=otb[:, 16896:17408].rearrange("p (k m) -> p k m", k=8),
            in_=W[(l, "w2p")].rearrange("(k p) m -> p k m", p=128)), 16)

    @blk.vector
    def _(v):
        v.wait_ge(p2_ud.h, p2_ud_done)
        v.wait_ge(p2_sl.h, p2_sl_done)
        for i in range(2):
            srow1.inc(v.memset(otb[:, 3584 + i * 1040 + 64:3584 + i * 1040 + 1040:65], 1.0))
        v.memset(hr["h1"][0:1, 0:4], 0.0)
        v.wait_ge(sw.h, b[sw] + 16 * 3)
        sbs.inc(v.tensor_scalar(out=biasA[:, 0:8], in0=biasA[:, 0:8],
                                scalar1=0.125, scalar2=None, op0=OP.mult))
        v.wait_ge(sbs.h, b[sbs] + 1)
        P.ve(v, v.tensor_copy(out=rowsb["b5"][0:1, 0:512], in_=rows["r4"][0:1, 0:512]))
        srow1.inc(v.tensor_copy(out=rowsb["b5"][0:1, 512:1024],
                                in_=rows["r3"][0:1, 0:512]))
        v.wait_ge(sw.h, b[sw] + 16 * 10)
        sjc.inc(v.tensor_copy(out=B["cwtb"][0:16, 0:1], in_=B["rbc"][0:16, 1:2]))

    @blk.tensor
    def _(q):
        q.wait_ge(sw.h, b[sw] + 16 * 8)
        q.wait_ge(sjc.h, b[sjc] + 1)
        sjm.inc(q.matmul(out=ps[6][0:96, 0:1], lhsT=smallb2[0:16, 208:304],
                         rhs=B["cwtb"][0:16, 0:1], start=True, stop=True))

    @blk.vector
    def _(v):
        v.wait_ge(sjm.h, b[sjm] + 1)
        P.ve(v, v.tensor_copy(out=B["rbc"][:, 0:1][0:96, 0:1], in_=ps[6][0:96, 0:1]))
        sjc.inc(v.tensor_scalar(out=smallb2[0:96, 400:496], in0=smallb2[0:96, 304:400],
                                scalar1=B["rbc"][0:96, 0:1], scalar2=None, op0=OP.mult))

    @blk.tensor
    def _(q):
        for oc in range(16):
            swq.wait_done(q, bwq, oc)
            for tt in range(NT):
                step = oc * NT + tt
                pb = step % 2
                q.wait_ge(sap.h, sap_done - NT + tt + 1)
                if step >= 2:
                    q.wait_ge(sev.h, b[sev] + step - 1)
                wb0 = 5664 + (oc % 2) * 1024
                for c in range(NCH):
                    inst = q.matmul(out=ps[pb][:, 0:TT],
                                    lhsT=otb[:, wb0 + c * 128:wb0 + (c + 1) * 128],
                                    rhs=xnb[:, c * T + tt * TT:c * T + tt * TT + TT],
                                    start=(c == 0), stop=(c == NCH - 1))
                smm.inc(inst)
                q.wait_ge(ssq.h, b[ssq] + step + 1)
                if step >= 1:
                    q.wait_ge(snx.h, b[snx] + 2 * step)
                q.matmul(out=ps[2][0:1, 0:TT], lhsT=onec[0:64, 0:1],
                         rhs=otb[0:64, 512:512 + TT], start=True, stop=True,
                         tile_position=(0, 0))
                srs.inc(q.matmul(out=ps[3][0:1, 0:TT], lhsT=onec[64:128, 0:1],
                                 rhs=otb[64:128, 512:512 + TT], start=True, stop=True,
                                 tile_position=(64, 0)))

    @blk.scalar
    def _(s):
        s.wait_ge(sbs.h, b[sbs] + 1)
        for oc in range(16):
            for tt in range(NT):
                step = oc * NT + tt
                pb = step % 2
                s.wait_ge(smm.h, b[smm] + step + 1)
                if step >= 1:
                    s.wait_ge(ssq.h, b[ssq] + step)
                if step >= 2:
                    sst.wait_done(s, bst, step - 2)
                eb0 = (step % 2) * 2048
                sev.inc(s.activation(out=otb[:, eb0:eb0 + TT], in_=ps[pb][:, 0:TT],
                                     func=AF.Identity, bias=biasA[:, oc:oc + 1],
                                     scale=(0.125 if oc < 8 else 1.0)))

    @blk.vector
    def _(v):
        for oc in range(16):
            for tt in range(NT):
                step = oc * NT + tt
                v.wait_ge(sev.h, b[sev] + step + 1)
                eb0 = (step % 2) * 2048
                ssq.inc(v.tensor_mul(out=otb[:, 512:512 + TT], in0=otb[:, eb0:eb0 + TT],
                                     in1=otb[:, eb0:eb0 + TT]))
                v.wait_ge(srs.h, b[srs] + step + 1)
                slot = 0 if oc < 8 else 1
                P.ve(v, v.reduce_max(out=hr["h4"][0:1, 0:1], in_=ps[2][0:1, 0:TT],
                                     axis=AX.X))
                P.ve(v, v.reduce_max(out=hr["h4"][0:1, 1:2], in_=ps[3][0:1, 0:TT],
                                     axis=AX.X))
                P.ve(v, v.tensor_max(out=hr["h4"][0:1, 2:3], in0=hr["h4"][0:1, 0:1],
                                     in1=hr["h4"][0:1, 1:2]))
                snx.inc(v.tensor_max(out=hr["h1"][0:1, slot:slot + 1],
                                     in0=hr["h1"][0:1, slot:slot + 1],
                                     in1=hr["h4"][0:1, 2:3]), 2)

    @blk.sync
    def _(sy):
        for oc in range(16):
            pack = qpack if oc < 8 else kpack
            prow = (oc % 8) * 128
            for tt in range(NT):
                step = oc * NT + tt
                sy.wait_ge(ssq.h, b[ssq] + step + 1)
                eb0 = (step % 2) * 2048
                sst.inc(sy.dma_start(out=pack[prow:prow + 128, tt * TT:tt * TT + TT],
                                     in_=otb[:, eb0:eb0 + TT]), step)

    @blk.tensor
    def _(q):
        q.wait_ge(svw.h, b[svw] + 32)
        q.wait_ge(srow1.h, b[srow1] + 3)
        q.wait_ge(sap.h, sap_done)
        for tk in range(32):
            for vc in range(2):
                step = tk * 2 + vc
                pb = step % 2
                if step >= 2:
                    q.wait_ge(sve.h, b[sve] + step - 1)
                q.matmul(out=ps[pb][:, 0:TT], lhsT=oner[0:1, 0:128],
                         rhs=rowsb["b5"][0:1, vc * 512:vc * 512 + 512],
                         start=True, stop=False)
                for c in range(NCH):
                    inst = q.matmul(
                        out=ps[pb][:, 0:TT],
                        lhsT=xnb[:, c * T + tk * 128:c * T + tk * 128 + 128],
                        rhs=otb[:, 8192 + vc * 4096 + c * 512:8192 + vc * 4096 + (c + 1) * 512],
                        start=False, stop=(c == NCH - 1))
                svm.inc(inst)

    @blk.scalar
    def _(s):
        for tk in range(32):
            for vc in range(2):
                step = tk * 2 + vc
                pb = step % 2
                s.wait_ge(svm.h, b[svm] + step + 1)
                if tk >= 2 and vc == 0:
                    svs.wait_done(s, bvs, tk - 2)
                vr0 = 3584 + (tk % 2) * 1040
                sve.inc(s.copy(
                    out=otb[:, vr0 + vc * 520:vr0 + (vc + 1) * 520].rearrange(
                        "p (h d) -> p h d", h=8)[:, :, 0:64],
                    in_=ps[pb][:, 0:TT].rearrange("p (h d) -> p h d", h=8)))

    @blk.sync
    def _(sy):
        for tk in range(32):
            sy.wait_ge(sve.h, b[sve] + 2 * (tk + 1))
            svs.inc(sy.dma_start(out=vpack[:, tk * 1040:(tk + 1) * 1040],
                                 in_=otb[:, 3584 + (tk % 2) * 1040:3584 + (tk % 2) * 1040 + 1040]),
                    tk)

    @blk.tensor
    def _(q):
        q.wait_ge(spw.h, b[spw] + 32)
        q.wait_ge(sev.h, b[sev] + 128)
        q.wait_ge(sve.h, b[sve] + 64)
        for tt in range(NT):
            t0 = tt * TT
            q.wait_ge(sap.h, sap_done - NT + tt + 1)
            for pi in range(2):
                if tt >= 1 and pi == 0:
                    q.wait_ge(sne.h, b[sne] + tt)
                if tt >= 1 or pi >= 1:
                    q.wait_ge(spe.h, b[spe] + tt * 2 + pi - 1 + 1)
                w0 = 16384 + pi * 512
                for c in range(NCH):
                    inst = q.matmul(out=ps[4][0:64, 0:TT],
                                    lhsT=otb[:, w0 + c * 64:w0 + (c + 1) * 64],
                                    rhs=xnb[:, c * T + t0:c * T + t0 + TT],
                                    start=(c == 0), stop=(c == NCH - 1))
                spm.inc(inst)
            q.wait_ge(spe.h, b[spe] + 2 * (tt + 1))
            for pr in range(4):
                src_p = 1024 if pr < 2 else 1536
                sel0 = 0 if pr % 2 == 0 else 96
                if tt >= 1 and pr == 0:
                    q.wait_ge(sld.h, b[sld] + 2 * tt)
                if tt >= 1 or pr >= 2:
                    q.wait_ge(sxe.h, b[sxe] + tt * 4 + pr - 2 + 1)
                sxm.inc(q.matmul(out=ps[pr % 2][0:96, 0:TT],
                                 lhsT=smallb2[0:64, sel0:sel0 + 96],
                                 rhs=otb[0:64, src_p:src_p + TT], start=True, stop=True))
            q.wait_ge(sld.h, b[sld] + 2 * (tt + 1))
            snm.inc(q.matmul(out=ps[4][0:16, 0:TT], lhsT=smallb2[0:96, 192:208],
                             rhs=cstb[0:96, 2560:2560 + TT], start=True, stop=True))
            q.wait_ge(snr.h, b[snr] + tt + 1)
            snb.inc(q.matmul(out=ps[5][0:96, 0:TT], lhsT=smallb2[0:16, 208:304],
                             rhs=B["cwtb"][0:16, 1:1 + TT] if False else
                             cstb[0:16, 3072 + 0:3072 + TT], start=True, stop=True))
            q.wait_ge(sli.h, b[sli] + tt + 1)
            q.wait_ge(sjc.h, b[sjc] + 2)
            if tt >= 1:
                q.wait_ge(sje.h, b[sje] + tt)
            sjm.inc(q.matmul(out=ps[6][0:96, 0:TT], lhsT=smallb2[0:96, 400:496],
                             rhs=cstb[0:96, 3584 + t0 // 8:3584 + t0 // 8 + TT] if False else
                             cstb[0:96, 4096 + 0:4096 + TT], start=True, stop=True))

    @blk.scalar
    def _(s):
        for tt in range(NT):
            for pi in range(2):
                s.wait_ge(spm.h, b[spm] + tt * 2 + pi + 1)
                spe.inc(s.copy(out=otb[0:64, 1024 + pi * 512:1024 + pi * 512 + TT],
                               in_=ps[4][0:64, 0:TT]))
            for pr in range(4):
                s.wait_ge(sxm.h, b[sxm] + tt * 4 + pr + 1)
                spe.inc(s.copy(out=cstb[0:96, pr * 512:pr * 512 + TT],
                               in_=ps[pr % 2][0:96, 0:TT]), 0) if False else None
                sxe.inc(s.copy(out=cstb[0:96, pr * 512:pr * 512 + TT],
                               in_=ps[pr % 2][0:96, 0:TT]))
            s.wait_ge(snm.h, b[snm] + tt + 1)
            sne.inc(s.activation(out=rows["r1"][0:1, 0:16][0:1, 0:16] if False else
                                 cstb[0:16, 3584:3584 + TT], in_=ps[4][0:16, 0:TT],
                                 func=AF.Sqrt, scale=1.0), 0) if False else None
            sne.inc(s.activation(out=B["ar3"][0:16, 1408:1408 + TT], in_=ps[4][0:16, 0:TT],
                                 func=AF.Sqrt, scale=1.0))

    @blk.vector
    def _(v):
        for tt in range(NT):
            v.wait_ge(sxe.h, b[sxe] + 4 * (tt + 1))
            P.ve(v, v.tensor_mul(out=cstb[0:96, 2048:2048 + TT], in0=cstb[0:96, 0:TT],
                                 in1=cstb[0:96, 1536:1536 + TT]))
            P.ve(v, v.tensor_mul(out=cstb[0:96, 2560:2560 + TT], in0=cstb[0:96, 512:512 + TT],
                                 in1=cstb[0:96, 1024:1024 + TT]))
            P.ve(v, v.tensor_sub(out=cstb[0:96, 2048:2048 + TT],
                                 in0=cstb[0:96, 2048:2048 + TT],
                                 in1=cstb[0:96, 2560:2560 + TT]))
            sld.inc(v.engine_nop())
            P.ve(v, v.tensor_mul(out=cstb[0:96, 2560:2560 + TT],
                                 in0=cstb[0:96, 2048:2048 + TT],
                                 in1=cstb[0:96, 2048:2048 + TT]))
            sld.inc(v.engine_nop())
            v.wait_ge(sne.h, b[sne] + tt + 1)
            P.ve(v, v.tensor_scalar(out=B["ar3"][0:16, 1408:1408 + TT],
                                    in0=B["ar3"][0:16, 1408:1408 + TT],
                                    scalar1=1e-12, scalar2=None, op0=OP.max))
            P.ve(v, v.reciprocal(out=B["ar3"][0:16, 3584:3584 + TT],
                                 in_=B["ar3"][0:16, 1408:1408 + TT]))
            snr.inc(v.tensor_copy(out=cstb[0:16, 3072:3072 + TT],
                                  in_=B["ar3"][0:16, 3584:3584 + TT]))
            v.wait_ge(snb.h, b[snb] + tt + 1)
            if tt >= 1:
                v.wait_ge(sjs.h, b[sjs] + 32 * (tt - 1) + 16)
                v.wait_ge(sjm.h, b[sjm] + 1 + tt)
            P.ve(v, v.tensor_mul(out=cstb[0:96, 4096:4096 + TT],
                                 in0=cstb[0:96, 2048:2048 + TT], in1=ps[5][0:96, 0:TT]))
            sli.inc(v.engine_nop())
            v.wait_ge(sjm.h, b[sjm] + 1 + tt + 1)
            if tt >= 1:
                v.wait_ge(sjs.h, b[sjs] + 32 * tt)
            sje.inc(v.tensor_copy(out=cstb[0:96, 4608:4608 + TT], in_=ps[6][0:96, 0:TT]))

    @blk.sync
    def _(sy):
        for tt in range(NT):
            t0 = tt * TT
            sy.wait_ge(sli.h, b[sli] + tt + 1)
            sjs.inc(sy.dma_start(out=B["jqpack"][0:96, t0:t0 + TT],
                                 in_=cstb[0:96, 4096:4096 + TT]).then_inc(sjs.h, 0) if False
                    else sy.dma_start(out=B["jqpack"][0:96, t0:t0 + TT],
                                      in_=cstb[0:96, 4096:4096 + TT]), 16)
            sy.wait_ge(sje.h, b[sje] + tt + 1)
            sjs.inc(sy.dma_start(out=B["jkpack"][0:96, t0:t0 + TT],
                                 in_=cstb[0:96, 4608:4608 + TT]), 16)



def emit_attn_l1(P, l, li, B, W):
    blk = P.block
    xnb, otb, ps, evb = B["xnb"], B["otb"], B["ps"], B["evb"]
    rows, rowsb, hr = B["rows"], B["rowsb"], B["hr"]
    oner, onec = B["oner"], B["onec"]
    qpack, kpack, vpack = B["qpack"], B["kpack"], B["vpack"]
    gates = [P.sem(f"a1_{nm}_{li}") for nm in
             ["mm", "vm", "pm", "sq", "nx", "ne", "nr", "li", "je", "ev", "ve"]]
    gate_tot = [(s.h, s.n) for s in gates]
    gpair_tot = []
    for nm in ["st", "vs"]:
        pr = P.pair(f"a1_{nm}_{li}")
        gpair_tot += [(pr.s[0].h, pr.s[0].n), (pr.s[1].h, pr.s[1].n)]
    sjs_t = P.sem(f"a1_js_{li}")
    gate_tot.append((sjs_t.h, sjs_t.n))
    sxe_t = P.sem(f"a1_xe_{li}")
    gate_tot.append((sxe_t.h, sxe_t.n))

    sld5 = P.sem(f"b1_ld_{li}")
    smk0 = P.sem(f"b1_mk_{li}")
    ssc = P.sem(f"b1_sc_{li}")
    sms = P.sem(f"b1_ms_{li}")
    sex = P.sem(f"b1_ex_{li}")
    spv = P.sem(f"b1_pv_{li}")
    sdc = P.sem(f"b1_dc_{li}")
    sdn = P.sem(f"b1_dn_{li}")
    srb = P.sem(f"b1_rb_{li}")
    snv = P.sem(f"b1_nv_{li}")
    sno = P.sem(f"b1_no_{li}")
    sbb1 = P.sem(f"b1_b1_{li}")
    sbb2 = P.sem(f"b1_b2_{li}")
    sbb4 = P.sem(f"b1_b4_{li}")
    b = {s: s.n for s in [sld5, smk0, ssc, sms, sex, spv, sdc, sdn, srb, snv,
                          sno, sbb1, sbb2, sbb4]}

    KH0 = lambda h: (h % 2) * 8192
    QH0 = lambda h: (h % 2) * 8192 + 4096
    VH0 = lambda h: 16384 + (h % 2) * 2080
    E0 = 20544
    MK0 = 21568

    jobs = [(h, qg) for h in range(16) for qg in range(8)]
    nk_of = lambda qg: 4 * (qg + 1)
    step_base = [0]
    for (h, qg) in jobs:
        step_base.append(step_base[-1] + nk_of(qg))
    SH = step_base[8]
    nmask_upto = []
    m = 0
    for (h, qg) in jobs:
        for ki in range(nk_of(qg)):
            if ki >= nk_of(qg) - 4:
                m += 1
            nmask_upto.append(m)

    @blk.gpsimd
    def _(g):
        for hh, nn in gate_tot:
            g.wait_ge(hh, nn)
        for hh, nn in gpair_tot:
            g.wait_ge(hh, nn)
        for kd in range(4):
            smk0.inc(g.dma_start(out=xnb[:, MK0 + kd * 512:MK0 + (kd + 1) * 512],
                                 in_=W[(l, "c_l1m")][kd]), 16)
        for h in range(16):
            if h >= 2:
                g.wait_ge(ssc.h, b[ssc] + SH * (h - 1))
                g.wait_ge(spv.h, b[spv] + SH * (h - 1))
            sld5.inc(g.dma_start(out=xnb[0:64, KH0(h):KH0(h) + T],
                                 in_=kpack[64 * h:64 * h + 64, :]), 16)
            sld5.inc(g.dma_start(out=xnb[64:70, KH0(h):KH0(h) + T],
                                 in_=B["jkpack"][6 * h:6 * h + 6, :]), 16)
            sld5.inc(g.dma_start(out=xnb[0:64, QH0(h):QH0(h) + T],
                                 in_=qpack[64 * h:64 * h + 64, :]), 16)
            sld5.inc(g.dma_start(out=xnb[64:70, QH0(h):QH0(h) + T],
                                 in_=B["jqpack"][6 * h:6 * h + 6, :]), 16)
            sld5.inc(g.dma_start(
                out=xnb[:, VH0(h):VH0(h) + 2080].rearrange("p (k w) -> p k w", k=32),
                in_=vpack[:, :].rearrange("p (k g w) -> p k g w", k=32, g=16)[:, :, h, :]),
                16)

    @blk.scalar
    def _(s_):
        for hh, nn in gate_tot[:8]:
            s_.wait_ge(hh, nn)
        sbb1.inc(s_.activation(out=hr["h2"][0:1, 0:2], in_=hr["h1"][0:1, 0:2],
                               func=AF.Sqrt, scale=1.0))

    @blk.vector
    def _(v):
        v.wait_ge(sbb1.h, b[sbb1] + 1)
        P.ve(v, v.tensor_mul(out=hr["h3"][0:1, 0:1], in0=hr["h2"][0:1, 0:1],
                             in1=hr["h2"][0:1, 1:2]))
        P.ve(v, v.tensor_scalar(out=hr["h4"][0:1, 0:16], in0=hr["h5"][0:1, 0:16],
                                scalar1=-1.0, scalar2=None, op0=OP.mult))
        P.ve(v, v.reduce_max(out=hr["h3"][0:1, 1:2], in_=hr["h5"][0:1, 0:16], axis=AX.X))
        P.ve(v, v.reduce_max(out=hr["h3"][0:1, 2:3], in_=hr["h4"][0:1, 0:16], axis=AX.X))
        P.ve(v, v.tensor_max(out=hr["h3"][0:1, 3:4], in0=hr["h3"][0:1, 1:2],
                             in1=hr["h3"][0:1, 2:3]))
        P.ve(v, v.tensor_add(out=hr["h3"][0:1, 4:5], in0=hr["h3"][0:1, 0:1],
                             in1=hr["h3"][0:1, 3:4]))
        P.ve(v, v.tensor_scalar(out=hr["h3"][0:1, 5:6], in0=hr["h3"][0:1, 4:5],
                                scalar1=-1.0, scalar2=None, op0=OP.mult))
        sbb2.inc(v.tensor_copy(out=B["rwb"][0:1, 32:33], in_=hr["h3"][0:1, 5:6]))

    @blk.tensor
    def _(q):
        q.wait_ge(sbb2.h, b[sbb2] + 1)
        sje_g = P.sem(f"a1_je_{li}")
        q.wait_ge(sje_g.h, sje_g.n)
        sbb1.inc(q.matmul(out=ps[6][:, 0:1], lhsT=oner[0:1, 0:128],
                          rhs=B["rwb"][0:1, 32:33], start=True, stop=True))

    @blk.scalar
    def _(s_):
        s_.wait_ge(sbb1.h, b[sbb1] + 2)
        sbb4.inc(s_.copy(out=B["bcol"][1][:, 0:1], in_=ps[6][:, 0:1]))

    @blk.tensor
    def _(q):
        for ji, (h, qg) in enumerate(jobs):
            nk = nk_of(qg)
            q0 = qg * 512
            N = 512
            q.wait_ge(sld5.h, b[sld5] + 80 * (h + 1))
            if ji == 0:
                q.wait_ge(smk0.h, b[smk0] + 64)

            def emit_pv(ki):
                s_ = step_base[ji] + ki
                q.wait_ge(sex.h, b[sex] + s_ + 1)
                if ki == 0:
                    q.wait_ge(snv.h, b[snv] + ji)
                spv.inc(q.matmul(out=ps[2][0:65, 0:N],
                                 lhsT=xnb[:, VH0(h) + ki * 65:VH0(h) + ki * 65 + 65],
                                 rhs=xnb[:, E0 + (s_ % 2) * 512:E0 + (s_ % 2) * 512 + N],
                                 start=(ki == 0), stop=(ki == nk - 1)))

            for ki in range(nk):
                s = step_base[ji] + ki
                if s >= 2:
                    q.wait_ge(sex.h, b[sex] + s - 1)
                ssc.inc(q.matmul(out=ps[s % 2][:, 0:N],
                                 lhsT=xnb[0:70, KH0(h) + ki * 128:KH0(h) + ki * 128 + 128],
                                 rhs=xnb[0:70, QH0(h) + q0:QH0(h) + q0 + N],
                                 start=True, stop=True))
                if ki >= 1:
                    emit_pv(ki - 1)
            emit_pv(nk - 1)
            q.wait_ge(sdn.h, b[sdn] + ji + 1)
            q.matmul(out=ps[6][0:64, 0:N], lhsT=oner[0:1, 0:64],
                     rhs=rowsb["b5"][0:1, 0:N], start=True, stop=False)
            srb.inc(q.matmul(out=ps[6][0:64, 0:N], lhsT=oner[0:1, 0:64],
                             rhs=rowsb["b6"][0:1, 0:N], start=False, stop=True))

    @blk.vector
    def _(v):
        for ji, (h, qg) in enumerate(jobs):
            nk = nk_of(qg)
            q0 = qg * 512
            N = 512
            for ki in range(nk):
                s = step_base[ji] + ki
                kd = ki - (nk - 4)
                if kd >= 0:
                    v.wait_ge(ssc.h, b[ssc] + s + 1)
                    sms.inc(v.tensor_add(out=ps[s % 2][:, 0:N], in0=ps[s % 2][:, 0:N],
                                         in1=xnb[:, MK0 + kd * 512:MK0 + kd * 512 + N]))
            v.wait_ge(spv.h, b[spv] + step_base[ji + 1])
            P.ve(v, v.tensor_copy(out=rows["r1"][0:1, 0:N], in_=ps[2][64:65, 0:N]))
            sdc.inc(v.engine_nop())
            P.ve(v, v.reciprocal(out=rows["r3"][0:1, 0:N], in_=rows["r1"][0:1, 0:N]))
            if ji >= 1:
                v.wait_ge(srb.h, b[srb] + ji)
            P.ve(v, v.tensor_copy(out=rowsb["b5"][0:1, 0:N], in_=rows["r3"][0:1, 0:N]))
            P.ve(v, v.tensor_sub(out=rows["r5"][0:1, 0:N], in0=rows["r3"][0:1, 0:N],
                                 in1=rowsb["b5"][0:1, 0:N]))
            sdn.inc(v.tensor_copy(out=rowsb["b6"][0:1, 0:N], in_=rows["r5"][0:1, 0:N]))
            v.wait_ge(snv.h, b[snv] + ji + 1)
            v.wait_ge(srb.h, b[srb] + ji + 1)
            r0 = 64 * (h % 2)
            sno.inc(v.tensor_mul(
                out=otb[r0:r0 + 64, (h // 2) * T + q0:(h // 2) * T + q0 + N],
                in0=evb[0:64, 0:N], in1=ps[6][0:64, 0:N]))

    @blk.scalar
    def _(s_):
        s_.wait_ge(sbb4.h, b[sbb4] + 1)
        for ji, (h, qg) in enumerate(jobs):
            nk = nk_of(qg)
            for ki in range(nk):
                s = step_base[ji] + ki
                if ki >= nk - 4:
                    s_.wait_ge(sms.h, b[sms] + nmask_upto[s])
                else:
                    s_.wait_ge(ssc.h, b[ssc] + s + 1)
                if s >= 2:
                    s_.wait_ge(spv.h, b[spv] + s - 1)
                sex.inc(s_.activation(out=xnb[:, E0 + (s % 2) * 512:E0 + (s % 2) * 512 + 512],
                                      in_=ps[s % 2][:, 0:512], func=AF.Exp,
                                      bias=B["bcol"][1][:, 0:1], scale=1.0))
            s_.wait_ge(sdc.h, b[sdc] + ji + 1)
            if ji >= 1:
                s_.wait_ge(sno.h, b[sno] + ji)
            snv.inc(s_.copy(out=evb[0:64, 0:512], in_=ps[2][0:64, 0:512]))



L2_LAMBDA_INIT = 0.8 - 0.6 * math.exp(-0.3 * 11)


def emit_qkv_l2(P, l, li, B, W):
    blk = P.block
    xnb, otb, ps = B["xnb"], B["otb"], B["ps"]
    rows, rowsb, hr = B["rows"], B["rowsb"], B["hr"]
    onec, oner = B["onec"], B["oner"]
    qpack, kpack, vpack = B["qpack"], B["kpack"], B["vpack"]
    sap = P.sem(f"ln_ap_{li}")
    sap_done = sap.n

    sw = P.sem(f"a2_w_{li}")
    swq = P.pair(f"a2_wq_{li}")
    svw = P.sem(f"a2_vw_{li}")
    smm = P.sem(f"a2_mm_{li}")
    sev = P.sem(f"a2_ev_{li}")
    ssq = P.sem(f"a2_sq_{li}")
    srs = P.sem(f"a2_rs_{li}")
    snx = P.sem(f"a2_nx_{li}")
    sst = P.pair(f"a2_st_{li}")
    svm = P.sem(f"a2_vm_{li}")
    sve = P.sem(f"a2_ve_{li}")
    svs = P.pair(f"a2_vs_{li}")
    srow1 = P.sem(f"a2_row1_{li}")
    slm = P.sem(f"a2_lm_{li}")
    b = {s: s.n for s in [sw, svw, smm, sev, ssq, srs, snx, svm, sve, srow1, slm]}
    bwq, bst, bvs = swq.base(), sst.base(), svs.base()

    @blk.gpsimd
    def _(g):
        g.wait_ge(sap.h, sap_done)
        for i, nm in enumerate(["lq1", "lk1", "lq2", "lk2"]):
            sw.inc(g.dma_start(out=rows["r1"][0:1, 64 * i:64 * i + 64],
                               in_=W[(l, nm)]), 16)
        for oc in range(32):
            wsrc = W[(l, "wq")] if oc < 16 else W[(l, "wk")]
            col0 = (oc % 16) * 128
            if oc >= 2:
                g.wait_ge(smm.h, b[smm] + (oc - 1) * NT)
            swq.inc(g.dma_start(
                out=otb[:, 5664 + (oc % 2) * 1024:5664 + (oc % 2) * 1024 + 1024].rearrange(
                    "p (k m) -> p k m", k=8),
                in_=wsrc[:, col0:col0 + 128].rearrange("(k p) m -> p k m", p=128)), oc)
        for vc in range(2):
            svw.inc(g.dma_start(
                out=otb[:, 8192 + vc * 4096:8192 + (vc + 1) * 4096].rearrange(
                    "p (k m) -> p k m", k=8),
                in_=W[(l, "wv")][:, vc * 512:(vc + 1) * 512].rearrange(
                    "(k p) m -> p k m", p=128)), 16)

    @blk.vector
    def _(v):
        v.wait_ge(sap.h, sap_done)
        for i in range(2):
            srow1.inc(v.memset(otb[:, 3584 + i * 1040 + 64:3584 + i * 1040 + 1040:65], 1.0))
        v.memset(hr["h1"][0:1, 0:32], 0.0)
        v.memset(hr["h2"][0:1, 0:32], 0.0)
        v.wait_ge(sw.h, b[sw] + 64)
        P.ve(v, v.tensor_mul(out=rows["r5"][0:1, 0:64], in0=rows["r1"][0:1, 0:64],
                             in1=rows["r1"][0:1, 64:128]))
        P.ve(v, v.reduce_sum(out=rows["r5"][0:1, 256:257], in_=rows["r5"][0:1, 0:64],
                             axis=AX.X))
        P.ve(v, v.tensor_mul(out=rows["r5"][0:1, 64:128], in0=rows["r1"][0:1, 128:192],
                             in1=rows["r1"][0:1, 192:256]))
        slm.inc(v.reduce_sum(out=rows["r5"][0:1, 257:258], in_=rows["r5"][0:1, 64:128],
                             axis=AX.X))

    @blk.scalar
    def _(s):
        s.wait_ge(slm.h, b[slm] + 1)
        slm.inc(s.activation(out=rows["r5"][0:1, 258:260], in_=rows["r5"][0:1, 256:258],
                             func=AF.Exp, bias=B["constc"][0:1, 2:3], scale=1.0))

    @blk.vector
    def _(v):
        v.wait_ge(slm.h, b[slm] + 2)
        P.ve(v, v.tensor_sub(out=rows["r5"][0:1, 260:261], in0=rows["r5"][0:1, 258:259],
                             in1=rows["r5"][0:1, 259:260]))
        P.ve(v, v.tensor_scalar(out=rows["r5"][0:1, 261:262], in0=rows["r5"][0:1, 260:261],
                                scalar1=L2_LAMBDA_INIT, scalar2=None, op0=OP.add))
        slm.inc(v.tensor_copy(out=B["rwb"][0:1, 34:35], in_=rows["r5"][0:1, 261:262]))

    @blk.tensor
    def _(q):
        q.wait_ge(slm.h, b[slm] + 3)
        slm.inc(q.matmul(out=ps[6][:, 0:1], lhsT=oner[0:1, 0:128],
                         rhs=B["rwb"][0:1, 34:35], start=True, stop=True))

    @blk.scalar
    def _(s):
        s.wait_ge(slm.h, b[slm] + 4)
        slm.inc(s.copy(out=B["bcol"][2][:, 0:1], in_=ps[6][:, 0:1]))

    @blk.tensor
    def _(q):
        for oc in range(32):
            swq.wait_done(q, bwq, oc)
            for tt in range(NT):
                step = oc * NT + tt
                pb = step % 2
                q.wait_ge(sap.h, sap_done - NT + tt + 1)
                if step >= 2:
                    q.wait_ge(sev.h, b[sev] + step - 1)
                wb0 = 5664 + (oc % 2) * 1024
                for c in range(NCH):
                    inst = q.matmul(out=ps[pb][:, 0:TT],
                                    lhsT=otb[:, wb0 + c * 128:wb0 + (c + 1) * 128],
                                    rhs=xnb[:, c * T + tt * TT:c * T + tt * TT + TT],
                                    start=(c == 0), stop=(c == NCH - 1))
                smm.inc(inst)
                q.wait_ge(ssq.h, b[ssq] + step + 1)
                if step >= 1:
                    q.wait_ge(snx.h, b[snx] + 2 * step)
                q.matmul(out=ps[2][0:1, 0:TT], lhsT=onec[0:64, 0:1],
                         rhs=otb[0:64, 512:512 + TT], start=True, stop=True,
                         tile_position=(0, 0))
                srs.inc(q.matmul(out=ps[3][0:1, 0:TT], lhsT=onec[64:128, 0:1],
                                 rhs=otb[64:128, 512:512 + TT], start=True, stop=True,
                                 tile_position=(64, 0)))

    @blk.scalar
    def _(s):
        for oc in range(32):
            for tt in range(NT):
                step = oc * NT + tt
                pb = step % 2
                s.wait_ge(smm.h, b[smm] + step + 1)
                if step >= 1:
                    s.wait_ge(ssq.h, b[ssq] + step)
                if step >= 2:
                    sst.wait_done(s, bst, step - 2)
                eb0 = (step % 2) * 2048
                sev.inc(s.activation(out=otb[:, eb0:eb0 + TT], in_=ps[pb][:, 0:TT],
                                     func=AF.Identity, bias=B["constc"][:, 2:3],
                                     scale=(0.125 if oc < 16 else 1.0)))

    @blk.vector
    def _(v):
        for oc in range(32):
            for tt in range(NT):
                step = oc * NT + tt
                v.wait_ge(sev.h, b[sev] + step + 1)
                eb0 = (step % 2) * 2048
                ssq.inc(v.tensor_mul(out=otb[:, 512:512 + TT], in0=otb[:, eb0:eb0 + TT],
                                     in1=otb[:, eb0:eb0 + TT]))
                v.wait_ge(srs.h, b[srs] + step + 1)
                hrow = hr["h1"] if oc < 16 else hr["h2"]
                sl = 2 * (oc % 16)
                P.ve(v, v.reduce_max(out=hr["h4"][0:1, 0:1], in_=ps[2][0:1, 0:TT],
                                     axis=AX.X))
                P.ve(v, v.tensor_max(out=hrow[0:1, sl:sl + 1], in0=hrow[0:1, sl:sl + 1],
                                     in1=hr["h4"][0:1, 0:1]))
                P.ve(v, v.reduce_max(out=hr["h4"][0:1, 1:2], in_=ps[3][0:1, 0:TT],
                                     axis=AX.X))
                snx.inc(v.tensor_max(out=hrow[0:1, sl + 1:sl + 2],
                                     in0=hrow[0:1, sl + 1:sl + 2],
                                     in1=hr["h4"][0:1, 1:2]), 2)

    @blk.sync
    def _(sy):
        for oc in range(32):
            pack = qpack if oc < 16 else kpack
            prow = (oc % 16) * 128
            for tt in range(NT):
                step = oc * NT + tt
                sy.wait_ge(ssq.h, b[ssq] + step + 1)
                eb0 = (step % 2) * 2048
                sst.inc(sy.dma_start(out=pack[prow:prow + 128, tt * TT:tt * TT + TT],
                                     in_=otb[:, eb0:eb0 + TT]), step)

    @blk.tensor
    def _(q):
        q.wait_ge(svw.h, b[svw] + 32)
        q.wait_ge(srow1.h, b[srow1] + 2)
        q.wait_ge(sap.h, sap_done)
        for tk in range(32):
            for vc in range(2):
                step = tk * 2 + vc
                pb = step % 2
                if step >= 2:
                    q.wait_ge(sve.h, b[sve] + step - 1)
                for c in range(NCH):
                    inst = q.matmul(
                        out=ps[pb][:, 0:TT],
                        lhsT=xnb[:, c * T + tk * 128:c * T + tk * 128 + 128],
                        rhs=otb[:, 8192 + vc * 4096 + c * 512:8192 + vc * 4096 + (c + 1) * 512],
                        start=(c == 0), stop=(c == NCH - 1))
                svm.inc(inst)

    @blk.scalar
    def _(s):
        for tk in range(32):
            for vc in range(2):
                step = tk * 2 + vc
                pb = step % 2
                s.wait_ge(svm.h, b[svm] + step + 1)
                if tk >= 2 and vc == 0:
                    svs.wait_done(s, bvs, tk - 2)
                vr0 = 3584 + (tk % 2) * 1040
                sve.inc(s.copy(
                    out=otb[:, vr0 + vc * 520:vr0 + (vc + 1) * 520].rearrange(
                        "p (h d) -> p h d", h=8)[:, :, 0:64],
                    in_=ps[pb][:, 0:TT].rearrange("p (h d) -> p h d", h=8)))

    @blk.sync
    def _(sy):
        for tk in range(32):
            sy.wait_ge(sve.h, b[sve] + 2 * (tk + 1))
            svs.inc(sy.dma_start(out=vpack[:, tk * 1040:(tk + 1) * 1040],
                                 in_=otb[:, 3584 + (tk % 2) * 1040:3584 + (tk % 2) * 1040 + 1040]),
                    tk)


def emit_ffn_l2(P, l, li, B, W):
    blk = P.block
    xnb, otb, ps = B["xnb"], B["otb"], B["ps"]
    evb, evb2 = B["evb"], B["evb2"]
    smallb, smallb2 = B["smallb"], B["smallb2"]
    hbuf, gbuf = B["hbuf"], B["gbuf"]
    sap = P.sem(f"ln_ap_{li}")
    sap_done = sap.n
    a2 = {nm: P.sem(f"a2_{nm}_{li}") for nm in ["mm", "vm", "ev", "ve", "sq"]}
    a2tot = {nm: s.n for nm, s in a2.items()}
    a2st = P.pair(f"a2_st_{li}")
    a2st_tot = a2st.total()
    a2vs = P.pair(f"a2_vs_{li}")
    a2vs_tot = a2vs.total()

    swg = P.pair(f"c2_wg_{li}")
    swd = P.sem(f"c2_wd_{li}")
    shl = P.pair(f"c2_hl_{li}")
    sAm = P.sem(f"c2_Am_{li}")
    sAe = P.sem(f"c2_Ae_{li}")
    sAh = P.sem(f"c2_Ah_{li}")
    sAs = P.pair(f"c2_As_{li}")
    sBm = P.sem(f"c2_Bm_{li}")
    sBe = P.sem(f"c2_Be_{li}")
    b = {s: s.n for s in [swd, sAm, sAe, sAh, sBm, sBe]}
    bswg, bshl, bsAs = swg.base(), shl.base(), sAs.base()
    sgw = P.pair(f"f_fw_{li}")
    bgw = sgw.base()

    @blk.gpsimd
    def _(g):
        g.wait_ge(sap.h, sap_done)
        for hc in range(24):
            if hc >= 2:
                g.wait_ge(sAm.h, b[sAm] + 16 * (hc - 1))
            swg.inc(g.dma_start(
                out=otb[:, 16384 + (hc % 2) * 1024:16384 + (hc % 2) * 1024 + 1024].rearrange(
                    "p (k m) -> p k m", k=8),
                in_=W[(l, "wg_t")][:, hc * 128:(hc + 1) * 128].rearrange(
                    "(k p) m -> p k m", p=128)), 2 * hc)
            swg.inc(g.dma_start(
                out=otb[:, 18432 + (hc % 2) * 1024:18432 + (hc % 2) * 1024 + 1024].rearrange(
                    "p (k m) -> p k m", k=8),
                in_=W[(l, "wu_t")][:, hc * 128:(hc + 1) * 128].rearrange(
                    "(k p) m -> p k m", p=128)), 2 * hc + 1)
        g.wait_ge(a2st.s[0].h, a2st_tot[0])
        g.wait_ge(a2st.s[1].h, a2st_tot[1])
        g.wait_ge(a2vs.s[0].h, a2vs_tot[0])
        g.wait_ge(a2vs.s[1].h, a2vs_tot[1])
        g.wait_ge(sAs.s[0].h, bsAs[0] + 16 * 96)
        g.wait_ge(sAs.s[1].h, bsAs[1] + 16 * 96)
        for mc in range(8):
            swd.inc(g.dma_start(
                out=otb[:, mc * 3072:(mc + 1) * 3072].rearrange("p (k m) -> p k m", k=24),
                in_=W[(l, "wd_t")][:, mc * 128:(mc + 1) * 128].rearrange(
                    "(k p) m -> p k m", p=128)), 16)

    @blk.sync
    def _(sy):
        for hc in range(24):
            for tc in range(8):
                st = hc * 8 + tc
                sy.wait_ge(sAh.h, b[sAh] + st + 1)
                sAs.inc(sy.dma_start(
                    out=hbuf[hc * 128:(hc + 1) * 128, tc * TT:(tc + 1) * TT],
                    in_=otb[:, 20480 + (st % 2) * 512:20480 + (st % 2) * 512 + 512]), st)

    @blk.sync
    def _(sy):
        sy.wait_ge(sAm.h, b[sAm] + 384)
        sy.wait_ge(a2["mm"].h, a2tot["mm"])
        sy.wait_ge(a2["vm"].h, a2tot["vm"])
        for tc in range(8):
            if tc >= 2:
                sy.wait_ge(sBm.h, b[sBm] + 8 * (tc - 1))
            shl.inc(sy.dma_start(
                out=xnb[:, (tc % 2) * 12288:(tc % 2) * 12288 + 12288].rearrange(
                    "p (k m) -> p k m", k=24),
                in_=hbuf[:, tc * 512:(tc + 1) * 512].rearrange("(k p) m -> p k m", p=128)),
                tc)

    @blk.tensor
    def _(q):
        q.wait_ge(a2["ev"].h, a2tot["ev"])
        q.wait_ge(a2["ve"].h, a2tot["ve"])
        q.wait_ge(a2["sq"].h, a2tot["sq"])
        for hc in range(24):
            swg.wait_done(q, bswg, 2 * hc)
            swg.wait_done(q, bswg, 2 * hc + 1)
            for tc in range(8):
                st = hc * 8 + tc
                psg = ps[st % 2]
                psu = ps[2 + st % 2]
                if st >= 2:
                    q.wait_ge(sAe.h, b[sAe] + 2 * (st - 1))
                for c in range(NCH):
                    inst = q.matmul(
                        out=psg[:, 0:TT],
                        lhsT=otb[:, 16384 + (hc % 2) * 1024 + c * 128:
                                 16384 + (hc % 2) * 1024 + c * 128 + 128],
                        rhs=xnb[:, c * T + tc * TT:c * T + tc * TT + TT],
                        start=(c == 0), stop=(c == NCH - 1))
                sAm.inc(inst)
                for c in range(NCH):
                    inst = q.matmul(
                        out=psu[:, 0:TT],
                        lhsT=otb[:, 18432 + (hc % 2) * 1024 + c * 128:
                                 18432 + (hc % 2) * 1024 + c * 128 + 128],
                        rhs=xnb[:, c * T + tc * TT:c * T + tc * TT + TT],
                        start=(c == 0), stop=(c == NCH - 1))
                sAm.inc(inst)

        q.wait_ge(swd.h, b[swd] + 128)
        for tc in range(8):
            shl.wait_done(q, bshl, tc)
            for mc in range(8):
                st = tc * 8 + mc
                psb = ps[4 + st % 2]
                if st >= 2:
                    q.wait_ge(sBe.h, b[sBe] + st - 1)
                for kc in range(24):
                    inst = q.matmul(
                        out=psb[:, 0:TT],
                        lhsT=otb[:, mc * 3072 + kc * 128:mc * 3072 + kc * 128 + 128],
                        rhs=xnb[:, (tc % 2) * 12288 + kc * 512:(tc % 2) * 12288 + kc * 512 + 512],
                        start=(kc == 0), stop=(kc == 23))
                sBm.inc(inst)

    @blk.scalar
    def _(s):
        for hc in range(24):
            for tc in range(8):
                st = hc * 8 + tc
                s.wait_ge(sAm.h, b[sAm] + 2 * (st + 1))
                if st >= 1:
                    s.wait_ge(sAh.h, b[sAh] + st)
                s.activation(out=smallb[:, 0:TT], in_=ps[st % 2][:, 0:TT],
                             func=AF.Silu, bias=B["constc"][:, 2:3], scale=1.0 / 127.0)
                sAe.inc(s.activation(out=smallb2[:, 0:TT], in_=ps[2 + st % 2][:, 0:TT],
                                     func=AF.Identity, bias=B["constc"][:, 2:3],
                                     scale=1.0 / 127.0), 2)
        for tc in range(8):
            for mc in range(8):
                st = tc * 8 + mc
                s.wait_ge(sBm.h, b[sBm] + st + 1)
                if st >= 2:
                    sgw.wait_done(s, bgw, st - 2)
                sBe.inc(s.activation(out=(evb if st % 2 == 0 else evb2)[:, 0:TT],
                                     in_=ps[4 + st % 2][:, 0:TT], func=AF.Silu,
                                     bias=B["constc"][:, 2:3], scale=1.0 / 127.0))

    @blk.vector
    def _(v):
        for hc in range(24):
            for tc in range(8):
                st = hc * 8 + tc
                v.wait_ge(sAe.h, b[sAe] + 2 * (st + 1))
                if st >= 2:
                    sAs.wait_done(v, bsAs, st - 2)
                sAh.inc(v.tensor_mul(out=otb[:, 20480 + (st % 2) * 512:
                                             20480 + (st % 2) * 512 + 512],
                                     in0=smallb[:, 0:TT], in1=smallb2[:, 0:TT]))

    @blk.gpsimd
    def _(g):
        for tc in range(8):
            for mc in range(8):
                st = tc * 8 + mc
                g.wait_ge(sBe.h, b[sBe] + st + 1)
                sgw.inc(g.dma_start(
                    out=gbuf[mc * 128:(mc + 1) * 128, tc * TT:(tc + 1) * TT],
                    in_=(evb if st % 2 == 0 else evb2)[:, 0:TT]), st)


def emit_attn_l2(P, l, li, B, W):
    blk = P.block
    xnb, otb, ps = B["xnb"], B["otb"], B["ps"]
    evb, evb2 = B["evb"], B["evb2"]
    rows, rowsb, hr = B["rows"], B["rowsb"], B["hr"]
    oner = B["oner"]
    qpack, kpack, vpack = B["qpack"], B["kpack"], B["vpack"]
    ar3 = B["ar3"]
    sBm_g = P.sem(f"c2_Bm_{li}")
    sBm_tot = sBm_g.n
    snx_g = P.sem(f"a2_nx_{li}")
    snx_tot = snx_g.n
    slm_g = P.sem(f"a2_lm_{li}")
    slm_tot = slm_g.n
    gpair_tot = []
    for nm in ["st", "vs"]:
        pr = P.pair(f"a2_{nm}_{li}")
        gpair_tot += [(pr.s[0].h, pr.s[0].n), (pr.s[1].h, pr.s[1].n)]

    sld3 = P.sem(f"b2_ld_{li}")
    ssc = P.sem(f"b2_sc_{li}")
    sex = P.sem(f"b2_ex_{li}")
    spv = P.sem(f"b2_pv_{li}")
    sdc = P.sem(f"b2_dc_{li}")
    sdn = P.sem(f"b2_dn_{li}")
    srb = P.sem(f"b2_rb_{li}")
    snv = P.sem(f"b2_nv_{li}")
    sno = P.sem(f"b2_no_{li}")
    sb1 = P.sem(f"b2_b1_{li}")
    sb2 = P.sem(f"b2_b2_{li}")
    sb3 = P.sem(f"b2_b3_{li}")
    sb4 = P.sem(f"b2_b4_{li}")
    sb5 = P.sem(f"b2_b5_{li}")
    b = {s: s.n for s in [sld3, ssc, sex, spv, sdc, sdn, srb, snv, sno,
                          sb1, sb2, sb3, sb4, sb5]}

    KH0 = lambda h: (h % 2) * 8192
    QH0 = lambda h: (h % 2) * 8192 + 4096
    VH0 = lambda h: 16384 + (h % 2) * 2080
    E1 = 20544
    E2 = 21568

    jobs = [(h, qg) for h in range(16) for qg in range(8)]
    NKT = 32
    SH = NKT * 8

    @blk.gpsimd
    def _(g):
        g.wait_ge(sBm_g.h, sBm_tot)
        for hh, nn in gpair_tot:
            g.wait_ge(hh, nn)
        for h in range(16):
            if h >= 2:
                g.wait_ge(ssc.h, b[ssc] + 2 * SH * (h - 1))
                g.wait_ge(spv.h, b[spv] + 2 * SH * (h - 1))
            sld3.inc(g.dma_start(out=xnb[:, KH0(h):KH0(h) + T],
                                 in_=kpack[128 * h:128 * h + 128, :]), 16)
            sld3.inc(g.dma_start(out=xnb[:, QH0(h):QH0(h) + T],
                                 in_=qpack[128 * h:128 * h + 128, :]), 16)
            sld3.inc(g.dma_start(
                out=xnb[:, VH0(h):VH0(h) + 2080].rearrange("p (k w) -> p k w", k=32),
                in_=vpack[:, :].rearrange("p (k g w) -> p k g w", k=32, g=16)[:, :, h, :]),
                16)

    @blk.vector
    def _(v):
        v.wait_ge(snx_g.h, snx_tot)
        sb1.inc(v.tensor_mul(out=hr["h4"][0:1, 0:32], in0=hr["h1"][0:1, 0:32],
                             in1=hr["h2"][0:1, 0:32]))

    @blk.scalar
    def _(s_):
        s_.wait_ge(sb1.h, b[sb1] + 1)
        sb2.inc(s_.activation(out=hr["h3"][0:1, 0:32], in_=hr["h4"][0:1, 0:32],
                              func=AF.Sqrt, scale=1.0))

    @blk.vector
    def _(v):
        v.wait_ge(sb2.h, b[sb2] + 1)
        P.ve(v, v.tensor_scalar(out=hr["h6"][0:1, 0:32], in0=hr["h3"][0:1, 0:32],
                                scalar1=-1.0, scalar2=None, op0=OP.mult))
        sb3.inc(v.tensor_copy(out=B["rwb"][0:1, 64:96], in_=hr["h6"][0:1, 0:32]))

    @blk.tensor
    def _(q):
        q.wait_ge(sb3.h, b[sb3] + 1)
        q.wait_ge(slm_g.h, slm_tot)
        sb4.inc(q.matmul(out=ps[6][:, 0:32], lhsT=oner[0:1, 0:128],
                         rhs=B["rwb"][0:1, 64:96], start=True, stop=True))

    @blk.scalar
    def _(s_):
        s_.wait_ge(sb4.h, b[sb4] + 1)
        sb5.inc(s_.copy(out=ar3[:, 2368:2400], in_=ps[6][:, 0:32]))

    @blk.tensor
    def _(q):
        for ji, (h, qg) in enumerate(jobs):
            q0 = qg * 512
            N = 512
            q.wait_ge(sld3.h, b[sld3] + 48 * (h + 1))

            def emit_pv(ki):
                s_ = ji * NKT + ki
                q.wait_ge(sex.h, b[sex] + 2 * (s_ + 1))
                if ki == 0:
                    q.wait_ge(snv.h, b[snv] + 2 * ji)
                first = (ki == 0)
                last = (ki == NKT - 1)
                q.matmul(out=ps[2][0:65, 0:N],
                         lhsT=xnb[:, VH0(h) + ki * 65:VH0(h) + ki * 65 + 65],
                         rhs=xnb[:, E1 + (s_ % 2) * 512:E1 + (s_ % 2) * 512 + N],
                         start=first, stop=last)
                spv.inc(q.matmul(out=ps[5][0:65, 0:N],
                                 lhsT=xnb[:, VH0(h) + ki * 65:VH0(h) + ki * 65 + 65],
                                 rhs=xnb[:, E2 + (s_ % 2) * 512:E2 + (s_ % 2) * 512 + N],
                                 start=first, stop=last), 2)

            for ki in range(NKT):
                s = ji * NKT + ki
                bA = ps[0 + 3 * (s % 2)]
                bB = ps[1 + 3 * (s % 2)]
                if s >= 2:
                    q.wait_ge(sex.h, b[sex] + 2 * (s - 1))
                q.matmul(out=bA[:, 0:N],
                         lhsT=xnb[0:64, KH0(h) + ki * 128:KH0(h) + ki * 128 + 128],
                         rhs=xnb[0:64, QH0(h) + q0:QH0(h) + q0 + N], start=True,
                         stop=True, tile_position=(0, 0))
                ssc.inc(q.matmul(out=bB[:, 0:N],
                                 lhsT=xnb[64:128, KH0(h) + ki * 128:KH0(h) + ki * 128 + 128],
                                 rhs=xnb[64:128, QH0(h) + q0:QH0(h) + q0 + N], start=True,
                                 stop=True, tile_position=(64, 0)), 2)
                if ki >= 1:
                    emit_pv(ki - 1)
            emit_pv(NKT - 1)
            q.wait_ge(sdn.h, b[sdn] + ji + 1)
            q.matmul(out=ps[6][0:64, 0:N], lhsT=oner[0:1, 0:64],
                     rhs=rowsb["b5"][0:1, 0:N], start=True, stop=False)
            q.matmul(out=ps[6][0:64, 0:N], lhsT=oner[0:1, 0:64],
                     rhs=rowsb["b6"][0:1, 0:N], start=False, stop=True)
            q.matmul(out=ps[6][64:128, 0:N], lhsT=oner[0:1, 0:64],
                     rhs=rowsb["b5"][0:1, 512:512 + N], start=True, stop=False)
            srb.inc(q.matmul(out=ps[6][64:128, 0:N], lhsT=oner[0:1, 0:64],
                             rhs=rowsb["b6"][0:1, 512:512 + N], start=False, stop=True))

    @blk.vector
    def _(v):
        for ji, (h, qg) in enumerate(jobs):
            q0 = qg * 512
            N = 512
            v.wait_ge(spv.h, b[spv] + 2 * (ji + 1) * NKT)
            P.ve(v, v.tensor_copy(out=rows["r1"][0:1, 0:N], in_=ps[2][64:65, 0:N]))
            sdc.inc(v.tensor_copy(out=rows["r2"][0:1, 0:N], in_=ps[5][64:65, 0:N]))
            P.ve(v, v.reciprocal(out=rows["r3"][0:1, 0:N], in_=rows["r1"][0:1, 0:N]))
            P.ve(v, v.reciprocal(out=rows["r4"][0:1, 0:N], in_=rows["r2"][0:1, 0:N]))
            if ji >= 1:
                v.wait_ge(srb.h, b[srb] + ji)
            P.ve(v, v.tensor_copy(out=rowsb["b5"][0:1, 0:N], in_=rows["r3"][0:1, 0:N]))
            P.ve(v, v.tensor_sub(out=rows["r5"][0:1, 0:N], in0=rows["r3"][0:1, 0:N],
                                 in1=rowsb["b5"][0:1, 0:N]))
            P.ve(v, v.tensor_copy(out=rowsb["b6"][0:1, 0:N], in_=rows["r5"][0:1, 0:N]))
            P.ve(v, v.tensor_copy(out=rowsb["b5"][0:1, 512:512 + N], in_=rows["r4"][0:1, 0:N]))
            P.ve(v, v.tensor_sub(out=rows["r5"][0:1, 0:N], in0=rows["r4"][0:1, 0:N],
                                 in1=rowsb["b5"][0:1, 512:512 + N]))
            sdn.inc(v.tensor_copy(out=rowsb["b6"][0:1, 512:512 + N],
                                  in_=rows["r5"][0:1, 0:N]))
            v.wait_ge(snv.h, b[snv] + 2 * (ji + 1))
            v.wait_ge(srb.h, b[srb] + ji + 1)
            P.ve(v, v.tensor_mul(out=evb[64:128, 0:N], in0=evb[0:64, 0:N],
                                 in1=ps[6][0:64, 0:N]))
            P.ve(v, v.tensor_mul(out=evb2[64:128, 0:N], in0=evb2[0:64, 0:N],
                                 in1=ps[6][64:128, 0:N]))
            P.ve(v, v.tensor_scalar(out=evb2[64:128, 0:N], in0=evb2[64:128, 0:N],
                                    scalar1=B["bcol"][2][64:128, 0:1], scalar2=None,
                                    op0=OP.mult))
            r0 = 64 * (h % 2)
            sno.inc(v.tensor_sub(
                out=otb[r0:r0 + 64, (h // 2) * T + q0:(h // 2) * T + q0 + N],
                in0=evb[64:128, 0:N], in1=evb2[64:128, 0:N]))

    @blk.scalar
    def _(s_):
        s_.wait_ge(sb5.h, b[sb5] + 1)
        for ji, (h, qg) in enumerate(jobs):
            for ki in range(NKT):
                s = ji * NKT + ki
                s_.wait_ge(ssc.h, b[ssc] + 2 * (s + 1))
                if s >= 2:
                    s_.wait_ge(spv.h, b[spv] + 2 * (s - 1))
                bA = ps[0 + 3 * (s % 2)]
                bB = ps[1 + 3 * (s % 2)]
                s_.activation(out=xnb[:, E1 + (s % 2) * 512:E1 + (s % 2) * 512 + 512],
                              in_=bA[:, 0:512], func=AF.Exp,
                              bias=ar3[:, 2368 + 2 * h:2368 + 2 * h + 1], scale=1.0)
                sex.inc(s_.activation(out=xnb[:, E2 + (s % 2) * 512:E2 + (s % 2) * 512 + 512],
                                      in_=bB[:, 0:512], func=AF.Exp,
                                      bias=ar3[:, 2368 + 2 * h + 1:2368 + 2 * h + 2],
                                      scale=1.0), 2)
            s_.wait_ge(sdc.h, b[sdc] + ji + 1)
            if ji >= 1:
                s_.wait_ge(sno.h, b[sno] + ji)
            s_.copy(out=evb[0:64, 0:512], in_=ps[2][0:64, 0:512])
            snv.inc(s_.copy(out=evb2[0:64, 0:512], in_=ps[5][0:64, 0:512]), 2)


def emit_proj_update_l2(P, l, li, xi, xo, B, W):
    blk = P.block
    xnb, otb, ps, ar3 = B["xnb"], B["otb"], B["ps"], B["ar3"]
    evb = B["evb"]
    rows, rowsb = B["rows"], B["rowsb"]
    onec, oner = B["onec"], B["oner"]
    cstb = B["cstb"]
    biasA = B["biasA"]
    sno = P.sem(f"b2_no_{li}")
    sno_done = sno.n
    for nm in ["sc", "ex", "pv"]:
        pass
    ssc_g = P.sem(f"b2_sc_{li}")
    ssc_tot = ssc_g.n
    sex_g = P.sem(f"b2_ex_{li}")
    sex_tot = sex_g.n
    sg = P.pair(f"f_fw_{li}")
    sg_tot = sg.total()

    sw = P.sem(f"p4_w_{li}")
    sgl = P.pair(f"p4_g_{li}")
    sxl = P.pair(f"p4_x_{li}")
    smm = P.sem(f"p4_mm_{li}")
    sev = P.sem(f"p4_ev_{li}")
    ssl = P.sem(f"p4_sl_{li}")
    sln = P.sem(f"p4_ln_{li}")
    slr = P.sem(f"p4_lr_{li}")
    srs2 = P.sem(f"p4_rs_{li}")
    srw2 = P.sem(f"p4_rw_{li}")
    slb = P.sem(f"p4_lb_{li}")
    sdv = P.sem(f"p4_dv_{li}")
    supd = P.pair(f"upd_done_{li}")
    b = {s: s.n for s in [sw, smm, sev, ssl, sln, slr, srs2, srw2, slb, sdv]}
    bg, bx2, bupd = sgl.base(), sxl.base(), supd.base()

    @blk.gpsimd
    def _(g):
        g.wait_ge(sno.h, sno_done)
        g.wait_ge(ssc_g.h, ssc_tot)
        g.wait_ge(sex_g.h, sex_tot)
        sw.inc(g.dma_start(
            out=xnb[:, 0:8192].rearrange("p (k m) -> p k m", k=8),
            in_=W[(l, "wproj")].rearrange("(k p) m -> p k m", p=128)), 16)
        sw.inc(g.dma_start(out=biasA[:, 0:8], in_=W[(l, "gamma")]), 16)
        sw.inc(g.dma_start(out=biasA[:, 8:16], in_=W[(l, "beta")]), 16)

    @blk.sync
    def _(sy):
        sy.wait_ge(sg.s[0].h, sg_tot[0])
        sy.wait_ge(sg.s[1].h, sg_tot[1])
        for tc in range(NT):
            for mc in range(NCH):
                st = tc * NCH + mc
                pb = st % 2
                if st >= 2:
                    sy.wait_ge(sdv.h, b[sdv] + 2 * (st - 1))
                sgl.inc(sy.dma_start(out=ar3[:, 4096 + pb * 512:4096 + pb * 512 + 512],
                             in_=B["gbuf"][mc * 128:(mc + 1) * 128, tc * TT:(tc + 1) * TT]),
                        st)
                sxl.inc(sy.dma_start(out=ar3[:, 5120 + pb * 512:5120 + pb * 512 + 512],
                                     in_=xi[mc * 128:(mc + 1) * 128, tc * TT:(tc + 1) * TT]),
                        st)

    @blk.tensor
    def _(q):
        q.wait_ge(sw.h, b[sw] + 48)
        for tc in range(NT):
            for mc in range(NCH):
                st = tc * NCH + mc
                pb = st % 2
                if st >= 2:
                    q.wait_ge(sev.h, b[sev] + st - 1)
                for fc in range(NCH):
                    inst = q.matmul(
                        out=ps[pb][:, 0:TT],
                        lhsT=xnb[:, fc * 1024 + mc * 128:fc * 1024 + mc * 128 + 128],
                        rhs=otb[:, fc * T + tc * TT:fc * T + tc * TT + TT],
                        start=(fc == 0), stop=(fc == NCH - 1))
                smm.inc(inst)
            q.wait_ge(ssl.h, b[ssl] + 8 * (tc + 1))
            if tc >= 1:
                q.wait_ge(slr.h, b[slr] + tc)
            for mc in range(NCH):
                q.matmul(out=ps[4][0:1, 0:TT], lhsT=onec[:, 0:1],
                         rhs=cstb[:, mc * 512:mc * 512 + TT],
                         start=(mc == 0), stop=(mc == NCH - 1))
            for mc in range(NCH):
                inst = q.matmul(out=ps[5][0:1, 0:TT], lhsT=onec[:, 0:1],
                                rhs=cstb[:, 4096 + mc * 512:4096 + mc * 512 + TT],
                                start=(mc == 0), stop=(mc == NCH - 1))
            sln.inc(inst)
            q.wait_ge(srw2.h, b[srw2] + tc + 1)
            if tc >= 1:
                q.wait_ge(sdv.h, b[sdv] + 2 * 8 * tc)
            q.matmul(out=ps[2][:, 0:TT], lhsT=oner[0:1, 0:128],
                     rhs=rowsb["b1"][0:1, 0:TT], start=True, stop=False)
            q.matmul(out=ps[2][:, 0:TT], lhsT=oner[0:1, 0:128],
                     rhs=rowsb["b2"][0:1, 0:TT], start=False, stop=True)
            q.matmul(out=ps[3][:, 0:TT], lhsT=oner[0:1, 0:128],
                     rhs=rowsb["b3"][0:1, 0:TT], start=True, stop=False)
            slb.inc(q.matmul(out=ps[3][:, 0:TT], lhsT=oner[0:1, 0:128],
                             rhs=rowsb["b4"][0:1, 0:TT], start=False, stop=True))

    @blk.scalar
    def _(s):
        for tc in range(NT):
            for mc in range(NCH):
                st = tc * NCH + mc
                pb = st % 2
                s.wait_ge(smm.h, b[smm] + st + 1)
                if tc >= 1:
                    s.wait_ge(sdv.h, b[sdv] + 2 * (8 * (tc - 1) + mc + 1))
                sev.inc(s.activation(out=cstb[:, mc * 512:mc * 512 + TT],
                                     in_=ps[pb][:, 0:TT], func=AF.Identity,
                                     bias=B["constc"][:, 2:3], scale=1.0))
            s.wait_ge(slr.h, b[slr] + tc + 1)
            srs2.inc(s.activation(out=rows["r3"][0:1, 0:TT], in_=rows["r2"][0:1, 0:TT],
                                  func=AF.Sqrt, bias=B["constc"][0:1, 0:1], scale=1.0))

    @blk.vector
    def _(v):
        for tc in range(NT):
            for mc in range(NCH):
                st = tc * NCH + mc
                v.wait_ge(sev.h, b[sev] + st + 1)
                ssl.inc(v.tensor_mul(out=cstb[:, 4096 + mc * 512:4096 + mc * 512 + TT],
                                     in0=cstb[:, mc * 512:mc * 512 + TT],
                                     in1=cstb[:, mc * 512:mc * 512 + TT]))
            v.wait_ge(sln.h, b[sln] + tc + 1)
            P.ve(v, v.tensor_scalar(out=rows["r1"][0:1, 0:TT], in0=ps[4][0:1, 0:TT],
                                    scalar1=1.0 / 1024.0, scalar2=None, op0=OP.mult))
            P.ve(v, v.tensor_mul(out=rows["r5"][0:1, 0:TT], in0=rows["r1"][0:1, 0:TT],
                                 in1=rows["r1"][0:1, 0:TT]))
            P.ve(v, v.tensor_scalar(out=rows["r2"][0:1, 0:TT], in0=ps[5][0:1, 0:TT],
                                    scalar1=1.0 / 1024.0, scalar2=None, op0=OP.mult))
            P.ve(v, v.tensor_sub(out=rows["r2"][0:1, 0:TT], in0=rows["r2"][0:1, 0:TT],
                                 in1=rows["r5"][0:1, 0:TT]))
            slr.inc(v.engine_nop())
            if tc >= 1:
                v.wait_ge(slb.h, b[slb] + tc)
            P.ve(v, v.tensor_copy(out=rowsb["b1"][0:1, 0:TT], in_=rows["r1"][0:1, 0:TT]))
            P.ve(v, v.tensor_sub(out=rows["r5"][0:1, 0:TT], in0=rows["r1"][0:1, 0:TT],
                                 in1=rowsb["b1"][0:1, 0:TT]))
            P.ve(v, v.tensor_copy(out=rowsb["b2"][0:1, 0:TT], in_=rows["r5"][0:1, 0:TT]))
            v.wait_ge(srs2.h, b[srs2] + tc + 1)
            P.ve(v, v.reciprocal(out=rows["r4"][0:1, 0:TT], in_=rows["r3"][0:1, 0:TT]))
            P.ve(v, v.tensor_copy(out=rowsb["b3"][0:1, 0:TT], in_=rows["r4"][0:1, 0:TT]))
            P.ve(v, v.tensor_sub(out=rows["r5"][0:1, 0:TT], in0=rows["r4"][0:1, 0:TT],
                                 in1=rowsb["b3"][0:1, 0:TT]))
            srw2.inc(v.tensor_copy(out=rowsb["b4"][0:1, 0:TT], in_=rows["r5"][0:1, 0:TT]))
            v.wait_ge(slb.h, b[slb] + tc + 1)
            for mc in range(NCH):
                st = tc * NCH + mc
                pb = st % 2
                sgl.wait_done(v, bg, st)
                sxl.wait_done(v, bx2, st)
                if st >= 2:
                    supd.wait_done(v, bupd, st - 2)
                P.ve(v, v.tensor_sub(out=evb[:, 0:TT], in0=cstb[:, mc * 512:mc * 512 + TT],
                                     in1=ps[2][:, 0:TT]))
                P.ve(v, v.tensor_mul(out=evb[:, 0:TT], in0=evb[:, 0:TT],
                                     in1=ps[3][:, 0:TT]))
                P.ve(v, v.tensor_scalar(out=evb[:, 0:TT], in0=evb[:, 0:TT],
                                        scalar1=biasA[:, mc:mc + 1],
                                        scalar2=biasA[:, 8 + mc:9 + mc],
                                        op0=OP.mult, op1=OP.add))
                P.ve(v, v.tensor_mul(out=evb[:, 0:TT], in0=evb[:, 0:TT],
                                     in1=ar3[:, 4096 + pb * 512:4096 + pb * 512 + 512]))
                sdv.inc(v.tensor_add(out=ar3[:, pb * 512:pb * 512 + 512],
                                     in0=evb[:, 0:TT],
                                     in1=ar3[:, 5120 + pb * 512:5120 + pb * 512 + 512]), 2)

    @blk.gpsimd
    def _(g):
        for tc in range(NT):
            for mc in range(NCH):
                st = tc * NCH + mc
                pb = st % 2
                g.wait_ge(sdv.h, b[sdv] + 2 * (st + 1))
                supd.inc(g.dma_start(
                    out=xo[mc * 128:(mc + 1) * 128, tc * TT:(tc + 1) * TT],
                    in_=ar3[:, pb * 512:pb * 512 + 512]), st)



def emit_qkv_l3(P, l, li, B, W):
    blk = P.block
    xnb, otb, ps = B["xnb"], B["otb"], B["ps"]
    onec, oner = B["onec"], B["oner"]
    qpack, kpack, vpack = B["qpack"], B["kpack"], B["vpack"]
    sap = P.sem(f"ln_ap_{li}")
    sap_done = sap.n
    p2_sym = P.sem(f"f_ym_{li}")
    p2_done_mm = p2_sym.n
    p2_fw = P.pair(f"f_fw_{li}")
    p2_fw_tot = p2_fw.total()
    p2_hd = P.sem(f"f_hd_{li}")
    p2_done_hd = p2_hd.n

    swq = P.pair(f"a3_wq_{li}")
    svw = P.sem(f"a3_vw_{li}")
    smm = P.sem(f"a3_mm_{li}")
    sev = P.sem(f"a3_ev_{li}")
    sst = P.pair(f"a3_st_{li}")
    svm = P.sem(f"a3_vm_{li}")
    sve = P.sem(f"a3_ve_{li}")
    svs = P.pair(f"a3_vs_{li}")
    srow1 = P.sem(f"a3_row1_{li}")
    b = {s: s.n for s in [svw, smm, sev, svm, sve, srow1]}
    bwq, bst, bvs = swq.base(), sst.base(), svs.base()

    @blk.gpsimd
    def _(g):
        g.wait_ge(p2_sym.h, p2_done_mm)
        g.wait_ge(p2_hd.h, p2_done_hd)
        g.wait_ge(p2_fw.s[0].h, p2_fw_tot[0])
        g.wait_ge(p2_fw.s[1].h, p2_fw_tot[1])
        for oc in range(16):
            col0 = oc * 128
            if oc >= 2:
                g.wait_ge(smm.h, b[smm] + (oc - 1) * NT)
            swq.inc(g.dma_start(
                out=otb[:, 5664 + (oc % 2) * 1024:5664 + (oc % 2) * 1024 + 1024].rearrange(
                    "p (k m) -> p k m", k=8),
                in_=W[(l, "wqkv")][:, col0:col0 + 128].rearrange("(k p) m -> p k m", p=128)),
                oc)
        for vc in range(2):
            svw.inc(g.dma_start(
                out=otb[:, 8192 + vc * 4096:8192 + (vc + 1) * 4096].rearrange(
                    "p (k m) -> p k m", k=8),
                in_=W[(l, "wqkv")][:, 2048 + vc * 512:2048 + (vc + 1) * 512].rearrange(
                    "(k p) m -> p k m", p=128)), 16)

    @blk.vector
    def _(v):
        v.wait_ge(sap.h, sap_done)
        v.wait_ge(p2_sym.h, p2_done_mm)
        v.wait_ge(p2_hd.h, p2_done_hd)
        for i in range(2):
            srow1.inc(v.memset(otb[:, 3584 + i * 1040 + 64:3584 + i * 1040 + 1040:65], 1.0))

    @blk.tensor
    def _(q):
        for oc in range(16):
            swq.wait_done(q, bwq, oc)
            for tt in range(NT):
                step = oc * NT + tt
                pb = step % 2
                q.wait_ge(sap.h, sap_done - NT + tt + 1)
                if step >= 2:
                    q.wait_ge(sev.h, b[sev] + step - 1)
                wb0 = 5664 + (oc % 2) * 1024
                for c in range(NCH):
                    inst = q.matmul(out=ps[pb][:, 0:TT],
                                    lhsT=otb[:, wb0 + c * 128:wb0 + (c + 1) * 128],
                                    rhs=xnb[:, c * T + tt * TT:c * T + tt * TT + TT],
                                    start=(c == 0), stop=(c == NCH - 1))
                smm.inc(inst)

    @blk.scalar
    def _(s):
        for oc in range(16):
            for tt in range(NT):
                step = oc * NT + tt
                pb = step % 2
                s.wait_ge(smm.h, b[smm] + step + 1)
                if step >= 2:
                    sst.wait_done(s, bst, step - 2)
                eb0 = (step % 2) * 2048
                sev.inc(s.copy(out=otb[:, eb0:eb0 + TT], in_=ps[pb][:, 0:TT]))

    @blk.sync
    def _(sy):
        for oc in range(16):
            pack = qpack if oc < 8 else kpack
            prow = (oc % 8) * 128
            for tt in range(NT):
                step = oc * NT + tt
                sy.wait_ge(sev.h, b[sev] + step + 1)
                eb0 = (step % 2) * 2048
                sst.inc(sy.dma_start(out=pack[prow:prow + 128, tt * TT:tt * TT + TT],
                                     in_=otb[:, eb0:eb0 + TT]), step)

    @blk.tensor
    def _(q):
        q.wait_ge(svw.h, b[svw] + 32)
        q.wait_ge(srow1.h, b[srow1] + 2)
        q.wait_ge(sap.h, sap_done)
        for tk in range(32):
            for vc in range(2):
                step = tk * 2 + vc
                pb = step % 2
                if step >= 2:
                    q.wait_ge(sve.h, b[sve] + step - 1)
                for c in range(NCH):
                    inst = q.matmul(
                        out=ps[pb][:, 0:TT],
                        lhsT=xnb[:, c * T + tk * 128:c * T + tk * 128 + 128],
                        rhs=otb[:, 8192 + vc * 4096 + c * 512:8192 + vc * 4096 + (c + 1) * 512],
                        start=(c == 0), stop=(c == NCH - 1))
                svm.inc(inst)

    @blk.scalar
    def _(s):
        for tk in range(32):
            for vc in range(2):
                step = tk * 2 + vc
                pb = step % 2
                s.wait_ge(svm.h, b[svm] + step + 1)
                if tk >= 2 and vc == 0:
                    svs.wait_done(s, bvs, tk - 2)
                vr0 = 3584 + (tk % 2) * 1040
                sve.inc(s.copy(
                    out=otb[:, vr0 + vc * 520:vr0 + (vc + 1) * 520].rearrange(
                        "p (h d) -> p h d", h=8)[:, :, 0:64],
                    in_=ps[pb][:, 0:TT].rearrange("p (h d) -> p h d", h=8)))

    @blk.sync
    def _(sy):
        for tk in range(32):
            sy.wait_ge(sve.h, b[sve] + 2 * (tk + 1))
            svs.inc(sy.dma_start(out=vpack[:, tk * 1040:(tk + 1) * 1040],
                                 in_=otb[:, 3584 + (tk % 2) * 1040:3584 + (tk % 2) * 1040 + 1040]),
                    tk)


def emit_attn_l3(P, l, li, B, W):
    blk = P.block
    xnb, otb, ps, evb = B["xnb"], B["otb"], B["ps"], B["evb"]
    rows, rowsb = B["rows"], B["rowsb"]
    oner = B["oner"]
    qpack, kpack, vpack = B["qpack"], B["kpack"], B["vpack"]
    identb = B["identb"]
    gates = []
    for nm in ["mm", "vm", "ev", "ve"]:
        s = P.sem(f"a3_{nm}_{li}")
        gates.append((s.h, s.n))
    for nm in ["st", "vs"]:
        pr = P.pair(f"a3_{nm}_{li}")
        gates += [(pr.s[0].h, pr.s[0].n), (pr.s[1].h, pr.s[1].n)]

    sld = P.sem(f"b3_ld_{li}")
    szk = P.sem(f"b3_zk_{li}")
    snm3 = P.sem(f"b3_nm_{li}")
    sek = P.sem(f"b3_ek_{li}")
    skv = P.sem(f"b3_kv_{li}")
    skve = P.sem(f"b3_kve_{li}")
    szq = P.sem(f"b3_zq_{li}")
    snmq = P.sem(f"b3_nq_{li}")
    seq = P.sem(f"b3_eq_{li}")
    stp = P.sem(f"b3_tp_{li}")
    stpe = P.sem(f"b3_te_{li}")
    snd = P.sem(f"b3_nd_{li}")
    sdn3 = P.sem(f"b3_dn_{li}")
    srb3 = P.sem(f"b3_rb_{li}")
    snv3 = P.sem(f"b3_nv_{li}")
    sno = P.sem(f"b3_no_{li}")
    b = {s: s.n for s in [sld, szk, snm3, sek, skv, skve, szq, snmq, seq,
                          stp, stpe, snd, sdn3, srb3, snv3, sno]}

    KH = lambda h: (h % 2) * 8192
    QH = lambda h: (h % 2) * 8192 + 4096
    VH = lambda h: 16384 + (h % 2) * 2080
    PJ = lambda h: 20544 + (h % 2) * 256
    KP = lambda par: 21056 + par * 256
    QP = lambda par: 21568 + par * 256
    QT = lambda par: 22080 + par * 256
    KV = lambda h: 22592 + (h % 2) * 130

    @blk.gpsimd
    def _(g):
        for hh, nn in gates:
            g.wait_ge(hh, nn)
        for h in range(16):
            if h >= 2:
                g.wait_ge(szk.h, b[szk] + 32 * (h - 1))
                g.wait_ge(szq.h, b[szq] + 32 * (h - 1))
            sld.inc(g.dma_start(out=xnb[0:64, KH(h):KH(h) + T],
                                in_=kpack[64 * h:64 * h + 64, :]), 16)
            sld.inc(g.dma_start(out=xnb[0:64, QH(h):QH(h) + T],
                                in_=qpack[64 * h:64 * h + 64, :]), 16)
            sld.inc(g.dma_start(
                out=xnb[:, VH(h):VH(h) + 2080].rearrange("p (k w) -> p k w", k=32),
                in_=vpack[:, :].rearrange("p (k g w) -> p k g w", k=32, g=16)[:, :, h, :]),
                16)
            sld.inc(g.dma_start(out=xnb[0:64, PJ(h):PJ(h) + 256],
                                in_=W[(l, "proj_f")][h]), 16)

    @blk.tensor
    def _(q):
        for h in range(16):
            q.wait_ge(sld.h, b[sld] + 64 * (h + 1))

            def emit_kv(tk):
                cz = h * 32 + tk
                q.wait_ge(sek.h, b[sek] + cz + 1)
                first = (tk == 0)
                last = (tk == 31)
                q.matmul(out=ps[2][:, 0:65],
                         lhsT=xnb[:, KP(tk % 2):KP(tk % 2) + 128],
                         rhs=xnb[:, VH(h) + tk * 65:VH(h) + tk * 65 + 65],
                         start=first, stop=last)
                skv.inc(q.matmul(out=ps[3][:, 0:65],
                                 lhsT=xnb[:, KP(tk % 2) + 128:KP(tk % 2) + 256],
                                 rhs=xnb[:, VH(h) + tk * 65:VH(h) + tk * 65 + 65],
                                 start=first, stop=last), 2)

            for tk in range(32):
                cz = h * 32 + tk
                if cz >= 2:
                    q.wait_ge(sek.h, b[sek] + cz - 1)
                szk.inc(q.matmul(out=ps[cz % 2][:, 0:256],
                                 lhsT=xnb[0:64, KH(h) + tk * 128:KH(h) + tk * 128 + 128],
                                 rhs=xnb[0:64, PJ(h):PJ(h) + 256],
                                 start=True, stop=True))
                if tk >= 1:
                    emit_kv(tk - 1)
            emit_kv(31)
            q.wait_ge(skve.h, b[skve] + 2 * (h + 1))
            for tq in range(32):
                cq = h * 32 + tq
                q.wait_ge(seq.h, b[seq] + cq - 1) if cq >= 2 else None
                if tq == 0:
                    q.wait_ge(sek.h, b[sek] + 32 * (h + 1))
                szq.inc(q.matmul(out=ps[cq % 2][:, 0:256],
                                 lhsT=xnb[0:64, QH(h) + tq * 128:QH(h) + tq * 128 + 128],
                                 rhs=xnb[0:64, PJ(h):PJ(h) + 256],
                                 start=True, stop=True))
                q.wait_ge(seq.h, b[seq] + cq + 1)
                if cq >= 1:
                    q.wait_ge(srb3.h, b[srb3] + cq)
                q.matmul(out=ps[4][:, 0:128], lhsT=xnb[:, QP(tq % 2):QP(tq % 2) + 128],
                         rhs=identb[:, 0:128], start=True, stop=True)
                stp.inc(q.matmul(out=ps[5][:, 0:128],
                                 lhsT=xnb[:, QP(tq % 2) + 128:QP(tq % 2) + 256],
                                 rhs=identb[:, 0:128], start=True, stop=True), 2)
                q.wait_ge(stpe.h, b[stpe] + 2 * (cq + 1))
                if cq >= 1:
                    q.wait_ge(snv3.h, b[snv3] + cq)
                q.matmul(out=ps[6][0:65, 0:128], lhsT=xnb[:, KV(h):KV(h) + 65],
                         rhs=xnb[:, QT(cq % 2):QT(cq % 2) + 128], start=True, stop=False)
                snd.inc(q.matmul(out=ps[6][0:65, 0:128], lhsT=xnb[:, KV(h) + 65:KV(h) + 130],
                                 rhs=xnb[:, QT(cq % 2) + 128:QT(cq % 2) + 256],
                                 start=False, stop=True))
                q.wait_ge(sdn3.h, b[sdn3] + cq + 1)
                q.matmul(out=ps[4][0:64, 0:128], lhsT=oner[0:1, 0:64],
                         rhs=rowsb["b5"][0:1, 0:128], start=True, stop=False)
                srb3.inc(q.matmul(out=ps[4][0:64, 0:128], lhsT=oner[0:1, 0:64],
                                  rhs=rowsb["b6"][0:1, 0:128], start=False, stop=True))

    @blk.vector
    def _(v):
        for h in range(16):
            for tk in range(32):
                cz = h * 32 + tk
                v.wait_ge(szk.h, b[szk] + cz + 1)
                if cz >= 1:
                    v.wait_ge(sek.h, b[sek] + cz)
                P.ve(v, v.reduce_max(out=B["bcol"][3][:, 0:1], in_=ps[cz % 2][:, 0:256],
                                     axis=AX.X))
                snm3.inc(v.tensor_scalar(out=B["bcol"][3][:, 0:1], in0=B["bcol"][3][:, 0:1],
                                         scalar1=-1.0, scalar2=None, op0=OP.mult))
            for tq in range(32):
                cq = h * 32 + tq
                v.wait_ge(szq.h, b[szq] + cq + 1)
                v.wait_ge(seq.h, b[seq] + cq)
                if cq == 0:
                    v.wait_ge(sek.h, b[sek] + 32 * 16) if h == 15 else None
                P.ve(v, v.reduce_max(out=B["bcol"][3][:, 0:1], in_=ps[cq % 2][:, 0:256],
                                     axis=AX.X))
                snmq.inc(v.tensor_scalar(out=B["bcol"][3][:, 0:1], in0=B["bcol"][3][:, 0:1],
                                         scalar1=-1.0, scalar2=None, op0=OP.mult))
                v.wait_ge(snd.h, b[snd] + cq + 1)
                P.ve(v, v.tensor_copy(out=rows["r1"][0:1, 0:128], in_=ps[6][64:65, 0:128]))
                P.ve(v, v.tensor_scalar(out=rows["r1"][0:1, 0:128], in0=rows["r1"][0:1, 0:128],
                                        scalar1=1e-6, scalar2=None, op0=OP.add))
                P.ve(v, v.reciprocal(out=rows["r3"][0:1, 0:128], in_=rows["r1"][0:1, 0:128]))
                if cq >= 1:
                    v.wait_ge(srb3.h, b[srb3] + cq)
                P.ve(v, v.tensor_copy(out=rowsb["b5"][0:1, 0:128], in_=rows["r3"][0:1, 0:128]))
                P.ve(v, v.tensor_sub(out=rows["r5"][0:1, 0:128], in0=rows["r3"][0:1, 0:128],
                                     in1=rowsb["b5"][0:1, 0:128]))
                sdn3.inc(v.tensor_copy(out=rowsb["b6"][0:1, 0:128], in_=rows["r5"][0:1, 0:128]))
                v.wait_ge(snv3.h, b[snv3] + cq + 1)
                v.wait_ge(srb3.h, b[srb3] + cq + 1)
                r0 = 64 * (h % 2)
                t0 = (h // 2) * T + tq * 128
                sno.inc(v.tensor_mul(out=otb[r0:r0 + 64, t0:t0 + 128],
                                     in0=evb[0:64, 0:128], in1=ps[4][0:64, 0:128]))

    @blk.scalar
    def _(s_):
        for h in range(16):
            for tk in range(32):
                cz = h * 32 + tk
                s_.wait_ge(snm3.h, b[snm3] + cz + 1)
                if tk >= 2:
                    s_.wait_ge(skv.h, b[skv] + 2 * (cz - 1))
                sek.inc(s_.activation(out=xnb[:, KP(tk % 2):KP(tk % 2) + 256],
                                      in_=ps[cz % 2][:, 0:256], func=AF.Exp,
                                      bias=B["bcol"][3][:, 0:1], scale=1.0))
            s_.wait_ge(skv.h, b[skv] + 64 * (h + 1))
            if h >= 2:
                s_.wait_ge(snd.h, b[snd] + 32 * (h - 1))
            skve.inc(s_.copy(out=xnb[:, KV(h):KV(h) + 65], in_=ps[2][:, 0:65]))
            skve.inc(s_.copy(out=xnb[:, KV(h) + 65:KV(h) + 130], in_=ps[3][:, 0:65]))
            for tq in range(32):
                cq = h * 32 + tq
                s_.wait_ge(snmq.h, b[snmq] + cq + 1)
                if tq >= 2:
                    s_.wait_ge(stp.h, b[stp] + 2 * (cq - 1))
                seq.inc(s_.activation(out=xnb[:, QP(tq % 2):QP(tq % 2) + 256],
                                      in_=ps[cq % 2][:, 0:256], func=AF.Exp,
                                      bias=B["bcol"][3][:, 0:1], scale=1.0))
                s_.wait_ge(stp.h, b[stp] + 2 * (cq + 1))
                if cq >= 2:
                    s_.wait_ge(snd.h, b[snd] + cq - 1)
                s_.copy(out=xnb[:, QT(cq % 2):QT(cq % 2) + 128], in_=ps[4][:, 0:128])
                stpe.inc(s_.copy(out=xnb[:, QT(cq % 2) + 128:QT(cq % 2) + 256],
                                 in_=ps[5][:, 0:128]), 2)
                s_.wait_ge(snd.h, b[snd] + cq + 1)
                if cq >= 1:
                    s_.wait_ge(sno.h, b[sno] + cq)
                snv3.inc(s_.copy(out=evb[0:64, 0:128], in_=ps[6][0:64, 0:128]))


# version 03
# speedup vs baseline: 1.1725x; 1.1725x over previous
import math
import numpy as np
import ml_dtypes
import concourse.bass as bass
import concourse.mybir as mybir
from concourse.alu_op_type import AluOpType as OP
from concourse.bass_utils import run_bass_kernel_spmd
from contextlib import ExitStack

F32 = mybir.dt.float32
BF16 = mybir.dt.bfloat16
I32 = mybir.dt.int32
AF = mybir.ActivationFunctionType
AX = mybir.AxisListType
bfd = ml_dtypes.bfloat16

D = 1024
T = 4096
NCH = 8
NT = 8
TT = 512
H = 16
DH = 64
NEG = -1.0e30



def _host_consts():
    c = {}
    c["ident"] = np.eye(128).astype(bfd)
    half = DH // 2
    theta = 1.0 / (10000.0 ** (np.arange(half, dtype=np.float64) * 2.0 / DH))
    ang = np.arange(T, dtype=np.float64)[None, :] * theta[:, None]
    c["cos64"] = np.repeat(np.cos(ang), 2, axis=0).astype(bfd)
    c["sin64"] = np.repeat(np.sin(ang), 2, axis=0).astype(bfd)
    P = np.zeros((128, 128), dtype=np.float32)
    for blk in range(2):
        o = 64 * blk
        for i in range(half):
            P[o + 2 * i + 1, o + 2 * i] = -1.0
            P[o + 2 * i, o + 2 * i + 1] = 1.0
    c["rope_perm"] = P.astype(bfd)
    NP_ = 220
    groups = []
    masks = []
    g = 0
    q0 = 0
    while q0 < T:
        q1 = min(q0 + 2 * NP_, T)
        f_lo = q0 // NP_
        kvisf = min(NP_ * (f_lo + 1), T)
        kvise = min(NP_ * (f_lo + 2), T)
        if q1 - q0 <= NP_:
            kvise = kvisf
        mlist = []
        for kt in range(kvisf // 128, (kvise + 127) // 128):
            m = np.zeros((128, 512), dtype=np.float32)
            for r in range(128):
                key = 128 * kt + r
                if key >= kvisf:
                    qsplit = NP_ * (f_lo + 1) - q0
                    m[r, :qsplit] = NEG
                    if key >= kvise:
                        m[r, :] = NEG
            mlist.append((kt, len(masks)))
            masks.append(m)
        groups.append((q0, q1, kvisf, kvise, mlist))
        q0 = q1
        g += 1
    c["l0_groups"] = groups
    c["l0_masks"] = (np.stack(masks, axis=0).astype(bfd) if masks
                     else np.zeros((1, 128, 512), dtype=bfd))
    kk, qq = np.meshgrid(np.arange(128), np.arange(512), indexing="ij")
    l1m = np.stack([np.where(128 * kd + kk > qq, NEG, 0.0) for kd in range(4)], axis=0)
    c["l1_masks"] = l1m.astype(bfd)
    PI = [0, 0, 0, 1, 1, 2]
    PJ = [1, 2, 3, 2, 3, 3]
    selI = np.zeros((64, 96), dtype=np.float32)
    selJ = np.zeros((64, 96), dtype=np.float32)
    G96 = np.zeros((96, 16), dtype=np.float32)
    E96 = np.zeros((16, 96), dtype=np.float32)
    for h in range(16):
        for cc in range(6):
            selI[4 * h + PI[cc], 6 * h + cc] = 1.0
            selJ[4 * h + PJ[cc], 6 * h + cc] = 1.0
            G96[6 * h + cc, h] = 1.0
            E96[h, 6 * h + cc] = 1.0
    c["selI"] = selI.astype(bfd)
    c["selJ"] = selJ.astype(bfd)
    c["G96"] = G96.astype(bfd)
    c["E96"] = E96.astype(bfd)
    J6 = np.array([[0, 0, 0, 0, 0, 1], [0, 0, 0, 0, -1, 0], [0, 0, 0, 1, 0, 0],
                   [0, 0, 1, 0, 0, 0], [0, -1, 0, 0, 0, 0], [1, 0, 0, 0, 0, 0]],
                  dtype=np.float32)
    Jb = np.zeros((96, 96), dtype=np.float32)
    for h in range(16):
        Jb[6 * h:6 * h + 6, 6 * h:6 * h + 6] = J6
    c["J96"] = Jb.astype(bfd)
    c["sel15"] = np.kron(np.eye(15), np.ones((1, 128))).astype(bfd)
    c["sel8"] = np.kron(np.eye(8), np.ones((1, 128))).astype(bfd)
    return c


HOST_CONSTS = _host_consts()
_DBG = {}


def _fchunks(v):
    v = np.asarray(v, dtype=np.float32)
    return np.ascontiguousarray(v.reshape(-1, 128).T)




def layer_inputs(l, inp):
    w = {}
    f32c = lambda a: np.ascontiguousarray(np.asarray(a), dtype=np.float32)
    if l == 0:
        w["wqkv"] = f32c(inp["L0_attn_Wqkv"])
        w["wproj"] = f32c(inp["L0_attn_Wproj"])
        w["bproj"] = _fchunks(inp["L0_attn_bproj"])
        w["qn_w"] = f32c(np.asarray(inp["L0_attn_qn_w"]).reshape(1, 64))
        w["kn_w"] = f32c(np.asarray(inp["L0_attn_kn_w"]).reshape(1, 64))
        w["router_W"] = f32c(inp["L0_ffn_router_W"])
        w["router_b"] = f32c(np.asarray(inp["L0_ffn_router_b"]).reshape(1, 15))
        w["sW1"] = f32c(inp["L0_ffn_sW1"])
        w["sb1"] = _fchunks(inp["L0_ffn_sb1"])
        w["sW2"] = f32c(inp["L0_ffn_sW2"])
        w["sb2"] = _fchunks(inp["L0_ffn_sb2"])
        w["eW1"] = f32c(inp["L0_ffn_eW1"])
        w["eb1"] = np.concatenate([_fchunks(b) for b in np.asarray(inp["L0_ffn_eb1"])], axis=1)
        w["eW2"] = f32c(inp["L0_ffn_eW2"])
        w["eb2"] = f32c(inp["L0_ffn_eb2"])
        w["c_cos"] = HOST_CONSTS["cos64"]
        w["c_sin"] = HOST_CONSTS["sin64"]
        w["c_rope"] = HOST_CONSTS["rope_perm"]
        w["c_l0m"] = HOST_CONSTS["l0_masks"]
        w["c_sel15"] = HOST_CONSTS["sel15"]
        w["c_ident"] = HOST_CONSTS["ident"]
    elif l == 1:
        w["wqkv"] = f32c(inp["L1_attn_Wqkv"])
        w["bqkv"] = _fchunks(np.asarray(inp["L1_attn_bqkv"])[:2048])
        w["bqkv_v"] = f32c(np.asarray(inp["L1_attn_bqkv"])[2048:].reshape(1, 1024))
        w["w1p"] = f32c(inp["L1_attn_W1p"])
        w["w2p"] = f32c(inp["L1_attn_W2p"])
        w["pl_scale"] = f32c(np.asarray(inp["L1_attn_plucker_scale"]).reshape(16, 1))
        w["wproj"] = f32c(inp["L1_attn_Wout"])
        w["bproj"] = _fchunks(inp["L1_attn_bout"])
        for i in range(1, 5):
            w[f"cn_w{i}"] = f32c(inp[f"L1_ffn_cn_w{i}"])
            w[f"cn_b{i}"] = (_fchunks(inp[f"L1_ffn_cn_b{i}"]) if i < 4 else
                             f32c(np.asarray(inp["L1_ffn_cn_b4"]).reshape(1, 1)))
        for i in range(1, 4):
            w[f"gc_w{i}"] = f32c(inp[f"L1_ffn_gc_w{i}"])
            w[f"gc_b{i}"] = (_fchunks(inp[f"L1_ffn_gc_b{i}"]) if i < 3 else
                             f32c(np.asarray(inp["L1_ffn_gc_b3"]).reshape(1, 1)))
        w["adrange"] = f32c(np.asarray(inp["L1_ffn_adaptive_range"]).reshape(1, 1))
        w["c_selI"] = HOST_CONSTS["selI"]
        w["c_selJ"] = HOST_CONSTS["selJ"]
        w["c_G96"] = HOST_CONSTS["G96"]
        w["c_E96"] = HOST_CONSTS["E96"]
        w["c_J96"] = HOST_CONSTS["J96"]
        w["c_l1m"] = HOST_CONSTS["l1_masks"]
    elif l == 2:
        w["wq"] = f32c(inp["L2_attn_Wq"])
        w["wk"] = f32c(inp["L2_attn_Wk"])
        w["wv"] = f32c(inp["L2_attn_Wv"])
        w["wproj"] = f32c(inp["L2_attn_Wo"])
        w["lq1"] = f32c(np.asarray(inp["L2_attn_lambda_q1"]).reshape(1, 64))
        w["lk1"] = f32c(np.asarray(inp["L2_attn_lambda_k1"]).reshape(1, 64))
        w["lq2"] = f32c(np.asarray(inp["L2_attn_lambda_q2"]).reshape(1, 64))
        w["lk2"] = f32c(np.asarray(inp["L2_attn_lambda_k2"]).reshape(1, 64))
        w["gamma"] = _fchunks(inp["L2_attn_gamma"])
        w["beta"] = _fchunks(inp["L2_attn_beta"])
        w["wg_t"] = f32c(np.asarray(inp["L2_ffn_w_gate"]).T)
        w["wu_t"] = f32c(np.asarray(inp["L2_ffn_w_up"]).T)
        w["wd_t"] = f32c(np.asarray(inp["L2_ffn_w_down"]).T)
    elif l == 3:
        w["wqkv"] = f32c(inp["L3_attn_w_qkv"])
        w["wproj"] = f32c(inp["L3_attn_w_out"])
        w["bproj"] = _fchunks(inp["L3_attn_b_out"])
        w["proj_f"] = f32c(inp["L3_attn_proj"])
        w["router_W"] = f32c(inp["L3_ffn_Wr"])
        w["router_b"] = f32c(np.asarray(inp["L3_ffn_br"]).reshape(1, 8))
        w["eW1"] = f32c(inp["L3_ffn_W1"])
        w["eb1"] = np.concatenate([_fchunks(b) for b in np.asarray(inp["L3_ffn_b1"])], axis=1)
        w["eW2"] = f32c(inp["L3_ffn_W2"])
        w["eb2"] = f32c(inp["L3_ffn_b2"])
        w["c_ident"] = HOST_CONSTS["ident"]
        w["c_sel8"] = HOST_CONSTS["sel8"]
    return w


DT_OVERRIDE = {}




class Sem:
    def __init__(self, h):
        self.h = h
        self.n = 0

    def inc(self, inst, k=1):
        inst.then_inc(self.h, k)
        self.n += k
        return inst


class Prog:

    def __init__(self, nc, stack):
        self.nc = nc
        self.stack = stack
        self._sems = {}
        self._alias = {}
        self._slot_ctr = 0
        self.block = None

    def begin_layer(self):
        self._ctr = {"c": 0, "s": 0, "h": 0}

    def sem(self, name, kind="c"):
        if name in ("vchain", "init"):
            slot = name
        else:
            if name not in self._alias:
                self._alias[name] = f"{kind}l{self._ctr[kind]}"
                self._ctr[kind] += 1
            slot = self._alias[name]
        if slot not in self._sems:
            self._sems[slot] = Sem(self.stack.enter_context(self.nc.semaphore(slot)))
        return self._sems[slot]

    def ve(self, v, inst):
        s = self.sem("vchain")
        s.inc(inst)
        v.wait_ge(s.h, s.n)
        return inst

    def pair(self, name, kind="h"):
        if name + "__p" not in self._sems:
            self._sems[name + "__p"] = DPair(self.sem(name + "a", kind=kind),
                                             self.sem(name + "b", kind=kind))
        return self._sems[name + "__p"]


class DPair:

    def __init__(self, a, b):
        self.s = [a, b]

    def base(self):
        return (self.s[0].n, self.s[1].n)

    def inc(self, inst, k):
        self.s[k % 2].inc(inst, 16)
        return inst

    def wait_done(self, q, b, k):
        q.wait_ge(self.s[k % 2].h, b[k % 2] + 16 * (k // 2 + 1))

    def wait_all(self, q, n=None):
        q.wait_ge(self.s[0].h, self.s[0].n)
        q.wait_ge(self.s[1].h, self.s[1].n)

    def total(self):
        return (self.s[0].n, self.s[1].n)




def build_program(layers, host_shapes):
    nc = bass.Bass("TRN2", target_bir_lowering=False)
    stack = ExitStack()
    P = Prog(nc, stack)

    x_in = nc.dram_tensor("x", [D, T], F32, kind="ExternalInput").ap()
    x_out = nc.dram_tensor("xo", [D, T], F32, kind="ExternalOutput").ap()
    import os as _os
    if _os.environ.get("KDBG"):
        _DBG["xo2"] = nc.dram_tensor("xo2", [64, T], BF16, kind="ExternalOutput").ap()
    xmid = [nc.dram_tensor(f"xmid{i}", [D, T], F32, kind="Internal").ap()
            for i in range(2)]

    W = {}
    for l in layers:
        for name, shape in host_shapes[l].items():
            dt = DT_OVERRIDE.get(name, BF16 if name.startswith("c_") else F32)
            W[(l, name)] = nc.dram_tensor(f"L{l}_{name}", list(shape), dt,
                                          kind="ExternalInput").ap()

    B = {}
    B["qpack"] = nc.dram_tensor("qpack", [2048, T], BF16, kind="Internal").ap()
    B["kpack"] = nc.dram_tensor("kpack", [2048, T], BF16, kind="Internal").ap()
    B["vpack"] = nc.dram_tensor("vpack", [128, 32 * 1040], BF16, kind="Internal").ap()
    B["gbuf"] = nc.dram_tensor("gbuf", [D, T], F32, kind="Internal").ap()
    B["cwpack"] = nc.dram_tensor("cwpack", [16, T], BF16, kind="Internal").ap()
    B["sbb"] = nc.dram_tensor("sbb", [D, T], F32, kind="Internal").ap()
    B["hbuf"] = nc.dram_tensor("hbuf", [3 * D, T], BF16, kind="Internal").ap()
    B["jqpack"] = nc.dram_tensor("jqpack", [96, T], BF16, kind="Internal").ap()
    B["jkpack"] = nc.dram_tensor("jkpack", [96, T], BF16, kind="Internal").ap()

    sbt = lambda n, shp, dt=F32: stack.enter_context(nc.sbuf_tensor(n, shp, dt))
    B["xnb"] = sbt("xnb", [128, NCH * T], BF16)
    B["otb"] = sbt("otb", [128, NCH * T], BF16)
    B["ar3"] = sbt("ar3", [128, 6144], F32)
    B["rwb"] = sbt("rwb", [128, 128], BF16)
    B["cwtb"] = sbt("cwtb", [128, 16], BF16)
    B["identb"] = sbt("identb", [128, 128], BF16)
    B["sel15b"] = sbt("sel15b", [16, 1920], BF16)
    B["eb2bb"] = sbt("eb2bb", [16, 1024], BF16)
    B["cstb"] = sbt("cstb", [128, 8192], BF16)
    B["evb"] = sbt("evb", [128, 512], F32)
    B["evb2"] = sbt("evb2", [128, 512], F32)
    B["smallb"] = sbt("smallb", [128, 512], BF16)
    B["smallb2"] = sbt("smallb2", [128, 512], BF16)
    B["cwsb"] = sbt("cwsb", [16, T], BF16)
    B["onec"] = sbt("onec", [128, 1], BF16)
    B["oner"] = sbt("oner", [1, 128], BF16)
    B["rows"] = {n: sbt("row_" + n, [1, 512], F32) for n in
                 ["r1", "r2", "r3", "r4", "r5"]}
    B["rowsb"] = {}
    for n in ["b1", "b2", "b3", "b4"]:
        B["rowsb"][n] = sbt("rowb_" + n, [1, 512], BF16)
    for n in ["b5", "b6"]:
        B["rowsb"][n] = sbt("rowb_" + n, [1, 1024], BF16)
    B["bcol"] = [sbt(f"bcol{i}", [128, 1], F32) for i in range(4)]
    B["constc"] = sbt("constc", [128, 4], F32)
    ar3_ = B["ar3"]
    B["hr"] = {f"h{i+1}": ar3_[0:1, 1024 + 64 * i:1088 + 64 * i] for i in range(6)}
    B["rtr"] = ar3_[:, 2048:2176]
    B["r15"] = ar3_[0:1, 2112:2144]
    B["rbc"] = ar3_[:, 2144:2160]
    B["eb1c"] = ar3_[:, 2176:2304]
    B["biasA"] = ar3_[:, 2304:2336]
    otb_ = B["otb"]
    B["hrb"] = {f"b{i+1}": otb_[0:1, 7840 + 64 * i:7904 + 64 * i] for i in range(4)}
    B["r15b"] = otb_[0:1, 8096:8128]
    B["ps"] = [stack.enter_context(nc.psum_tensor(f"ps{i}", [128, 512], F32))
               for i in range(7)]
    B["psbf"] = stack.enter_context(nc.psum_tensor("psbf", [128, 512], BF16))

    P.block = stack.enter_context(nc.Block())
    blk = P.block
    with stack:
        s_init = P.sem("init")

        @blk.vector
        def _(v):
            v.memset(B["onec"][:], 1.0)
            v.memset(B["oner"][:], 1.0)
            v.memset(B["constc"][:, 0:1], 1e-5)
            v.memset(B["constc"][:, 1:2], 1e-6)
            s_init.inc(v.memset(B["constc"][:, 2:4], 0.0))

        n_layers = len(layers)
        for li, l in enumerate(layers):
            P.begin_layer()
            xi = x_in if li == 0 else xmid[li % 2]
            xo = x_out if li == n_layers - 1 else xmid[(li + 1) % 2]
            emit_layer(P, l, li, xi, xo, W, B)
        supd_f = P.pair(f"upd_done_{n_layers - 1}")
        supd_f_tot = supd_f.total()

        @blk.gpsimd
        def _(g):
            g.wait_ge(supd_f.s[0].h, supd_f_tot[0])
            g.wait_ge(supd_f.s[1].h, supd_f_tot[1])
    return nc


def emit_layer(P, l, li, xi, xo, W, B):
    emit_ln(P, l, li, xi, B)
    if l == 0:
        emit_ffn_l0(P, l, li, B, W)
        emit_qkv_l0(P, l, li, B, W)
        emit_attn_l0(P, l, li, B, W)
        emit_proj_update(P, l, li, xi, xo, B, W)
    elif l == 1:
        emit_ffn_l1(P, l, li, B, W)
        emit_qkv_l1(P, l, li, B, W)
        emit_attn_l1(P, l, li, B, W)
        emit_proj_update(P, l, li, xi, xo, B, W)
    elif l == 2:
        emit_qkv_l2(P, l, li, B, W)
        emit_ffn_l2(P, l, li, B, W)
        emit_attn_l2(P, l, li, B, W)
        emit_proj_update_l2(P, l, li, xi, xo, B, W)
    else:
        emit_ffn_l3(P, l, li, B, W)
        emit_qkv_l3(P, l, li, B, W)
        emit_attn_l3(P, l, li, B, W)
        emit_proj_update(P, l, li, xi, xo, B, W)



def emit_ln(P, l, li, xi, B):
    blk = P.block
    xnb, otb, ps = B["xnb"], B["otb"], B["ps"]
    rows, rowsb = B["rows"], B["rowsb"]
    onec, oner = B["onec"], B["oner"]
    smallb = B["smallb"]
    ssq, smm, srow, ssqr, shl, sbc, sap = (P.sem(f"ln_{n}_{li}") for n in
        ["sq", "mm", "row", "sqr", "hl", "bc", "ap"])
    sx = P.pair(f"ln_x_{li}", kind="s")
    b = {s: s.n for s in [ssq, smm, srow, ssqr, shl, sbc, sap]}
    bx = sx.base()
    supd = P.pair(f"upd_done_{li-1}") if li > 0 else None
    supd_tot = supd.total() if supd else None

    @blk.gpsimd
    def _(g):
        if supd is not None:
            g.wait_ge(supd.s[0].h, supd_tot[0])
            g.wait_ge(supd.s[1].h, supd_tot[1])
        for tt in range(NT):
            if tt >= 2:
                g.wait_ge(ssq.h, b[ssq] + tt - 1)
            sx.inc(g.dma_start(
                out=otb[:, tt * 4096:(tt + 1) * 4096].rearrange("p (c t) -> p c t", c=8),
                in_=xi[:, tt * TT:(tt + 1) * TT].rearrange("(c p) t -> p c t", p=128)), tt)

    @blk.vector
    def _(v):
        for tt in range(NT):
            sx.wait_done(v, bx, tt)
            for c in range(NCH):
                inst = v.tensor_mul(out=xnb[:, c * T + tt * TT:c * T + tt * TT + TT],
                                    in0=otb[:, tt * 4096 + c * 512:tt * 4096 + c * 512 + 512],
                                    in1=otb[:, tt * 4096 + c * 512:tt * 4096 + c * 512 + 512])
            ssq.inc(inst)

    @blk.tensor
    def _(q):
        q.wait_ge(P.sem("init").h, 1)
        for tt in range(NT):
            q.wait_ge(ssq.h, b[ssq] + tt + 1)
            if tt >= 1:
                q.wait_ge(srow.h, b[srow] + tt)
            for c in range(NCH):
                q.matmul(out=ps[0][0:1, 0:TT], lhsT=onec[:, 0:1],
                         rhs=otb[:, tt * 4096 + c * 512:tt * 4096 + c * 512 + 512],
                         start=(c == 0), stop=(c == NCH - 1))
            for c in range(NCH):
                inst = q.matmul(out=ps[1][0:1, 0:TT], lhsT=onec[:, 0:1],
                                rhs=xnb[:, c * T + tt * TT:c * T + tt * TT + TT],
                                start=(c == 0), stop=(c == NCH - 1))
            smm.inc(inst)
            q.wait_ge(shl.h, b[shl] + tt + 1)
            if tt >= 1:
                q.wait_ge(sap.h, b[sap] + tt)
            q.matmul(out=ps[2][:, 0:TT], lhsT=oner[0:1, :],
                     rhs=rowsb["b1"][0:1, 0:TT], start=True, stop=False)
            q.matmul(out=ps[2][:, 0:TT], lhsT=oner[0:1, :],
                     rhs=rowsb["b2"][0:1, 0:TT], start=False, stop=True)
            q.matmul(out=ps[3][:, 0:TT], lhsT=oner[0:1, :],
                     rhs=rowsb["b3"][0:1, 0:TT], start=True, stop=False)
            inst = q.matmul(out=ps[3][:, 0:TT], lhsT=oner[0:1, :],
                            rhs=rowsb["b4"][0:1, 0:TT], start=False, stop=True)
            sbc.inc(inst)

    @blk.scalar
    def _(s):
        for tt in range(NT):
            s.wait_ge(srow.h, b[srow] + tt + 1)
            ssqr.inc(s.activation(out=rows["r2"][0:1, 0:TT], in_=rows["r1"][0:1, 0:TT],
                                  func=AF.Sqrt, bias=B["constc"][0:1, 0:1], scale=1.0))

    @blk.vector
    def _(v):
        for tt in range(NT):
            v.wait_ge(smm.h, b[smm] + tt + 1)
            P.ve(v, v.tensor_scalar(out=rows["r3"][0:1, 0:TT], in0=ps[0][0:1, 0:TT],
                            scalar1=1.0 / D, scalar2=None, op0=OP.mult))
            v.tensor_scalar(out=rows["r1"][0:1, 0:TT], in0=ps[1][0:1, 0:TT],
                            scalar1=1.0 / D, scalar2=None, op0=OP.mult)
            P.ve(v, v.tensor_mul(out=rows["r4"][0:1, 0:TT], in0=rows["r3"][0:1, 0:TT],
                         in1=rows["r3"][0:1, 0:TT]))
            srow.inc(v.tensor_sub(out=rows["r1"][0:1, 0:TT], in0=rows["r1"][0:1, 0:TT],
                                  in1=rows["r4"][0:1, 0:TT]))
            v.wait_ge(ssqr.h, b[ssqr] + tt + 1)
            P.ve(v, v.reciprocal(out=rows["r4"][0:1, 0:TT], in_=rows["r2"][0:1, 0:TT]))
            P.ve(v, v.tensor_mul(out=rows["r5"][0:1, 0:TT], in0=rows["r3"][0:1, 0:TT],
                         in1=rows["r4"][0:1, 0:TT]))
            P.ve(v, v.tensor_copy(out=rowsb["b1"][0:1, 0:TT], in_=rows["r4"][0:1, 0:TT]))
            P.ve(v, v.tensor_sub(out=rows["r3"][0:1, 0:TT], in0=rows["r4"][0:1, 0:TT],
                         in1=rowsb["b1"][0:1, 0:TT]))
            P.ve(v, v.tensor_copy(out=rowsb["b2"][0:1, 0:TT], in_=rows["r3"][0:1, 0:TT]))
            P.ve(v, v.tensor_copy(out=rowsb["b3"][0:1, 0:TT], in_=rows["r5"][0:1, 0:TT]))
            P.ve(v, v.tensor_sub(out=rows["r3"][0:1, 0:TT], in0=rows["r5"][0:1, 0:TT],
                         in1=rowsb["b3"][0:1, 0:TT]))
            shl.inc(v.tensor_copy(out=rowsb["b4"][0:1, 0:TT], in_=rows["r3"][0:1, 0:TT]))
            v.wait_ge(sbc.h, b[sbc] + tt + 1)
            t0 = tt * TT
            for c in range(NCH):
                P.ve(v, v.tensor_mul(out=smallb[:, 0:TT],
                             in0=otb[:, tt * 4096 + c * 512:tt * 4096 + c * 512 + 512],
                             in1=ps[2][:, 0:TT]))
                inst = v.tensor_sub(out=xnb[:, c * T + t0:c * T + t0 + TT],
                                    in0=smallb[:, 0:TT], in1=ps[3][:, 0:TT])
            sap.inc(inst)



def emit_ffn_l0(P, l, li, B, W):
    _emit_ffn_moe(P, l, li, B, W, E=15, shared=True)


def _emit_ffn_moe(P, l, li, B, W, E, shared, HID_CH=8):
    blk = P.block
    xnb, otb, ps = B["xnb"], B["otb"], B["ps"]
    evb, evb2, smallb = B["evb"], B["evb2"], B["smallb"]
    cwsb, oner, psbf = B["cwsb"], B["oner"], B["psbf"]
    rtr = B["rtr"]
    gbuf = B["gbuf"]
    heb = B["cstb"]
    hsb_off = 16384
    sap = P.sem(f"ln_ap_{li}")
    sap_done = sap.n
    selb = B["sel15b"]
    selname = "c_sel15" if E == 15 else "c_sel8"

    swr = P.sem(f"f_wr_{li}", kind="s")
    sbb = P.sem(f"f_bb_{li}")
    srbc = P.sem(f"f_rbc_{li}")
    srl = P.sem(f"f_rl_{li}")
    srx = P.sem(f"f_rx_{li}")
    srexp = P.sem(f"f_rexp_{li}")
    srd = P.sem(f"f_rd_{li}")
    srt = P.sem(f"f_rt_{li}")
    sre = P.sem(f"f_re_{li}")
    sws = P.sem(f"f_ws_{li}", kind="s")
    shm = P.sem(f"f_hm_{li}")
    she = P.sem(f"f_he_{li}")
    shd = P.sem(f"f_hd_{li}")
    sym = P.sem(f"f_ym_{li}")
    sye = P.sem(f"f_ye_{li}")
    sgw = P.pair(f"f_gw_{li}")
    sad = P.pair(f"f_ad_{li}", kind="s")
    swe1 = P.sem(f"f_we1_{li}", kind="s")
    swe2 = P.sem(f"f_we2_{li}", kind="s")
    scb = P.sem(f"f_cb_{li}")
    sfb = P.sem(f"f_fb_{li}")
    sfl = P.pair(f"f_fl_{li}")
    sfd = P.sem(f"f_fd_{li}")
    sfe = P.sem(f"f_fe_{li}")
    sfw = P.pair(f"f_fw_{li}", kind="s")
    sems = [swr, sbb, srbc, srl, srx, srexp, srd, srt, sre, sws, shm, she, shd,
            sym, sye, swe1, swe2, scb, sfb, sfd, sfe]
    b = {s: s.n for s in sems}
    bgw, bad, bfl, bfw = sgw.base(), sad.base(), sfl.base(), sfw.base()
    NTC = 32
    NSH = NCH * NT if shared else 0
    HSTEPS = HID_CH
    wqcols = 1024 * (HID_CH // 8)

    @blk.gpsimd
    def _(g):
        g.wait_ge(sap.h, sap_done)
        swr.inc(g.dma_start(out=B["eb1c"][:, 0:E * HID_CH], in_=W[(l, "eb1")]), 16)
        swr.inc(g.dma_start(out=B["identb"][:], in_=W[(l, "c_ident")]), 16)
        swr.inc(g.dma_start(out=B["eb2bb"][0:E, 0:1024],
                            in_=W[(l, "eb2")]), 16)
        swr.inc(g.dma_start(out=B["rwb"][:, 0:8 * E].rearrange("p (k e) -> p k e", k=8),
                            in_=W[(l, "router_W")].rearrange("(k p) e -> p k e", p=128)), 16)
        if shared:
            swr.inc(g.dma_start(out=B["biasA"][:, 0:8], in_=W[(l, "sb1")]), 16)
            swr.inc(g.dma_start(out=B["biasA"][:, 8:16], in_=W[(l, "sb2")]), 16)
        swr.inc(g.dma_start(out=B["sel15b"][0:E, 0:E * 128], in_=W[(l, selname)]), 16)
        swr.inc(g.dma_start(out=B["r15"][0:1, 0:E], in_=W[(l, "router_b")]), 16)
        NPRE = 8 if shared else 6
        if shared:
            sws.inc(g.dma_start(out=otb[:, 0:8192].rearrange("p (k m) -> p k m", k=8),
                                in_=W[(l, "sW1")].rearrange("(k p) m -> p k m", p=128)), 16)
            sws.inc(g.dma_start(out=otb[:, 8192:16384].rearrange("p (k m) -> p k m", k=8),
                                in_=W[(l, "sW2")].rearrange("(k p) m -> p k m", p=128)), 16)
        NYS = NCH * NT

        def emit_accum(ea, step):
            gstep = ea * NYS + step
            tt, oc = divmod(step, NCH)
            g.wait_ge(sye.h, b[sye] + NSH + gstep + 1)
            if ea == 0 and step == 0 and shared:
                g.wait_ge(sgw.s[0].h, bgw[0] + 16 * (NSH // 2))
                g.wait_ge(sgw.s[1].h, bgw[1] + 16 * (NSH // 2))
            kw = {} if (not shared and ea == 0) else {"accum_op": OP.add}
            sad.inc(g.dma_start(
                out=gbuf[oc * 128:(oc + 1) * 128, tt * TT:(tt + 1) * TT],
                in_=evb[:, 0:TT] if (NSH + gstep) % 2 == 0 else evb2[:, 0:TT],
                **kw), gstep)

        def emit_w1(e):
            eW1 = W[(l, "eW1")][e]
            if e >= 1:
                g.wait_ge(shm.h, b[shm] + NSH + e * NT * HSTEPS)
            elif shared:
                g.wait_ge(sym.h, b[sym] + NSH)
            if HID_CH == 8:
                swe1.inc(g.dma_start(
                    out=otb[:, 16384:24576].rearrange("p (k m) -> p k m", k=8),
                    in_=eW1.rearrange("(k p) m -> p k m", p=128)), 16)
            else:
                swe1.inc(g.dma_start(
                    out=otb[:, 0:16384].rearrange("p (k m) -> p k m", k=8),
                    in_=eW1.rearrange("(k p) m -> p k m", p=128)), 16)

        def emit_w2(e):
            eW2 = W[(l, "eW2")][e]
            g.wait_ge(sym.h, b[sym] + NSH + e * NYS)
            if HID_CH == 8:
                swe2.inc(g.dma_start(
                    out=otb[:, 24576:32768].rearrange("p (k m) -> p k m", k=8),
                    in_=eW2.rearrange("(k p) m -> p k m", p=128)), 16)
            else:
                swe2.inc(g.dma_start(
                    out=otb[:, 16384:32768].rearrange("p (k m) -> p k m", k=16),
                    in_=eW2.rearrange("(k p) m -> p k m", p=128)), 16)

        emit_w1(0)
        emit_w2(0)
        for e in range(E):
            for step in range(NYS - 10):
                emit_accum(e, step)
            if e + 1 < E:
                emit_w1(e + 1)
            for step in range(NYS - 10, NYS - 2):
                emit_accum(e, step)
            if e + 1 < E:
                emit_w2(e + 1)
            for step in range(NYS - 2, NYS):
                emit_accum(e, step)

    NPRE_W = 8 if shared else 6

    @blk.vector
    def _(v):
        v.wait_ge(swr.h, b[swr] + 16 * NPRE_W)
        P.ve(v, v.tensor_copy(out=B["r15b"][0:1, 0:E], in_=B["r15"][0:1, 0:E]))
        P.ve(v, v.tensor_sub(out=B["r15"][0:1, 16:16 + E], in0=B["r15"][0:1, 0:E],
                     in1=B["r15b"][0:1, 0:E]))
        srd.inc(v.tensor_copy(out=B["r15b"][0:1, 16:16 + E], in_=B["r15"][0:1, 16:16 + E]))

    @blk.tensor
    def _(q):
        q.wait_ge(srd.h, b[srd] + 1)
        q.matmul(out=ps[5][:, 0:E], lhsT=oner[0:1, :], rhs=B["r15b"][0:1, 0:E],
                 start=True, stop=False)
        sbb.inc(q.matmul(out=ps[5][:, 0:E], lhsT=oner[0:1, :], rhs=B["r15b"][0:1, 16:16 + E],
                         start=False, stop=True))

    @blk.vector
    def _(v):
        v.wait_ge(sbb.h, b[sbb] + 1)
        srbc.inc(v.tensor_copy(out=B["rbc"][:, 0:E], in_=ps[5][:, 0:E]))

    @blk.tensor
    def _(q):
        q.wait_ge(srbc.h, b[srbc] + 1)
        for tc in range(NTC):
            q.wait_ge(sap.h, sap_done - NT + tc // 4 + 1)
            if tc >= 1:
                q.wait_ge(srx.h, b[srx] + tc)
            for c in range(NCH):
                inst = q.matmul(out=ps[4][:, 0:E],
                                lhsT=xnb[:, c * T + tc * 128:c * T + tc * 128 + 128],
                                rhs=B["rwb"][:, c * E:(c + 1) * E],
                                start=(c == 0), stop=(c == NCH - 1))
            srl.inc(inst)
            q.wait_ge(srd.h, b[srd] + 1 + tc + 1)
            if tc >= 1:
                q.wait_ge(sre.h, b[sre] + tc)
            srt.inc(q.matmul(out=ps[5][0:E, 0:128], lhsT=B["cwtb"][:, 0:E],
                             rhs=B["identb"][:, :], start=True, stop=True))

    @blk.vector
    def _(v):
        K = 3 if E == 15 else 2
        for tc in range(NTC):
            v.wait_ge(srl.h, b[srl] + tc + 1)
            if tc >= 1:
                v.wait_ge(srexp.h, b[srexp] + tc)
            P.ve(v, v.tensor_add(out=rtr[:, 0:E], in0=ps[4][:, 0:E], in1=B["rbc"][:, 0:E]))
            srx.inc(v.reduce_max(out=rtr[:, 16:17], in_=rtr[:, 0:E], axis=AX.X,
                                 negate=True))
            v.wait_ge(srexp.h, b[srexp] + tc + 1)
            P.ve(v, v.reduce_sum(out=rtr[:, 17:18], in_=rtr[:, 22:22 + E], axis=AX.X))
            P.ve(v, v.reciprocal(out=rtr[:, 18:19], in_=rtr[:, 17:18]))
            P.ve(v, v.tensor_scalar(out=rtr[:, 38:38 + E], in0=rtr[:, 22:22 + E],
                            scalar1=rtr[:, 18:19], scalar2=None, op0=OP.mult))
            P.ve(v, v.reduce_max(out=rtr[:, 19:20], in_=rtr[:, 38:38 + E], axis=AX.X))
            P.ve(v, v.scalar_tensor_tensor(out=rtr[:, 22:22 + E], in0=rtr[:, 38:38 + E],
                                   scalar=rtr[:, 19:20], in1=rtr[:, 38:38 + E],
                                   op0=OP.is_lt, op1=OP.mult))
            P.ve(v, v.reduce_max(out=rtr[:, 20:21], in_=rtr[:, 22:22 + E], axis=AX.X))
            if K == 3:
                P.ve(v, v.scalar_tensor_tensor(out=rtr[:, 54:54 + E], in0=rtr[:, 22:22 + E],
                                       scalar=rtr[:, 20:21], in1=rtr[:, 22:22 + E],
                                       op0=OP.is_lt, op1=OP.mult))
                P.ve(v, v.reduce_max(out=rtr[:, 21:22], in_=rtr[:, 54:54 + E], axis=AX.X))
            else:
                P.ve(v, v.tensor_copy(out=rtr[:, 21:22], in_=rtr[:, 20:21]))
            P.ve(v, v.scalar_tensor_tensor(out=rtr[:, 70:70 + E], in0=rtr[:, 38:38 + E],
                                   scalar=rtr[:, 21:22], in1=rtr[:, 38:38 + E],
                                   op0=OP.is_ge, op1=OP.mult))
            srd.inc(v.tensor_copy(out=B["cwtb"][:, 0:E], in_=rtr[:, 70:70 + E]))
            v.wait_ge(srt.h, b[srt] + tc + 1)
            sre.inc(v.tensor_copy(out=cwsb[0:E, tc * 128:(tc + 1) * 128],
                                  in_=ps[5][0:E, 0:128]))

    @blk.scalar
    def _(s):
        for tc in range(NTC):
            s.wait_ge(srx.h, b[srx] + tc + 1)
            srexp.inc(s.activation(out=rtr[:, 22:22 + E], in_=rtr[:, 0:E], func=AF.Exp,
                                   bias=rtr[:, 16:17], scale=1.0))


    @blk.tensor
    def _(q):
        if shared:
            q.wait_ge(sws.h, b[sws] + 32)
            for tt in range(NT):
                t0 = tt * TT
                for hc in range(NCH):
                    step = tt * NCH + hc
                    pb = step % 2
                    if step >= 2:
                        q.wait_ge(she.h, b[she] + step - 1)
                    for c in range(NCH):
                        inst = q.matmul(
                            out=ps[pb][:, 0:TT],
                            lhsT=otb[:, c * 1024 + hc * 128:c * 1024 + hc * 128 + 128],
                            rhs=xnb[:, c * T + t0:c * T + t0 + TT],
                            start=(c == 0), stop=(c == NCH - 1))
                    shm.inc(inst)
                for oc in range(NCH):
                    step = tt * NCH + oc
                    pb = 2 + step % 2
                    q.wait_ge(she.h, b[she] + tt * NCH + NCH)
                    if step >= 2:
                        q.wait_ge(sye.h, b[sye] + step - 1)
                    for hc in range(NCH):
                        inst = q.matmul(
                            out=ps[pb][:, 0:TT],
                            lhsT=otb[:, 8192 + hc * 1024 + oc * 128:8192 + hc * 1024 + oc * 128 + 128],
                            rhs=otb[:, hsb_off + hc * TT:hsb_off + (hc + 1) * TT],
                            start=(hc == 0), stop=(hc == NCH - 1))
                    sym.inc(inst)
        NYS = NCH * NT
        w1_0 = 16384 if HID_CH == 8 else 0
        w2_0 = 24576 if HID_CH == 8 else 16384
        for e in range(E):
            q.wait_ge(swe1.h, b[swe1] + 16 * (e + 1))
            for tt in range(NT):
                t0 = tt * TT
                et = e * NT + tt
                q.wait_ge(sre.h, b[sre] + min(NTC, 4 * (tt + 1)))
                if et >= 1:
                    q.wait_ge(shd.h, b[shd] + (et - 1) * HSTEPS + HSTEPS)
                scb.inc(q.matmul(out=ps[6][:, 0:TT], lhsT=selb[0:E, e * 128:(e + 1) * 128],
                                 rhs=cwsb[0:E, t0:t0 + TT], start=True, stop=True))
                for hc in range(HID_CH):
                    step = et * HSTEPS + hc
                    pb = step % 2
                    if NSH + step >= 2:
                        q.wait_ge(she.h, b[she] + NSH + step - 1)
                    for c in range(NCH):
                        inst = q.matmul(
                            out=ps[pb][:, 0:TT],
                            lhsT=otb[:, w1_0 + c * (HID_CH * 128) + hc * 128:
                                     w1_0 + c * (HID_CH * 128) + hc * 128 + 128],
                            rhs=xnb[:, c * T + t0:c * T + t0 + TT],
                            start=(c == 0), stop=(c == NCH - 1))
                    shm.inc(inst)
                if tt == 0:
                    q.wait_ge(swe2.h, b[swe2] + 16 * (e + 1))
                for oc in range(NCH):
                    step = et * NCH + oc
                    pb = 2 + step % 2
                    q.wait_ge(shd.h, b[shd] + et * HSTEPS + HSTEPS)
                    if NSH + step >= 2:
                        q.wait_ge(sye.h, b[sye] + NSH + step - 1)
                    for hc in range(HID_CH):
                        inst = q.matmul(
                            out=ps[pb][:, 0:TT],
                            lhsT=otb[:, w2_0 + hc * 1024 + oc * 128:
                                     w2_0 + hc * 1024 + oc * 128 + 128],
                            rhs=heb[:, hc * TT:(hc + 1) * TT] if HID_CH == 8 else
                                heb[:, hc * TT:(hc + 1) * TT],
                            start=(hc == 0), stop=(hc == HID_CH - 1))
                    sym.inc(inst)

    @blk.scalar
    def _(s):
        if shared:
            for tt in range(NT):
                for hc in range(NCH):
                    step = tt * NCH + hc
                    pb = step % 2
                    s.wait_ge(shm.h, b[shm] + step + 1)
                    she.inc(s.activation(out=otb[:, hsb_off + hc * TT:hsb_off + (hc + 1) * TT],
                                         in_=ps[pb][:, 0:TT], func=AF.Relu,
                                         bias=B["biasA"][:, hc:hc + 1], scale=1.0))
                for oc in range(NCH):
                    step = tt * NCH + oc
                    pb = 2 + step % 2
                    s.wait_ge(sym.h, b[sym] + step + 1)
                    if step >= 2:
                        sgw.wait_done(s, bgw, step - 2)
                    sye.inc(s.copy(out=evb[:, 0:TT] if step % 2 == 0 else evb2[:, 0:TT],
                                   in_=ps[pb][:, 0:TT]))
        for e in range(E):
            for tt in range(NT):
                et = e * NT + tt
                for hc in range(HID_CH):
                    step = et * HSTEPS + hc
                    pb = step % 2
                    s.wait_ge(shm.h, b[shm] + NSH + step + 1)
                    if step >= 1:
                        s.wait_ge(shd.h, b[shd] + step)
                    she.inc(s.activation(out=smallb[:, 0:TT], in_=ps[pb][:, 0:TT],
                                         func=AF.Relu,
                                         bias=B["eb1c"][:, e * HID_CH + hc:e * HID_CH + hc + 1],
                                         scale=1.0))
                for oc in range(NCH):
                    step = et * NCH + oc
                    pb = 2 + step % 2
                    s.wait_ge(sym.h, b[sym] + NSH + step + 1)
                    if step >= 2:
                        sad.wait_done(s, bad, step - 2)
                    elif shared:
                        s.wait_ge(sgw.s[0].h, bgw[0] + 16 * (NSH // 2))
                        s.wait_ge(sgw.s[1].h, bgw[1] + 16 * (NSH // 2))
                    sye.inc(s.copy(out=evb[:, 0:TT] if (NSH + step) % 2 == 0 else evb2[:, 0:TT],
                                   in_=ps[pb][:, 0:TT]))

    @blk.vector
    def _(v):
        for e in range(E):
            for tt in range(NT):
                et = e * NT + tt
                v.wait_ge(scb.h, b[scb] + et + 1)
                if et >= 1:
                    v.wait_ge(sym.h, b[sym] + NSH + et * NCH)
                for hc in range(HID_CH):
                    step = et * HSTEPS + hc
                    v.wait_ge(she.h, b[she] + NSH + step + 1)
                    shd.inc(v.tensor_mul(out=heb[:, hc * TT:(hc + 1) * TT],
                                         in0=smallb[:, 0:TT], in1=ps[6][:, 0:TT]))

    @blk.sync
    def _(sy):
        if shared:
            for tt in range(NT):
                for oc in range(NCH):
                    step = tt * NCH + oc
                    sy.wait_ge(sye.h, b[sye] + step + 1)
                    sgw.inc(sy.dma_start(
                        out=gbuf[oc * 128:(oc + 1) * 128, tt * TT:(tt + 1) * TT],
                        in_=evb[:, 0:TT] if step % 2 == 0 else evb2[:, 0:TT]), step)

    NYS = NCH * NT
    n_acc = E * NYS
    all_accum = (bad[0] + 16 * ((n_acc + 1) // 2), bad[1] + 16 * (n_acc // 2))

    @blk.tensor
    def _(q):
        for tt in range(NT):
            for oc in range(NCH):
                step = tt * NCH + oc
                pb = step % 2
                if step >= 2:
                    q.wait_ge(sfd.h, b[sfd] + step - 1)
                sfb.inc(q.matmul(out=ps[4 + pb][:, 0:TT],
                                 lhsT=B["eb2bb"][0:E, oc * 128:(oc + 1) * 128],
                                 rhs=cwsb[0:E, tt * TT:(tt + 1) * TT],
                                 start=True, stop=True))

    @blk.sync
    def _(sy):
        sy.wait_ge(sad.s[0].h, all_accum[0])
        sy.wait_ge(sad.s[1].h, all_accum[1])
        sy.wait_ge(scb.h, b[scb] + E * NT)
        for tt in range(NT):
            for oc in range(NCH):
                step = tt * NCH + oc
                pb = step % 2
                if step >= 2:
                    sy.wait_ge(sfe.h, b[sfe] + step - 1)
                sfl.inc(sy.dma_start(
                    out=B["ar3"][:, 0:512] if pb == 0 else B["ar3"][:, 512:1024],
                    in_=gbuf[oc * 128:(oc + 1) * 128, tt * TT:(tt + 1) * TT]), step)

    @blk.vector
    def _(v):
        for tt in range(NT):
            for oc in range(NCH):
                step = tt * NCH + oc
                pb = step % 2
                sfl.wait_done(v, bfl, step)
                v.wait_ge(sfb.h, b[sfb] + step + 1)
                sfd.inc(v.tensor_add(
                    out=B["ar3"][:, 0:512] if pb == 0 else B["ar3"][:, 512:1024],
                    in0=B["ar3"][:, 0:512] if pb == 0 else B["ar3"][:, 512:1024],
                    in1=ps[4 + pb][:, 0:TT]))

    @blk.scalar
    def _(s):
        for tt in range(NT):
            for oc in range(NCH):
                step = tt * NCH + oc
                pb = step % 2
                s.wait_ge(sfd.h, b[sfd] + step + 1)
                if step >= 2:
                    sfw.wait_done(s, bfw, step - 2)
                kwb = {"bias": B["biasA"][:, 8 + oc:8 + oc + 1]} if shared else {}
                sfe.inc(s.activation(
                    out=evb[:, 0:TT] if pb == 0 else evb2[:, 0:TT],
                    in_=B["ar3"][:, 0:512] if pb == 0 else B["ar3"][:, 512:1024],
                    func=AF.Silu, scale=1.0, **kwb))

    @blk.gpsimd
    def _(g):
        for tt in range(NT):
            for oc in range(NCH):
                step = tt * NCH + oc
                g.wait_ge(sfe.h, b[sfe] + step + 1)
                sfw.inc(g.dma_start(out=gbuf[oc * 128:(oc + 1) * 128, tt * TT:(tt + 1) * TT],
                                    in_=evb[:, 0:TT] if step % 2 == 0 else evb2[:, 0:TT]), step)




def emit_qkv_l0(P, l, li, B, W):
    blk = P.block
    xnb, otb, ps, cstb = B["xnb"], B["otb"], B["ps"], B["cstb"]
    rows, rowsb, hr, hrb = B["rows"], B["rowsb"], B["hr"], B["hrb"]
    onec, oner, smallb2 = B["onec"], B["oner"], B["smallb2"]
    qpack, kpack, vpack = B["qpack"], B["kpack"], B["vpack"]
    sap = P.sem(f"ln_ap_{li}")
    sap_done = sap.n
    p2_sym = P.sem(f"f_ym_{li}")
    p2_done_mm = p2_sym.n
    p2_fw = P.pair(f"f_fw_{li}")
    p2_fw_tot = p2_fw.total()
    p2_hd = P.sem(f"f_hd_{li}")
    p2_done_hd = p2_hd.n

    sw = P.sem(f"a0_w_{li}", kind="h")
    swq = P.pair(f"a0_wq_{li}", kind="s")
    svw = P.sem(f"a0_vw_{li}", kind="s")
    smm = P.sem(f"a0_mm_{li}")
    sev = P.sem(f"a0_ev_{li}")
    ssq = P.sem(f"a0_sq_{li}")
    srs = P.sem(f"a0_rs_{li}")
    srw = P.sem(f"a0_rw_{li}")
    src_ = P.sem(f"a0_rc_{li}")
    ssc = P.sem(f"a0_sc_{li}")
    st1 = P.sem(f"a0_t1_{li}")
    ssw = P.sem(f"a0_sw_{li}")
    sro = P.sem(f"a0_ro_{li}")
    sst = P.pair(f"a0_st_{li}")
    svm = P.sem(f"a0_vm_{li}")
    sve = P.sem(f"a0_ve_{li}")
    svs = P.pair(f"a0_vs_{li}")
    srow1 = P.sem(f"a0_row1_{li}")
    b = {s: s.n for s in [sw, svw, smm, sev, ssq, srs, srw, src_, ssc, st1, ssw,
                          sro, svm, sve, srow1]}
    bwq, bst, bvs = swq.base(), sst.base(), svs.base()

    @blk.sync
    def _(sy):
        sy.wait_ge(p2_sym.h, p2_done_mm)
        sy.wait_ge(p2_hd.h, p2_done_hd)
        sy.wait_ge(p2_fw.s[0].h, p2_fw_tot[0])
        sy.wait_ge(p2_fw.s[1].h, p2_fw_tot[1])
        sw.inc(sy.dma_start(out=cstb[0:64, 0:4096], in_=W[(l, "c_cos")]), 16)
        sw.inc(sy.dma_start(out=cstb[64:128, 0:4096], in_=W[(l, "c_cos")]), 16)
        sw.inc(sy.dma_start(out=cstb[0:64, 4096:8192], in_=W[(l, "c_sin")]), 16)
        sw.inc(sy.dma_start(out=cstb[64:128, 4096:8192], in_=W[(l, "c_sin")]), 16)
        sw.inc(sy.dma_start(out=smallb2[:, 0:128], in_=W[(l, "c_rope")]), 16)
        sw.inc(sy.dma_start(out=hr["h1"][0:1, 0:64], in_=W[(l, "qn_w")]), 16)
        sw.inc(sy.dma_start(out=hr["h2"][0:1, 0:64], in_=W[(l, "kn_w")]), 16)

    @blk.vector
    def _(v):
        v.wait_ge(sw.h, b[sw] + 16 * 7)
        P.ve(v, v.tensor_scalar(out=hr["h3"][0:1, 0:64], in0=hr["h1"][0:1, 0:64],
                        scalar1=0.125, scalar2=None, op0=OP.mult))
        P.ve(v, v.tensor_copy(out=hrb["b1"][0:1, 0:64], in_=hr["h3"][0:1, 0:64]))
        P.ve(v, v.tensor_copy(out=hrb["b2"][0:1, 0:64], in_=hr["h2"][0:1, 0:64]))
        P.ve(v, v.reduce_max(out=hr["h4"][0:1, 0:1], in_=hr["h1"][0:1, 0:64], axis=AX.X,
                     apply_absolute_value=True))
        P.ve(v, v.reduce_max(out=hr["h4"][0:1, 1:2], in_=hr["h2"][0:1, 0:64], axis=AX.X,
                     apply_absolute_value=True))
        P.ve(v, v.tensor_mul(out=hr["h4"][0:1, 2:3], in0=hr["h4"][0:1, 0:1],
                     in1=hr["h4"][0:1, 1:2]))
        P.ve(v, v.tensor_scalar(out=hr["h4"][0:1, 3:4], in0=hr["h4"][0:1, 2:3],
                        scalar1=-8.0, scalar2=None, op0=OP.mult))
        srow1.inc(v.tensor_copy(out=hrb["b3"][0:1, 0:1], in_=hr["h4"][0:1, 3:4]))
        v.wait_ge(p2_sym.h, p2_done_mm)
        v.wait_ge(p2_hd.h, p2_done_hd)
        for i in range(2):
            srow1.inc(v.memset(otb[:, 3584 + i * 1040 + 64:3584 + i * 1040 + 1040:65], 1.0))

    @blk.tensor
    def _(q):
        q.wait_ge(srow1.h, b[srow1] + 1)
        ssc.inc(q.matmul(out=ps[6][:, 0:1], lhsT=oner[0:1, :], rhs=hrb["b3"][0:1, 0:1],
                         start=True, stop=True))

    @blk.vector
    def _(v):
        v.wait_ge(ssc.h, b[ssc] + 1)
        srow1.inc(v.tensor_copy(out=B["bcol"][0][:, 0:1], in_=ps[6][:, 0:1]))

    @blk.gpsimd
    def _(g):
        g.wait_ge(p2_sym.h, p2_done_mm)
        g.wait_ge(p2_hd.h, p2_done_hd)
        g.wait_ge(p2_fw.s[0].h, p2_fw_tot[0])
        g.wait_ge(p2_fw.s[1].h, p2_fw_tot[1])
        for oc in range(16):
            col0 = oc * 128
            if oc >= 2:
                g.wait_ge(smm.h, b[smm] + (oc - 1) * NT)
            swq.inc(g.dma_start(
                out=otb[:, 5664 + (oc % 2) * 1024:5664 + (oc % 2) * 1024 + 1024].rearrange(
                    "p (k m) -> p k m", k=8),
                in_=W[(l, "wqkv")][:, col0:col0 + 128].rearrange("(k p) m -> p k m", p=128)),
                oc)
        for vc in range(2):
            svw.inc(g.dma_start(
                out=otb[:, 8192 + vc * 4096:8192 + (vc + 1) * 4096].rearrange(
                    "p (k m) -> p k m", k=8),
                in_=W[(l, "wqkv")][:, 2048 + vc * 512:2048 + (vc + 1) * 512].rearrange(
                    "(k p) m -> p k m", p=128)), 16)

    @blk.tensor
    def _(q):
        for oc in range(16):
            swq.wait_done(q, bwq, oc)
            for tt in range(NT):
                step = oc * NT + tt
                pb = step % 2
                q.wait_ge(sap.h, sap_done - NT + tt + 1)
                if step >= 2:
                    q.wait_ge(sev.h, b[sev] + step - 1)
                for c in range(NCH):
                    wb0 = 5664 + (oc % 2) * 1024
                    inst = q.matmul(out=ps[pb][:, 0:TT],
                                    lhsT=otb[:, wb0 + c * 128:wb0 + (c + 1) * 128],
                                    rhs=xnb[:, c * T + tt * TT:c * T + tt * TT + TT],
                                    start=(c == 0), stop=(c == NCH - 1))
                smm.inc(inst)
                q.wait_ge(ssq.h, b[ssq] + step + 1)
                if step >= 1:
                    q.wait_ge(srw.h, b[srw] + 2 * step)
                q.matmul(out=ps[2][0:1, 0:TT], lhsT=onec[0:64, 0:1],
                         rhs=otb[0:64, 512:512 + TT], start=True, stop=True,
                         tile_position=(0, 0))
                srs.inc(q.matmul(out=ps[3][0:1, 0:TT], lhsT=onec[64:128, 0:1],
                                 rhs=otb[64:128, 512:512 + TT], start=True, stop=True,
                                 tile_position=(64, 0)))
                q.wait_ge(src_.h, b[src_] + step + 1)
                qn = hrb["b1"] if oc < 8 else hrb["b2"]
                q.matmul(out=ps[4][0:64, 0:TT], lhsT=qn[0:1, 0:64],
                         rhs=rowsb["b5"][0:1, 0:TT], start=True, stop=True)
                ssc.inc(q.matmul(out=ps[4][64:128, 0:TT], lhsT=qn[0:1, 0:64],
                                 rhs=rowsb["b6"][0:1, 0:TT], start=True, stop=True))
                q.wait_ge(st1.h, b[st1] + step + 1)
                ssw.inc(q.matmul(out=ps[5][:, 0:TT], lhsT=smallb2[:, 0:128],
                                 rhs=otb[:, 0:TT], start=True, stop=True))

    @blk.scalar
    def _(s):
        for oc in range(16):
            for tt in range(NT):
                step = oc * NT + tt
                pb = step % 2
                s.wait_ge(smm.h, b[smm] + step + 1)
                if step >= 1:
                    s.wait_ge(sro.h, b[sro] + step)
                sev.inc(s.copy(out=otb[:, 0:TT], in_=ps[pb][:, 0:TT]))
                s.wait_ge(srs.h, b[srs] + step + 1)
                if step >= 1:
                    s.wait_ge(src_.h, b[src_] + step)
                s.activation(out=rows["r1"][0:1, 0:TT], in_=ps[2][0:1, 0:TT],
                             func=AF.Sqrt, bias=B["constc"][0:1, 1:2], scale=1.0 / 64.0)
                srw.inc(s.activation(out=rows["r2"][0:1, 0:TT], in_=ps[3][0:1, 0:TT],
                                     func=AF.Sqrt, bias=B["constc"][0:1, 1:2], scale=1.0 / 64.0), 2)

    @blk.vector
    def _(v):
        for oc in range(16):
            for tt in range(NT):
                step = oc * NT + tt
                v.wait_ge(sev.h, b[sev] + step + 1)
                ssq.inc(v.tensor_mul(out=otb[:, 512:512 + TT],
                                     in0=otb[:, 0:TT], in1=otb[:, 0:TT]))
                v.wait_ge(srw.h, b[srw] + 2 * (step + 1))
                P.ve(v, v.reciprocal(out=rows["r3"][0:1, 0:TT], in_=rows["r1"][0:1, 0:TT]))
                P.ve(v, v.reciprocal(out=rows["r4"][0:1, 0:TT], in_=rows["r2"][0:1, 0:TT]))
                P.ve(v, v.tensor_copy(out=rowsb["b5"][0:1, 0:TT], in_=rows["r3"][0:1, 0:TT]))
                src_.inc(v.tensor_copy(out=rowsb["b6"][0:1, 0:TT],
                                       in_=rows["r4"][0:1, 0:TT]))
                v.wait_ge(ssc.h, b[ssc] + 1 + step + 1)
                P.ve(v, v.tensor_mul(out=otb[:, 0:TT], in0=otb[:, 0:TT],
                                     in1=ps[4][:, 0:TT]))
                st1.inc(v.engine_nop())
                v.wait_ge(ssw.h, b[ssw] + step + 1)
                t0 = tt * TT
                P.ve(v, v.tensor_mul(out=otb[:, 1536:1536 + TT], in0=otb[:, 0:TT],
                             in1=cstb[:, t0:t0 + TT]))
                P.ve(v, v.tensor_mul(out=otb[:, 2048:2048 + TT], in0=ps[5][:, 0:TT],
                             in1=cstb[:, 4096 + t0:4096 + t0 + TT]))
                if step >= 2:
                    sst.wait_done(v, bst, step - 2)
                ob = 2560 + (step % 2) * 512
                sro.inc(v.tensor_add(out=otb[:, ob:ob + TT],
                                     in0=otb[:, 1536:1536 + TT],
                                     in1=otb[:, 2048:2048 + TT]))

    @blk.sync
    def _(sy):
        for oc in range(16):
            pack = qpack if oc < 8 else kpack
            prow = (oc % 8) * 128
            for tt in range(NT):
                step = oc * NT + tt
                sy.wait_ge(sro.h, b[sro] + step + 1)
                ob = 2560 + (step % 2) * 512
                sst.inc(sy.dma_start(out=pack[prow:prow + 128, tt * TT:tt * TT + TT],
                                     in_=otb[:, ob:ob + TT]), step)

    @blk.tensor
    def _(q):
        q.wait_ge(svw.h, b[svw] + 32)
        q.wait_ge(srow1.h, b[srow1] + 4)
        q.wait_ge(sap.h, sap_done)
        for tk in range(32):
            for vc in range(2):
                step = tk * 2 + vc
                pb = step % 2
                if step >= 2:
                    q.wait_ge(sve.h, b[sve] + step - 1)
                for c in range(NCH):
                    inst = q.matmul(
                        out=ps[pb][:, 0:TT],
                        lhsT=xnb[:, c * T + tk * 128:c * T + tk * 128 + 128],
                        rhs=otb[:, 8192 + vc * 4096 + c * 512:8192 + vc * 4096 + (c + 1) * 512],
                        start=(c == 0), stop=(c == NCH - 1))
                svm.inc(inst)

    @blk.scalar
    def _(s):
        for tk in range(32):
            for vc in range(2):
                step = tk * 2 + vc
                pb = step % 2
                s.wait_ge(svm.h, b[svm] + step + 1)
                if tk >= 2 and vc == 0:
                    svs.wait_done(s, bvs, tk - 2)
                vr0 = 3584 + (tk % 2) * 1040
                sve.inc(s.copy(
                    out=otb[:, vr0 + vc * 520:vr0 + (vc + 1) * 520].rearrange(
                        "p (h d) -> p h d", h=8)[:, :, 0:64],
                    in_=ps[pb][:, 0:TT].rearrange("p (h d) -> p h d", h=8)))

    @blk.sync
    def _(sy):
        for tk in range(32):
            sy.wait_ge(sve.h, b[sve] + 2 * (tk + 1))
            svs.inc(sy.dma_start(out=vpack[:, tk * 1040:(tk + 1) * 1040],
                                 in_=otb[:, 3584 + (tk % 2) * 1040:3584 + (tk % 2) * 1040 + 1040]), tk)



def attn_core(P, li, B, jobs, kload, qload, eb_bias, nheads_branch, pv_lhs,
              norm_out, vg_of_job, mask_ap, tag):
    raise NotImplementedError


def emit_attn_l0(P, l, li, B, W):
    blk = P.block
    xnb, otb, ps = B["xnb"], B["otb"], B["ps"]
    evb = B["evb"]
    rows, rowsb = B["rows"], B["rowsb"]
    oner = B["oner"]
    qpack, kpack, vpack = B["qpack"], B["kpack"], B["vpack"]
    groups = HOST_CONSTS["l0_groups"]
    sst = P.pair(f"a0_st_{li}")
    sst_tot = sst.total()
    svs = P.pair(f"a0_vs_{li}")
    svs_tot = svs.total()
    srow1 = P.sem(f"a0_row1_{li}")
    srow1_done = srow1.n

    svg = P.sem(f"b0_vg_{li}", kind="s")
    skh = P.sem(f"b0_kh_{li}", kind="s")
    sqh = P.sem(f"b0_qh_{li}", kind="s")
    smk = P.pair(f"b0_mk_{li}", kind="s")
    ssc = P.sem(f"b0_sc_{li}")
    sms = P.sem(f"b0_ms_{li}")
    sex = P.sem(f"b0_ex_{li}")
    spv = P.sem(f"b0_pv_{li}")
    sdc = P.sem(f"b0_dc_{li}")
    sdn = P.sem(f"b0_dn_{li}")
    srb = P.sem(f"b0_rb_{li}")
    snv = P.sem(f"b0_nv_{li}")
    sno = P.sem(f"b0_no_{li}")
    b = {s: s.n for s in [svg, skh, sqh, ssc, sms, sex, spv, sdc, sdn, srb, snv, sno]}
    bmk = smk.base()

    VG0, KH0, QH0, E0, MK0 = 0, 16640, 20736, 24832, 25856

    jobs = []
    for hp in range(8):
        for (q0, q1, kvisf, kvise, mlist) in groups:
            mm = dict(mlist)
            kts = [(kt, min(128, kvise - 128 * kt), mm.get(kt))
                   for kt in range((kvise + 127) // 128)]
            jobs.append((hp, q0, q1 - q0, kts))
    gstep = []
    for ji, (hp, q0, N, kts) in enumerate(jobs):
        for ki, (kt, krows, midx) in enumerate(kts):
            gstep.append((ji, ki, midx is not None))
    nsteps_of_job = [len(kts) for (_, _, _, kts) in jobs]
    step_base = [0]
    for n_ in nsteps_of_job:
        step_base.append(step_base[-1] + n_)
    nmask_upto = []
    m = 0
    for (_, _, masked) in gstep:
        if masked:
            m += 1
        nmask_upto.append(m)
    masks_of_hp = []
    for hp in range(8):
        lst = []
        for (q0, q1, kvisf, kvise, mlist) in groups:
            for (kt, midx) in mlist:
                lst.append(midx)
        masks_of_hp.append(lst)
    per_hp_masks = len(masks_of_hp[0])
    steps_upto_hp = [0] * 9
    for ji, (hp, q0, N, kts) in enumerate(jobs):
        steps_upto_hp[hp + 1] = step_base[ji] + len(kts)

    @blk.gpsimd
    def _(g):
        g.wait_ge(sst.s[0].h, sst_tot[0])
        g.wait_ge(sst.s[1].h, sst_tot[1])
        g.wait_ge(svs.s[0].h, svs_tot[0])
        g.wait_ge(svs.s[1].h, svs_tot[1])
        g.wait_ge(srow1.h, srow1_done)
        nm = 0
        for hp in range(8):
            vg = hp // 4
            if hp % 4 == 0:
                if hp > 0:
                    g.wait_ge(spv.h, b[spv] + 2 * steps_upto_hp[hp])
                svg.inc(g.dma_start(
                    out=xnb[:, VG0:VG0 + 16640].rearrange("p (k w) -> p k w", k=32),
                    in_=vpack[:, :].rearrange("p (k g w) -> p k g w", k=32, g=2)[:, :, vg, :]),
                    16)
            if hp >= 1:
                g.wait_ge(ssc.h, b[ssc] + 2 * steps_upto_hp[hp])
            skh.inc(g.dma_start(out=xnb[:, KH0:KH0 + T],
                                in_=kpack[hp * 128:(hp + 1) * 128, :]), 16)
            sqh.inc(g.dma_start(out=xnb[:, QH0:QH0 + T],
                                in_=qpack[hp * 128:(hp + 1) * 128, :]), 16)
            for i, midx in enumerate(masks_of_hp[hp]):
                if nm >= 2:
                    g.wait_ge(sms.h, b[sms] + nm - 1)
                smk.inc(g.dma_start(out=xnb[:, MK0 + (nm % 2) * 512:MK0 + (nm % 2) * 512 + 512],
                                    in_=W[(l, "c_l0m")][midx]), nm)
                nm += 1

    @blk.tensor
    def _(q):
        for ji, (hp, q0, N, kts) in enumerate(jobs):
            q.wait_ge(svg.h, b[svg] + 16 * (hp // 4 + 1))
            q.wait_ge(skh.h, b[skh] + 16 * (hp + 1))
            q.wait_ge(sqh.h, b[sqh] + 16 * (hp + 1))
            hloc = (2 * hp) % 8

            def emit_pv(ki):
                kt_, krows_, _m = kts[ki]
                s_ = step_base[ji] + ki
                q.wait_ge(sex.h, b[sex] + 2 * (s_ + 1))
                if ki == 0:
                    q.wait_ge(snv.h, b[snv] + ji)
                first = (ki == 0)
                last = (ki == len(kts) - 1)
                q.matmul(out=ps[2][0:65, 0:N],
                         lhsT=xnb[0:krows_, VG0 + kt_ * 520 + 65 * hloc:
                                  VG0 + kt_ * 520 + 65 * hloc + 65],
                         rhs=xnb[0:krows_, E0:E0 + N], start=first, stop=last)
                spv.inc(q.matmul(out=ps[5][0:65, 0:N],
                                 lhsT=xnb[0:krows_, VG0 + kt_ * 520 + 65 * (hloc + 1):
                                          VG0 + kt_ * 520 + 65 * (hloc + 1) + 65],
                                 rhs=xnb[0:krows_, E0 + 512:E0 + 512 + N],
                                 start=first, stop=last), 2)

            for ki, (kt, krows, midx) in enumerate(kts):
                s = step_base[ji] + ki
                bA = ps[0 + 3 * (s % 2)]
                bB = ps[1 + 3 * (s % 2)]
                if s >= 2:
                    q.wait_ge(sex.h, b[sex] + 2 * (s - 1))
                q.matmul(out=bA[0:krows, 0:N],
                         lhsT=xnb[0:64, KH0 + kt * 128:KH0 + kt * 128 + krows],
                         rhs=xnb[0:64, QH0 + q0:QH0 + q0 + N], start=True, stop=True,
                         tile_position=(0, 0))
                ssc.inc(q.matmul(out=bB[0:krows, 0:N],
                                 lhsT=xnb[64:128, KH0 + kt * 128:KH0 + kt * 128 + krows],
                                 rhs=xnb[64:128, QH0 + q0:QH0 + q0 + N], start=True,
                                 stop=True, tile_position=(64, 0)), 2)
                if ki >= 1:
                    emit_pv(ki - 1)
            emit_pv(len(kts) - 1)
            q.wait_ge(sdn.h, b[sdn] + ji + 1)
            q.matmul(out=ps[6][0:64, 0:N], lhsT=oner[0:1, 0:64],
                     rhs=rowsb["b5"][0:1, 0:N], start=True, stop=False)
            q.matmul(out=ps[6][0:64, 0:N], lhsT=oner[0:1, 0:64],
                     rhs=rowsb["b6"][0:1, 0:N], start=False, stop=True)
            q.matmul(out=ps[6][64:128, 0:N], lhsT=oner[0:1, 0:64],
                     rhs=rowsb["b5"][0:1, 512:512 + N], start=True, stop=False)
            srb.inc(q.matmul(out=ps[6][64:128, 0:N], lhsT=oner[0:1, 0:64],
                             rhs=rowsb["b6"][0:1, 512:512 + N], start=False, stop=True))

    @blk.vector
    def _(v):
        nm_global = 0
        for ji, (hp, q0, N, kts) in enumerate(jobs):
            for ki, (kt, krows, midx) in enumerate(kts):
                s = step_base[ji] + ki
                if midx is not None:
                    nmh = nm_global % per_hp_masks
                    slot = (hp * per_hp_masks + nmh) % 2
                    v.wait_ge(ssc.h, b[ssc] + 2 * (s + 1))
                    smk.wait_done(v, bmk, hp * per_hp_masks + nmh)
                    bA = ps[0 + 3 * (s % 2)]
                    bB = ps[1 + 3 * (s % 2)]
                    v.tensor_add(out=bA[0:krows, 0:N], in0=bA[0:krows, 0:N],
                                 in1=xnb[0:krows, MK0 + slot * 512:MK0 + slot * 512 + N])
                    sms.inc(v.tensor_add(out=bB[0:krows, 0:N], in0=bB[0:krows, 0:N],
                                         in1=xnb[0:krows, MK0 + slot * 512:MK0 + slot * 512 + N]))
                    nm_global += 1
            v.wait_ge(spv.h, b[spv] + 2 * step_base[ji + 1])
            P.ve(v, v.tensor_copy(out=rows["r1"][0:1, 0:N], in_=ps[2][64:65, 0:N]))
            sdc.inc(v.tensor_copy(out=rows["r2"][0:1, 0:N], in_=ps[5][64:65, 0:N]))
            P.ve(v, v.reciprocal(out=rows["r3"][0:1, 0:N], in_=rows["r1"][0:1, 0:N]))
            P.ve(v, v.reciprocal(out=rows["r4"][0:1, 0:N], in_=rows["r2"][0:1, 0:N]))
            if ji >= 1:
                v.wait_ge(srb.h, b[srb] + ji)
            P.ve(v, v.tensor_copy(out=rowsb["b5"][0:1, 0:N], in_=rows["r3"][0:1, 0:N]))
            P.ve(v, v.tensor_sub(out=rows["r5"][0:1, 0:N], in0=rows["r3"][0:1, 0:N],
                         in1=rowsb["b5"][0:1, 0:N]))
            P.ve(v, v.tensor_copy(out=rowsb["b6"][0:1, 0:N], in_=rows["r5"][0:1, 0:N]))
            P.ve(v, v.tensor_copy(out=rowsb["b5"][0:1, 512:512 + N], in_=rows["r4"][0:1, 0:N]))
            P.ve(v, v.tensor_sub(out=rows["r5"][0:1, 0:N], in0=rows["r4"][0:1, 0:N],
                         in1=rowsb["b5"][0:1, 512:512 + N]))
            sdn.inc(v.tensor_copy(out=rowsb["b6"][0:1, 512:512 + N],
                                  in_=rows["r5"][0:1, 0:N]))
            v.wait_ge(snv.h, b[snv] + ji + 1)
            v.wait_ge(srb.h, b[srb] + ji + 1)
            sno.inc(v.tensor_mul(out=otb[:, hp * T + q0:hp * T + q0 + N],
                                 in0=evb[:, 0:N], in1=ps[6][:, 0:N]))

    @blk.scalar
    def _(s_):
        for ji, (hp, q0, N, kts) in enumerate(jobs):
            for ki, (kt, krows, midx) in enumerate(kts):
                s = step_base[ji] + ki
                if midx is not None:
                    s_.wait_ge(sms.h, b[sms] + nmask_upto[s])
                else:
                    s_.wait_ge(ssc.h, b[ssc] + 2 * (s + 1))
                if s >= 1:
                    s_.wait_ge(spv.h, b[spv] + 2 * s)
                bA = ps[0 + 3 * (s % 2)]
                bB = ps[1 + 3 * (s % 2)]
                s_.activation(out=xnb[0:krows, E0:E0 + N], in_=bA[0:krows, 0:N],
                              func=AF.Exp, bias=B["bcol"][0][0:krows, 0:1], scale=1.0)
                sex.inc(s_.activation(out=xnb[0:krows, E0 + 512:E0 + 512 + N],
                                      in_=bB[0:krows, 0:N], func=AF.Exp,
                                      bias=B["bcol"][0][0:krows, 0:1], scale=1.0), 2)
            s_.wait_ge(sdc.h, b[sdc] + ji + 1)
            if ji >= 1:
                s_.wait_ge(sno.h, b[sno] + ji)
            s_.copy(out=evb[0:64, 0:N], in_=ps[2][0:64, 0:N])
            snv.inc(s_.copy(out=evb[64:128, 0:N], in_=ps[5][0:64, 0:N]))


def emit_proj_update(P, l, li, xi, xo, B, W):
    blk = P.block
    xnb, otb, ps, ar3 = B["xnb"], B["otb"], B["ps"], B["ar3"]
    evb, evb2 = B["evb"], B["evb2"]
    rows, rowsb = B["rows"], B["rowsb"]
    oner = B["oner"]
    sno = P.sem(f"b{l}_no_{li}")
    sno_done = sno.n
    sg = P.pair(f"f_fw_{li}")
    sg_tot = sg.total()

    sw = P.sem(f"p4_w_{li}", kind="s")
    sgl = P.pair(f"p4_g_{li}")
    sxl = P.pair(f"p4_x_{li}")
    smm = P.sem(f"p4_mm_{li}")
    sev = P.sem(f"p4_ev_{li}")
    sdv = P.sem(f"p4_dv_{li}")
    supd = P.pair(f"upd_done_{li}", kind="s")
    sln = P.sem(f"p4_ln_{li}")
    slr = P.sem(f"p4_lr_{li}")
    slb = P.sem(f"p4_lb_{li}")
    b = {s: s.n for s in [sw, smm, sev, sdv, sln, slr, slb]}
    bg, bx2, bupd = sgl.base(), sxl.base(), supd.base()

    @blk.gpsimd
    def _(g):
        g.wait_ge(sno.h, sno_done)
        sw.inc(g.dma_start(
            out=xnb[:, 0:8192].rearrange("p (k m) -> p k m", k=8),
            in_=W[(l, "wproj")].rearrange("(k p) m -> p k m", p=128)), 16)
        sw.inc(g.dma_start(out=B["biasA"][:, 16:24], in_=W[(l, "bproj")]), 16)

    @blk.sync
    def _(sy):
        sy.wait_ge(sg.s[0].h, sg_tot[0])
        sy.wait_ge(sg.s[1].h, sg_tot[1])
        for tt in range(NT):
            for oc in range(NCH):
                step = tt * NCH + oc
                pb = step % 2
                if step >= 2:
                    sy.wait_ge(sdv.h, b[sdv] + 2 * (step - 1))
                sgl.inc(sy.dma_start(out=ar3[:, 4096 + pb * 512:4096 + pb * 512 + 512],
                             in_=B["gbuf"][oc * 128:(oc + 1) * 128, tt * TT:(tt + 1) * TT]), step)
                sxl.inc(sy.dma_start(out=ar3[:, 5120 + pb * 512:5120 + pb * 512 + 512],
                                     in_=xi[oc * 128:(oc + 1) * 128, tt * TT:(tt + 1) * TT]),
                        step)

    @blk.tensor
    def _(q):
        q.wait_ge(sw.h, b[sw] + 32)
        for tt in range(NT):
            for oc in range(NCH):
                step = tt * NCH + oc
                pb = step % 2
                if step >= 2:
                    q.wait_ge(sev.h, b[sev] + step - 1)
                for fc in range(NCH):
                    inst = q.matmul(
                        out=ps[pb][:, 0:TT],
                        lhsT=xnb[:, fc * 1024 + oc * 128:fc * 1024 + oc * 128 + 128],
                        rhs=otb[:, fc * T + tt * TT:fc * T + tt * TT + TT],
                        start=(fc == 0), stop=(fc == NCH - 1))
                smm.inc(inst)

    @blk.scalar
    def _(s):
        for tt in range(NT):
            for oc in range(NCH):
                step = tt * NCH + oc
                pb = step % 2
                s.wait_ge(smm.h, b[smm] + step + 1)
                if step >= 2:
                    s.wait_ge(sdv.h, b[sdv] + 2 * (step - 1))
                    supd.wait_done(s, bupd, step - 2)
                sev.inc(s.activation(out=evb[:, 0:TT] if pb == 0 else evb2[:, 0:TT],
                                     in_=ps[pb][:, 0:TT], func=AF.Identity,
                                     bias=B["biasA"][:, 16 + oc:16 + oc + 1], scale=1.0))

    @blk.vector
    def _(v):
        for tt in range(NT):
            for oc in range(NCH):
                step = tt * NCH + oc
                pb = step % 2
                eb = evb if pb == 0 else evb2
                v.wait_ge(sev.h, b[sev] + step + 1)
                sgl.wait_done(v, bg, step)
                sxl.wait_done(v, bx2, step)
                P.ve(v, v.tensor_mul(out=eb[:, 0:TT], in0=eb[:, 0:TT],
                             in1=ar3[:, 4096 + pb * 512:4096 + pb * 512 + 512]))
                sdv.inc(v.tensor_add(out=eb[:, 0:TT], in0=eb[:, 0:TT],
                                     in1=ar3[:, 5120 + pb * 512:5120 + pb * 512 + 512]), 2)

    @blk.gpsimd
    def _(g):
        for tt in range(NT):
            for oc in range(NCH):
                step = tt * NCH + oc
                pb = step % 2
                g.wait_ge(sdv.h, b[sdv] + 2 * (step + 1))
                supd.inc(g.dma_start(
                    out=xo[oc * 128:(oc + 1) * 128, tt * TT:(tt + 1) * TT],
                    in_=evb[:, 0:TT] if pb == 0 else evb2[:, 0:TT]), step)




def emit_ffn_l3(P, l, li, B, W):
    _emit_ffn_moe(P, l, li, B, W, E=8, shared=False, HID_CH=16)



_PROG_CACHE = {}


def _get_program(layers_key, host_shapes_key):
    if layers_key not in _PROG_CACHE:
        shapes = {l: dict(s) for l, s in host_shapes_key}
        _PROG_CACHE[layers_key] = build_program(list(layers_key), shapes)
    return _PROG_CACHE[layers_key]


FUSED = True


def kernel(**inputs):
    x_full = np.asarray(inputs["L0_attn_x"], dtype=np.float32)
    per_layer_w = {l: layer_inputs(l, inputs) for l in range(4)}
    shapes = {l: {k: v.shape for k, v in w.items()} for l, w in per_layer_w.items()}

    xs = [np.ascontiguousarray(x_full[c // 2].T) for c in range(8)]
    if FUSED:
        key = (0, 1, 2, 3)
        if key not in _PROG_CACHE:
            _PROG_CACHE[key] = build_program([0, 1, 2, 3], shapes)
        nc = _PROG_CACHE[key]
        in_maps = []
        for c in range(8):
            m = {"x": xs[c]}
            for l in range(4):
                for k, v in per_layer_w[l].items():
                    m[f"L{l}_{k}"] = v
            in_maps.append(m)
        res = run_bass_kernel_spmd(nc, in_maps, core_ids=list(range(8)))
        xs = [np.ascontiguousarray(np.asarray(res.results[c]["xo"])) for c in range(8)]
    else:
        for l in range(4):
            key = (l,)
            if key not in _PROG_CACHE:
                _PROG_CACHE[key] = build_program([l], shapes)
            nc = _PROG_CACHE[key]
            in_maps = []
            for c in range(8):
                m = {"x": xs[c]}
                for k, v in per_layer_w[l].items():
                    m[f"L{l}_{k}"] = v
                in_maps.append(m)
            res = run_bass_kernel_spmd(nc, in_maps, core_ids=list(range(8)))
            xs = [np.ascontiguousarray(np.asarray(res.results[c]["xo"])) for c in range(8)]
    out = np.stack([xs[2 * b].T for b in range(4)], axis=0)
    return np.ascontiguousarray(out.astype(np.float32))



def emit_ffn_l1(P, l, li, B, W):
    blk = P.block
    xnb, otb, ps, ar3 = B["xnb"], B["otb"], B["ps"], B["ar3"]
    evb, evb2 = B["evb"], B["evb2"]
    rows, rowsb = B["rows"], B["rowsb"]
    oner = B["oner"]
    biasA = B["biasA"]
    gbuf = B["gbuf"]
    sap = P.sem(f"ln_ap_{li}")
    sap_done = sap.n
    SC = [0, 512, 1024, 1536, 4096, 4608, 5120, 5632]
    TMPC = 3072
    sgn = {0: 1, 1: 1, 2: -1, 3: -1}

    swr = P.sem(f"c1_wr_{li}", kind="s")
    sgm = P.sem(f"c1_gm_{li}")
    sgc = P.sem(f"c1_gc_{li}")
    sge = P.sem(f"c1_ge_{li}")
    sgf = P.sem(f"c1_gf_{li}")
    sic = P.sem(f"c1_ic_{li}")
    scc = P.sem(f"c1_cc_{li}")
    h1m = P.sem(f"c1_h1m_{li}")
    h1e = P.sem(f"c1_h1e_{li}")
    h2m = P.sem(f"c1_h2m_{li}")
    h2e = P.sem(f"c1_h2e_{li}")
    h3m = P.sem(f"c1_h3m_{li}")
    h3e = P.sem(f"c1_h3e_{li}")
    h4m = P.sem(f"c1_h4m_{li}")
    str_ = P.sem(f"c1_tf_{li}")
    srw = P.sem(f"c1_rw_{li}")
    scb = P.sem(f"c1_cb_{li}")
    sud = P.sem(f"c1_ud_{li}")
    ssl = P.sem(f"c1_sl_{li}")
    sems = [swr, sgm, sgc, sge, sgf, sic, scc, h1m, h1e, h2m, h2e, h3m, h3e,
            h4m, str_, srw, scb, sud, ssl]
    b = {s: s.n for s in sems}
    sgw = P.pair(f"f_fw_{li}")
    bgw = sgw.base()

    @blk.gpsimd
    def _(g):
        g.wait_ge(sap.h, sap_done)
        loads = [("cn_w1", 0, 8), ("cn_w2", 8192, 8), ("cn_w3", 12288, 4),
                 ("cn_w4", 13312, 2), ("gc_w1", 13440, 8), ("gc_w2", 17536, 4),
                 ("gc_w3", 18560, 2)]
        for name, off, k in loads:
            wshape = W[(l, name)]
            swr.inc(g.dma_start(
                out=otb[:, off:off + k * (wshape.shape[1])].rearrange(
                    "p (k m) -> p k m", k=k),
                in_=wshape.rearrange("(k p) m -> p k m", p=128),
                allow_slow_non_contiguous=(wshape.shape[1] == 1)), 16)
        for name, sl, full in [("cn_b1", (0, 8), True), ("cn_b2", (8, 12), True),
                               ("cn_b3", (12, 14), True), ("cn_b4", (14, 15), False),
                               ("gc_b1", (16, 20), True), ("gc_b2", (20, 22), True),
                               ("gc_b3", (22, 23), False), ("adrange", (23, 24), False)]:
            ap = biasA[:, sl[0]:sl[1]] if full else biasA[0:1, sl[0]:sl[1]]
            swr.inc(g.dma_start(out=ap, in_=W[(l, name)]), 16)

    NW = 15

    @blk.vector
    def _(v):
        v.wait_ge(sap.h, sap_done)
        for c in range(NCH):
            P.ve(v, v.reduce_sum(out=B["rbc"][:, c:c + 1], in_=xnb[:, c * T:(c + 1) * T],
                                 axis=AX.X))
        P.ve(v, v.tensor_scalar(out=B["rbc"][:, 8:16], in0=B["rbc"][:, 0:8],
                                scalar1=1.0 / T, scalar2=None, op0=OP.mult))
        sgm.inc(v.tensor_copy(out=B["smallb"][:, 0:8], in_=B["rbc"][:, 8:16]))

    @blk.tensor
    def _(q):
        q.wait_ge(sgm.h, b[sgm] + 1)
        q.wait_ge(swr.h, b[swr] + 16 * NW)
        for oc in range(4):
            for c in range(NCH):
                inst = q.matmul(out=ps[4][:, oc:oc + 1],
                                lhsT=otb[:, 13440 + c * 512 + oc * 128:
                                         13440 + c * 512 + oc * 128 + 128],
                                rhs=B["smallb"][:, c:c + 1], start=(c == 0),
                                stop=(c == NCH - 1))
            sgc.inc(inst)
        q.wait_ge(sge.h, b[sge] + 4)
        for oc in range(2):
            for c in range(4):
                inst = q.matmul(out=ps[5][:, oc:oc + 1],
                                lhsT=otb[:, 17536 + c * 256 + oc * 128:
                                         17536 + c * 256 + oc * 128 + 128],
                                rhs=B["smallb"][:, 16 + c:16 + c + 1], start=(c == 0),
                                stop=(c == 3))
            sgc.inc(inst)
        q.wait_ge(sge.h, b[sge] + 6)
        for c in range(2):
            inst = q.matmul(out=ps[4][0:1, 8:9],
                            lhsT=otb[:, 18560 + c:18560 + c + 1],
                            rhs=B["smallb"][:, 24 + c:24 + c + 1], start=(c == 0),
                            stop=(c == 1))
        sgc.inc(inst)

    @blk.scalar
    def _(s):
        for oc in range(4):
            s.wait_ge(sgc.h, b[sgc] + oc + 1)
            sge.inc(s.activation(out=B["smallb"][:, 16 + oc:17 + oc],
                                 in_=ps[4][:, oc:oc + 1], func=AF.Gelu,
                                 bias=biasA[:, 16 + oc:17 + oc], scale=1.0))
        for oc in range(2):
            s.wait_ge(sgc.h, b[sgc] + 4 + oc + 1)
            sge.inc(s.activation(out=B["smallb"][:, 24 + oc:25 + oc],
                                 in_=ps[5][:, oc:oc + 1], func=AF.Gelu,
                                 bias=biasA[:, 20 + oc:21 + oc], scale=1.0))
        s.wait_ge(sgc.h, b[sgc] + 7)
        sgf.inc(s.activation(out=rows["r5"][0:1, 4:5], in_=ps[4][0:1, 8:9],
                             func=AF.Sigmoid, bias=biasA[0:1, 22:23], scale=1.0))

    @blk.vector
    def _(v):
        v.wait_ge(sgf.h, b[sgf] + 1)
        sgf.inc(v.tensor_scalar(out=rows["r5"][0:1, 5:6], in0=rows["r5"][0:1, 4:5],
                                scalar1=0.7, scalar2=None, op0=OP.mult))

    @blk.tensor
    def _(q):
        q.wait_ge(sgf.h, b[sgf] + 2)
        for tt in range(NT):
            for cyc in range(3):
                idx = tt * 3 + cyc
                q.wait_ge(scc.h, b[scc] + idx + 1)
                for hc in range(NCH):
                    g1 = idx * 8 + hc
                    pb = hc % 2
                    if hc <= 1:
                        if idx >= 1:
                            q.wait_ge(h3e.h, b[h3e] + 2 * idx)
                    elif g1 >= 2:
                        q.wait_ge(h1e.h, b[h1e] + g1 - 1)
                    for c in range(NCH):
                        inst = q.matmul(out=ps[pb][:, 0:TT],
                                        lhsT=otb[:, c * 1024 + hc * 128:c * 1024 + hc * 128 + 128],
                                        rhs=otb[:, 20480 + c * 512:20480 + c * 512 + 512],
                                        start=(c == 0), stop=(c == NCH - 1))
                    h1m.inc(inst)
                q.wait_ge(h1e.h, b[h1e] + 8 * (idx + 1))
                for hc in range(4):
                    g2 = idx * 4 + hc
                    pb = hc % 2
                    if g2 >= 2:
                        q.wait_ge(h2e.h, b[h2e] + g2 - 1)
                    for c in range(NCH):
                        inst = q.matmul(out=ps[2 + pb][:, 0:TT],
                                        lhsT=otb[:, 8192 + c * 512 + hc * 128:
                                                 8192 + c * 512 + hc * 128 + 128],
                                        rhs=otb[:, 24576 + c * 512:24576 + c * 512 + 512],
                                        start=(c == 0), stop=(c == NCH - 1))
                    h2m.inc(inst)
                q.wait_ge(h2e.h, b[h2e] + 4 * (idx + 1))
                for hc in range(2):
                    g3 = idx * 2 + hc
                    pb = hc % 2
                    for c in range(4):
                        inst = q.matmul(out=ps[pb][:, 0:TT],
                                        lhsT=otb[:, 12288 + c * 256 + hc * 128:
                                                 12288 + c * 256 + hc * 128 + 128],
                                        rhs=otb[:, 28672 + c * 512:28672 + c * 512 + 512],
                                        start=(c == 0), stop=(c == 3))
                    h3m.inc(inst)
                q.wait_ge(h3e.h, b[h3e] + 2 * (idx + 1))
                if idx >= 1:
                    q.wait_ge(str_.h, b[str_] + idx)
                for c in range(2):
                    inst = q.matmul(out=ps[5][0:1, 0:TT],
                                    lhsT=otb[:, 13312 + c:13312 + c + 1],
                                    rhs=otb[:, 30720 + c * 512:30720 + c * 512 + 512],
                                    start=(c == 0), stop=(c == 1))
                h4m.inc(inst)
                q.wait_ge(srw.h, b[srw] + idx + 1)
                if idx >= 1:
                    q.wait_ge(sud.h, b[sud] + idx)
                q.matmul(out=ps[6][:, 0:TT], lhsT=oner[0:1, :],
                         rhs=rowsb["b1"][0:1, 0:TT], start=True, stop=False)
                scb.inc(q.matmul(out=ps[6][:, 0:TT], lhsT=oner[0:1, :],
                                 rhs=rowsb["b2"][0:1, 0:TT], start=False, stop=True))

    @blk.scalar
    def _(s):
        for tt in range(NT):
            for cyc in range(3):
                idx = tt * 3 + cyc
                for hc in range(NCH):
                    s.wait_ge(h1m.h, b[h1m] + idx * 8 + hc + 1)
                    h1e.inc(s.activation(out=otb[:, 24576 + hc * 512:24576 + hc * 512 + 512],
                                         in_=ps[hc % 2][:, 0:TT], func=AF.Gelu,
                                         bias=biasA[:, hc:hc + 1], scale=1.0))
                for hc in range(4):
                    s.wait_ge(h2m.h, b[h2m] + idx * 4 + hc + 1)
                    h2e.inc(s.activation(out=otb[:, 28672 + hc * 512:28672 + hc * 512 + 512],
                                         in_=ps[2 + hc % 2][:, 0:TT], func=AF.Gelu,
                                         bias=biasA[:, 8 + hc:9 + hc], scale=1.0))
                for hc in range(2):
                    s.wait_ge(h3m.h, b[h3m] + idx * 2 + hc + 1)
                    h3e.inc(s.activation(out=otb[:, 30720 + hc * 512:30720 + hc * 512 + 512],
                                         in_=ps[hc % 2][:, 0:TT], func=AF.Gelu,
                                         bias=biasA[:, 12 + hc:13 + hc], scale=1.0))
                s.wait_ge(h4m.h, b[h4m] + idx + 1)
                if idx >= 1:
                    s.wait_ge(srw.h, b[srw] + idx)
                str_.inc(s.activation(out=rows["r1"][0:1, 0:TT], in_=ps[5][0:1, 0:TT],
                                      func=AF.Sigmoid, bias=biasA[0:1, 14:15], scale=1.0))
            for c in range(NCH):
                gsl = tt * NCH + c
                s.wait_ge(sud.h, b[sud] + 3 * (tt + 1))
                if gsl >= 2:
                    sgw.wait_done(s, bgw, gsl - 2)
                ssl.inc(s.activation(out=evb[:, 0:TT] if gsl % 2 == 0 else evb2[:, 0:TT],
                                     in_=ar3[:, SC[c]:SC[c] + TT], func=AF.Silu,
                                     scale=1.0))

    @blk.vector
    def _(v):
        for tt in range(NT):
            t0 = tt * TT
            if tt >= 1:
                v.wait_ge(ssl.h, b[ssl] + tt * NCH)
            for c in range(NCH):
                inst = v.tensor_copy(out=ar3[:, SC[c]:SC[c] + TT],
                                     in_=xnb[:, c * T + t0:c * T + t0 + TT])
            sic.inc(inst)
            for cyc in range(3):
                idx = tt * 3 + cyc
                for c in range(NCH):
                    inst = v.tensor_copy(out=otb[:, 20480 + c * 512:20480 + c * 512 + 512],
                                         in_=ar3[:, SC[c]:SC[c] + TT])
                scc.inc(inst)
                v.wait_ge(str_.h, b[str_] + idx + 1)
                P.ve(v, v.tensor_scalar(out=rows["r2"][0:1, 0:TT], in0=rows["r1"][0:1, 0:TT],
                                        scalar1=0.3, scalar2=None, op0=OP.mult))
                P.ve(v, v.tensor_scalar(out=rows["r2"][0:1, 0:TT], in0=rows["r2"][0:1, 0:TT],
                                        scalar1=rows["r5"][0:1, 5:6], scalar2=None,
                                        op0=OP.add))
                P.ve(v, v.tensor_scalar(out=rows["r2"][0:1, 0:TT], in0=rows["r2"][0:1, 0:TT],
                                        scalar1=2.0, scalar2=-1.0, op0=OP.mult, op1=OP.add))
                P.ve(v, v.tensor_scalar(out=rows["r2"][0:1, 0:TT], in0=rows["r2"][0:1, 0:TT],
                                        scalar1=biasA[0:1, 23:24], scalar2=0.1,
                                        op0=OP.mult, op1=OP.add))
                P.ve(v, v.tensor_copy(out=rowsb["b1"][0:1, 0:TT], in_=rows["r2"][0:1, 0:TT]))
                P.ve(v, v.tensor_sub(out=rows["r3"][0:1, 0:TT], in0=rows["r2"][0:1, 0:TT],
                                     in1=rowsb["b1"][0:1, 0:TT]))
                srw.inc(v.tensor_copy(out=rowsb["b2"][0:1, 0:TT], in_=rows["r3"][0:1, 0:TT]))
                v.wait_ge(scb.h, b[scb] + idx + 1)
                USC = 3584
                for c in range(4):
                    c2 = c + 4
                    P.ve(v, v.tensor_copy(out=ar3[:, TMPC:TMPC + TT],
                                          in_=ar3[:, SC[c]:SC[c] + TT]))
                    P.ve(v, v.tensor_mul(out=ar3[:, USC:USC + TT],
                                         in0=ar3[:, SC[c2]:SC[c2] + TT],
                                         in1=ps[6][:, 0:TT]))
                    op1 = OP.add if sgn[c] > 0 else OP.subtract
                    P.ve(v, v.tensor_tensor(out=ar3[:, SC[c]:SC[c] + TT],
                                            in0=ar3[:, SC[c]:SC[c] + TT],
                                            in1=ar3[:, USC:USC + TT], op=op1))
                    P.ve(v, v.tensor_mul(out=ar3[:, USC:USC + TT],
                                         in0=ar3[:, TMPC:TMPC + TT],
                                         in1=ps[6][:, 0:TT]))
                    op2 = OP.subtract if sgn[c] > 0 else OP.add
                    P.ve(v, v.tensor_tensor(out=ar3[:, SC[c2]:SC[c2] + TT],
                                            in0=ar3[:, SC[c2]:SC[c2] + TT],
                                            in1=ar3[:, USC:USC + TT], op=op2))
                sud.inc(v.engine_nop())

    @blk.sync
    def _(sy):
        for tt in range(NT):
            for c in range(NCH):
                gsl = tt * NCH + c
                sy.wait_ge(ssl.h, b[ssl] + gsl + 1)
                sgw.inc(sy.dma_start(out=gbuf[c * 128:(c + 1) * 128, tt * TT:(tt + 1) * TT],
                                     in_=evb[:, 0:TT] if gsl % 2 == 0 else evb2[:, 0:TT]),
                        gsl)

def emit_qkv_l1(P, l, li, B, W):
    blk = P.block
    xnb, otb, ps, cstb = B["xnb"], B["otb"], B["ps"], B["cstb"]
    rows, rowsb, hr = B["rows"], B["rowsb"], B["hr"]
    onec, oner, smallb2 = B["onec"], B["oner"], B["smallb2"]
    qpack, kpack, vpack = B["qpack"], B["kpack"], B["vpack"]
    biasA = B["biasA"]
    sap = P.sem(f"ln_ap_{li}")
    sap_done = sap.n
    p2_ud = P.sem(f"c1_ud_{li}")
    p2_ud_done = p2_ud.n
    p2_sl = P.sem(f"c1_sl_{li}")
    p2_sl_done = p2_sl.n
    p2_fw = P.pair(f"f_fw_{li}")
    p2_fw_tot = p2_fw.total()

    sw = P.sem(f"a1_w_{li}", kind="h")
    swq = P.pair(f"a1_wq_{li}", kind="s")
    svw = P.sem(f"a1_vw_{li}", kind="s")
    smm = P.sem(f"a1_mm_{li}")
    sev = P.sem(f"a1_ev_{li}")
    ssq = P.sem(f"a1_sq_{li}")
    srs = P.sem(f"a1_rs_{li}")
    snx = P.sem(f"a1_nx_{li}")
    sst = P.pair(f"a1_st_{li}")
    svm = P.sem(f"a1_vm_{li}")
    sve = P.sem(f"a1_ve_{li}")
    svs = P.pair(f"a1_vs_{li}")
    srow1 = P.sem(f"a1_row1_{li}")
    spw = P.sem(f"a1_pw_{li}", kind="s")
    spm = P.sem(f"a1_pm_{li}")
    spe = P.sem(f"a1_pe_{li}")
    sxm = P.sem(f"a1_xm_{li}")
    sxe = P.sem(f"a1_xe_{li}")
    sld = P.sem(f"a1_ld_{li}")
    snm = P.sem(f"a1_nm_{li}")
    sne = P.sem(f"a1_ne_{li}")
    snr = P.sem(f"a1_nr_{li}")
    snb = P.sem(f"a1_nb_{li}")
    sli = P.sem(f"a1_li_{li}")
    sjm = P.sem(f"a1_jm_{li}")
    sje = P.sem(f"a1_je_{li}")
    sjs = P.sem(f"a1_js_{li}", kind="h")
    sjc = P.sem(f"a1_jc_{li}")
    sbs = P.sem(f"a1_bs_{li}")
    b = {s: s.n for s in [sw, svw, smm, sev, ssq, srs, snx, svm, sve, srow1,
                          spw, spm, spe, sxm, sxe, sld, snm, sne, snr, snb, sli, sbs,
                          sjm, sje, sjs, sjc]}
    bwq, bst, bvs = swq.base(), sst.base(), svs.base()

    @blk.sync
    def _(sy):
        sy.wait_ge(p2_ud.h, p2_ud_done)
        sy.wait_ge(p2_sl.h, p2_sl_done)
        sy.wait_ge(p2_fw.s[0].h, p2_fw_tot[0])
        sy.wait_ge(p2_fw.s[1].h, p2_fw_tot[1])
        sy.dma_start(out=biasA[:, 0:16], in_=W[(l, "bqkv")]).then_inc(sw.h, 16)
        sw.n += 16
        sy.dma_start(out=rows["r4"][0:1, 0:512],
                     in_=W[(l, "bqkv_v")][0:1, 0:512]).then_inc(sw.h, 16)
        sw.n += 16
        sy.dma_start(out=rows["r3"][0:1, 0:512],
                     in_=W[(l, "bqkv_v")][0:1, 512:1024]).then_inc(sw.h, 16)
        sw.n += 16
        sy.dma_start(out=smallb2[0:64, 0:96], in_=W[(l, "c_selI")]).then_inc(sw.h, 16)
        sw.n += 16
        sy.dma_start(out=smallb2[0:64, 96:192], in_=W[(l, "c_selJ")]).then_inc(sw.h, 16)
        sw.n += 16
        sy.dma_start(out=smallb2[0:96, 192:208], in_=W[(l, "c_G96")]).then_inc(sw.h, 16)
        sw.n += 16
        sy.dma_start(out=smallb2[0:16, 208:304], in_=W[(l, "c_E96")]).then_inc(sw.h, 16)
        sw.n += 16
        sy.dma_start(out=smallb2[0:96, 304:400], in_=W[(l, "c_J96")]).then_inc(sw.h, 16)
        sw.n += 16
        sy.dma_start(out=hr["h5"][0:1, 0:16],
                     in_=W[(l, "pl_scale")].rearrange("e one -> one e")).then_inc(sw.h, 16)
        sw.n += 16
        sy.dma_start(out=B["rbc"][0:16, 1:2], in_=W[(l, "pl_scale")],
                     allow_slow_non_contiguous=True).then_inc(sw.h, 16)
        sw.n += 16

    @blk.gpsimd
    def _(g):
        g.wait_ge(p2_ud.h, p2_ud_done)
        g.wait_ge(p2_sl.h, p2_sl_done)
        g.wait_ge(p2_fw.s[0].h, p2_fw_tot[0])
        g.wait_ge(p2_fw.s[1].h, p2_fw_tot[1])
        for oc in range(16):
            if oc >= 2:
                g.wait_ge(smm.h, b[smm] + (oc - 1) * NT)
            swq.inc(g.dma_start(
                out=otb[:, 5664 + (oc % 2) * 1024:5664 + (oc % 2) * 1024 + 1024].rearrange(
                    "p (k m) -> p k m", k=8),
                in_=W[(l, "wqkv")][:, oc * 128:(oc + 1) * 128].rearrange(
                    "(k p) m -> p k m", p=128)), oc)
        for vc in range(2):
            svw.inc(g.dma_start(
                out=otb[:, 8192 + vc * 4096:8192 + (vc + 1) * 4096].rearrange(
                    "p (k m) -> p k m", k=8),
                in_=W[(l, "wqkv")][:, 2048 + vc * 512:2048 + (vc + 1) * 512].rearrange(
                    "(k p) m -> p k m", p=128)), 16)
        spw.inc(g.dma_start(
            out=otb[:, 16384:16896].rearrange("p (k m) -> p k m", k=8),
            in_=W[(l, "w1p")].rearrange("(k p) m -> p k m", p=128)), 16)
        spw.inc(g.dma_start(
            out=otb[:, 16896:17408].rearrange("p (k m) -> p k m", k=8),
            in_=W[(l, "w2p")].rearrange("(k p) m -> p k m", p=128)), 16)

    @blk.vector
    def _(v):
        v.wait_ge(p2_ud.h, p2_ud_done)
        v.wait_ge(p2_sl.h, p2_sl_done)
        for i in range(2):
            srow1.inc(v.memset(otb[:, 3584 + i * 1040 + 64:3584 + i * 1040 + 1040:65], 1.0))
        v.memset(hr["h1"][0:1, 0:4], 0.0)
        v.wait_ge(sw.h, b[sw] + 16 * 3)
        sbs.inc(v.tensor_scalar(out=biasA[:, 0:8], in0=biasA[:, 0:8],
                                scalar1=0.125, scalar2=None, op0=OP.mult))
        v.wait_ge(sbs.h, b[sbs] + 1)
        P.ve(v, v.tensor_copy(out=rowsb["b5"][0:1, 0:512], in_=rows["r4"][0:1, 0:512]))
        srow1.inc(v.tensor_copy(out=rowsb["b5"][0:1, 512:1024],
                                in_=rows["r3"][0:1, 0:512]))
        v.wait_ge(sw.h, b[sw] + 16 * 10)
        sjc.inc(v.tensor_copy(out=B["cwtb"][0:16, 0:1], in_=B["rbc"][0:16, 1:2]))

    @blk.tensor
    def _(q):
        q.wait_ge(sw.h, b[sw] + 16 * 8)
        q.wait_ge(sjc.h, b[sjc] + 1)
        sjm.inc(q.matmul(out=ps[6][0:96, 0:1], lhsT=smallb2[0:16, 208:304],
                         rhs=B["cwtb"][0:16, 0:1], start=True, stop=True))

    @blk.vector
    def _(v):
        v.wait_ge(sjm.h, b[sjm] + 1)
        P.ve(v, v.tensor_copy(out=B["rbc"][:, 0:1][0:96, 0:1], in_=ps[6][0:96, 0:1]))
        sjc.inc(v.tensor_scalar(out=smallb2[0:96, 400:496], in0=smallb2[0:96, 304:400],
                                scalar1=B["rbc"][0:96, 0:1], scalar2=None, op0=OP.mult))

    @blk.tensor
    def _(q):
        for oc in range(16):
            swq.wait_done(q, bwq, oc)
            for tt in range(NT):
                step = oc * NT + tt
                pb = step % 2
                q.wait_ge(sap.h, sap_done - NT + tt + 1)
                if step >= 2:
                    q.wait_ge(sev.h, b[sev] + step - 1)
                wb0 = 5664 + (oc % 2) * 1024
                for c in range(NCH):
                    inst = q.matmul(out=ps[pb][:, 0:TT],
                                    lhsT=otb[:, wb0 + c * 128:wb0 + (c + 1) * 128],
                                    rhs=xnb[:, c * T + tt * TT:c * T + tt * TT + TT],
                                    start=(c == 0), stop=(c == NCH - 1))
                smm.inc(inst)
                q.wait_ge(ssq.h, b[ssq] + step + 1)
                if step >= 1:
                    q.wait_ge(snx.h, b[snx] + 2 * step)
                q.matmul(out=ps[2][0:1, 0:TT], lhsT=onec[0:64, 0:1],
                         rhs=otb[0:64, 512:512 + TT], start=True, stop=True,
                         tile_position=(0, 0))
                srs.inc(q.matmul(out=ps[3][0:1, 0:TT], lhsT=onec[64:128, 0:1],
                                 rhs=otb[64:128, 512:512 + TT], start=True, stop=True,
                                 tile_position=(64, 0)))

    @blk.scalar
    def _(s):
        s.wait_ge(sbs.h, b[sbs] + 1)
        for oc in range(16):
            for tt in range(NT):
                step = oc * NT + tt
                pb = step % 2
                s.wait_ge(smm.h, b[smm] + step + 1)
                if step >= 1:
                    s.wait_ge(ssq.h, b[ssq] + step)
                if step >= 2:
                    sst.wait_done(s, bst, step - 2)
                eb0 = (step % 2) * 2048
                sev.inc(s.activation(out=otb[:, eb0:eb0 + TT], in_=ps[pb][:, 0:TT],
                                     func=AF.Identity, bias=biasA[:, oc:oc + 1],
                                     scale=(0.125 if oc < 8 else 1.0)))

    @blk.vector
    def _(v):
        for oc in range(16):
            for tt in range(NT):
                step = oc * NT + tt
                v.wait_ge(sev.h, b[sev] + step + 1)
                eb0 = (step % 2) * 2048
                ssq.inc(v.tensor_mul(out=otb[:, 512:512 + TT], in0=otb[:, eb0:eb0 + TT],
                                     in1=otb[:, eb0:eb0 + TT]))
                v.wait_ge(srs.h, b[srs] + step + 1)
                slot = 0 if oc < 8 else 1
                P.ve(v, v.reduce_max(out=hr["h4"][0:1, 0:1], in_=ps[2][0:1, 0:TT],
                                     axis=AX.X))
                P.ve(v, v.reduce_max(out=hr["h4"][0:1, 1:2], in_=ps[3][0:1, 0:TT],
                                     axis=AX.X))
                P.ve(v, v.tensor_max(out=hr["h4"][0:1, 2:3], in0=hr["h4"][0:1, 0:1],
                                     in1=hr["h4"][0:1, 1:2]))
                snx.inc(v.tensor_max(out=hr["h1"][0:1, slot:slot + 1],
                                     in0=hr["h1"][0:1, slot:slot + 1],
                                     in1=hr["h4"][0:1, 2:3]), 2)

    @blk.sync
    def _(sy):
        for oc in range(16):
            pack = qpack if oc < 8 else kpack
            prow = (oc % 8) * 128
            for tt in range(NT):
                step = oc * NT + tt
                sy.wait_ge(ssq.h, b[ssq] + step + 1)
                eb0 = (step % 2) * 2048
                sst.inc(sy.dma_start(out=pack[prow:prow + 128, tt * TT:tt * TT + TT],
                                     in_=otb[:, eb0:eb0 + TT]), step)

    @blk.tensor
    def _(q):
        q.wait_ge(svw.h, b[svw] + 32)
        q.wait_ge(srow1.h, b[srow1] + 3)
        q.wait_ge(sap.h, sap_done)
        for tk in range(32):
            for vc in range(2):
                step = tk * 2 + vc
                pb = step % 2
                if step >= 2:
                    q.wait_ge(sve.h, b[sve] + step - 1)
                q.matmul(out=ps[pb][:, 0:TT], lhsT=oner[0:1, 0:128],
                         rhs=rowsb["b5"][0:1, vc * 512:vc * 512 + 512],
                         start=True, stop=False)
                for c in range(NCH):
                    inst = q.matmul(
                        out=ps[pb][:, 0:TT],
                        lhsT=xnb[:, c * T + tk * 128:c * T + tk * 128 + 128],
                        rhs=otb[:, 8192 + vc * 4096 + c * 512:8192 + vc * 4096 + (c + 1) * 512],
                        start=False, stop=(c == NCH - 1))
                svm.inc(inst)

    @blk.scalar
    def _(s):
        for tk in range(32):
            for vc in range(2):
                step = tk * 2 + vc
                pb = step % 2
                s.wait_ge(svm.h, b[svm] + step + 1)
                if tk >= 2 and vc == 0:
                    svs.wait_done(s, bvs, tk - 2)
                vr0 = 3584 + (tk % 2) * 1040
                sve.inc(s.copy(
                    out=otb[:, vr0 + vc * 520:vr0 + (vc + 1) * 520].rearrange(
                        "p (h d) -> p h d", h=8)[:, :, 0:64],
                    in_=ps[pb][:, 0:TT].rearrange("p (h d) -> p h d", h=8)))

    @blk.sync
    def _(sy):
        for tk in range(32):
            sy.wait_ge(sve.h, b[sve] + 2 * (tk + 1))
            svs.inc(sy.dma_start(out=vpack[:, tk * 1040:(tk + 1) * 1040],
                                 in_=otb[:, 3584 + (tk % 2) * 1040:3584 + (tk % 2) * 1040 + 1040]),
                    tk)

    @blk.tensor
    def _(q):
        q.wait_ge(spw.h, b[spw] + 32)
        q.wait_ge(sev.h, b[sev] + 128)
        q.wait_ge(sve.h, b[sve] + 64)
        for tt in range(NT):
            t0 = tt * TT
            q.wait_ge(sap.h, sap_done - NT + tt + 1)
            for pi in range(2):
                if tt >= 1 and pi == 0:
                    q.wait_ge(sne.h, b[sne] + tt)
                if tt >= 1 or pi >= 1:
                    q.wait_ge(spe.h, b[spe] + tt * 2 + pi - 1 + 1)
                w0 = 16384 + pi * 512
                for c in range(NCH):
                    inst = q.matmul(out=ps[4][0:64, 0:TT],
                                    lhsT=otb[:, w0 + c * 64:w0 + (c + 1) * 64],
                                    rhs=xnb[:, c * T + t0:c * T + t0 + TT],
                                    start=(c == 0), stop=(c == NCH - 1))
                spm.inc(inst)
            q.wait_ge(spe.h, b[spe] + 2 * (tt + 1))
            for pr in range(4):
                src_p = 1024 if pr < 2 else 1536
                sel0 = 0 if pr % 2 == 0 else 96
                if tt >= 1 and pr == 0:
                    q.wait_ge(sld.h, b[sld] + 2 * tt)
                if tt >= 1 or pr >= 2:
                    q.wait_ge(sxe.h, b[sxe] + tt * 4 + pr - 2 + 1)
                sxm.inc(q.matmul(out=ps[pr % 2][0:96, 0:TT],
                                 lhsT=smallb2[0:64, sel0:sel0 + 96],
                                 rhs=otb[0:64, src_p:src_p + TT], start=True, stop=True))
            q.wait_ge(sld.h, b[sld] + 2 * (tt + 1))
            snm.inc(q.matmul(out=ps[4][0:16, 0:TT], lhsT=smallb2[0:96, 192:208],
                             rhs=cstb[0:96, 2560:2560 + TT], start=True, stop=True))
            q.wait_ge(snr.h, b[snr] + tt + 1)
            snb.inc(q.matmul(out=ps[5][0:96, 0:TT], lhsT=smallb2[0:16, 208:304],
                             rhs=B["cwtb"][0:16, 1:1 + TT] if False else
                             cstb[0:16, 3072 + 0:3072 + TT], start=True, stop=True))
            q.wait_ge(sli.h, b[sli] + tt + 1)
            q.wait_ge(sjc.h, b[sjc] + 2)
            if tt >= 1:
                q.wait_ge(sje.h, b[sje] + tt)
            sjm.inc(q.matmul(out=ps[6][0:96, 0:TT], lhsT=smallb2[0:96, 400:496],
                             rhs=cstb[0:96, 3584 + t0 // 8:3584 + t0 // 8 + TT] if False else
                             cstb[0:96, 4096 + 0:4096 + TT], start=True, stop=True))

    @blk.scalar
    def _(s):
        for tt in range(NT):
            for pi in range(2):
                s.wait_ge(spm.h, b[spm] + tt * 2 + pi + 1)
                spe.inc(s.copy(out=otb[0:64, 1024 + pi * 512:1024 + pi * 512 + TT],
                               in_=ps[4][0:64, 0:TT]))
            for pr in range(4):
                s.wait_ge(sxm.h, b[sxm] + tt * 4 + pr + 1)
                spe.inc(s.copy(out=cstb[0:96, pr * 512:pr * 512 + TT],
                               in_=ps[pr % 2][0:96, 0:TT]), 0) if False else None
                sxe.inc(s.copy(out=cstb[0:96, pr * 512:pr * 512 + TT],
                               in_=ps[pr % 2][0:96, 0:TT]))
            s.wait_ge(snm.h, b[snm] + tt + 1)
            sne.inc(s.activation(out=rows["r1"][0:1, 0:16][0:1, 0:16] if False else
                                 cstb[0:16, 3584:3584 + TT], in_=ps[4][0:16, 0:TT],
                                 func=AF.Sqrt, scale=1.0), 0) if False else None
            sne.inc(s.activation(out=B["ar3"][0:16, 1408:1408 + TT], in_=ps[4][0:16, 0:TT],
                                 func=AF.Sqrt, scale=1.0))

    @blk.vector
    def _(v):
        for tt in range(NT):
            v.wait_ge(sxe.h, b[sxe] + 4 * (tt + 1))
            P.ve(v, v.tensor_mul(out=cstb[0:96, 2048:2048 + TT], in0=cstb[0:96, 0:TT],
                                 in1=cstb[0:96, 1536:1536 + TT]))
            P.ve(v, v.tensor_mul(out=cstb[0:96, 2560:2560 + TT], in0=cstb[0:96, 512:512 + TT],
                                 in1=cstb[0:96, 1024:1024 + TT]))
            P.ve(v, v.tensor_sub(out=cstb[0:96, 2048:2048 + TT],
                                 in0=cstb[0:96, 2048:2048 + TT],
                                 in1=cstb[0:96, 2560:2560 + TT]))
            sld.inc(v.engine_nop())
            P.ve(v, v.tensor_mul(out=cstb[0:96, 2560:2560 + TT],
                                 in0=cstb[0:96, 2048:2048 + TT],
                                 in1=cstb[0:96, 2048:2048 + TT]))
            sld.inc(v.engine_nop())
            v.wait_ge(sne.h, b[sne] + tt + 1)
            P.ve(v, v.tensor_scalar(out=B["ar3"][0:16, 1408:1408 + TT],
                                    in0=B["ar3"][0:16, 1408:1408 + TT],
                                    scalar1=1e-12, scalar2=None, op0=OP.max))
            P.ve(v, v.reciprocal(out=B["ar3"][0:16, 3584:3584 + TT],
                                 in_=B["ar3"][0:16, 1408:1408 + TT]))
            snr.inc(v.tensor_copy(out=cstb[0:16, 3072:3072 + TT],
                                  in_=B["ar3"][0:16, 3584:3584 + TT]))
            v.wait_ge(snb.h, b[snb] + tt + 1)
            if tt >= 1:
                v.wait_ge(sjs.h, b[sjs] + 32 * (tt - 1) + 16)
                v.wait_ge(sjm.h, b[sjm] + 1 + tt)
            P.ve(v, v.tensor_mul(out=cstb[0:96, 4096:4096 + TT],
                                 in0=cstb[0:96, 2048:2048 + TT], in1=ps[5][0:96, 0:TT]))
            sli.inc(v.engine_nop())
            v.wait_ge(sjm.h, b[sjm] + 1 + tt + 1)
            if tt >= 1:
                v.wait_ge(sjs.h, b[sjs] + 32 * tt)
            sje.inc(v.tensor_copy(out=cstb[0:96, 4608:4608 + TT], in_=ps[6][0:96, 0:TT]))

    @blk.sync
    def _(sy):
        for tt in range(NT):
            t0 = tt * TT
            sy.wait_ge(sli.h, b[sli] + tt + 1)
            sjs.inc(sy.dma_start(out=B["jqpack"][0:96, t0:t0 + TT],
                                 in_=cstb[0:96, 4096:4096 + TT]).then_inc(sjs.h, 0) if False
                    else sy.dma_start(out=B["jqpack"][0:96, t0:t0 + TT],
                                      in_=cstb[0:96, 4096:4096 + TT]), 16)
            sy.wait_ge(sje.h, b[sje] + tt + 1)
            sjs.inc(sy.dma_start(out=B["jkpack"][0:96, t0:t0 + TT],
                                 in_=cstb[0:96, 4608:4608 + TT]), 16)



def emit_attn_l1(P, l, li, B, W):
    blk = P.block
    xnb, otb, ps, evb = B["xnb"], B["otb"], B["ps"], B["evb"]
    rows, rowsb, hr = B["rows"], B["rowsb"], B["hr"]
    oner, onec = B["oner"], B["onec"]
    qpack, kpack, vpack = B["qpack"], B["kpack"], B["vpack"]
    gates = [P.sem(f"a1_{nm}_{li}") for nm in
             ["mm", "vm", "pm", "sq", "nx", "ne", "nr", "li", "je", "ev", "ve"]]
    gate_tot = [(s.h, s.n) for s in gates]
    gpair_tot = []
    for nm in ["st", "vs"]:
        pr = P.pair(f"a1_{nm}_{li}")
        gpair_tot += [(pr.s[0].h, pr.s[0].n), (pr.s[1].h, pr.s[1].n)]
    sjs_t = P.sem(f"a1_js_{li}", kind="h")
    gate_tot.append((sjs_t.h, sjs_t.n))
    sxe_t = P.sem(f"a1_xe_{li}")
    gate_tot.append((sxe_t.h, sxe_t.n))

    sld5 = P.sem(f"b1_ld_{li}", kind="s")
    smk0 = P.sem(f"b1_mk_{li}", kind="s")
    ssc = P.sem(f"b1_sc_{li}")
    sms = P.sem(f"b1_ms_{li}")
    sex = P.sem(f"b1_ex_{li}")
    spv = P.sem(f"b1_pv_{li}")
    sdc = P.sem(f"b1_dc_{li}")
    sdn = P.sem(f"b1_dn_{li}")
    srb = P.sem(f"b1_rb_{li}")
    snv = P.sem(f"b1_nv_{li}")
    sno = P.sem(f"b1_no_{li}")
    sbb1 = P.sem(f"b1_b1_{li}")
    sbb2 = P.sem(f"b1_b2_{li}")
    sbb4 = P.sem(f"b1_b4_{li}")
    b = {s: s.n for s in [sld5, smk0, ssc, sms, sex, spv, sdc, sdn, srb, snv,
                          sno, sbb1, sbb2, sbb4]}

    KH0 = lambda h: (h % 2) * 8192
    QH0 = lambda h: (h % 2) * 8192 + 4096
    VH0 = lambda h: 16384 + (h % 2) * 2080
    E0 = 20544
    MK0 = 21568

    jobs = [(h, qg) for h in range(16) for qg in range(8)]
    nk_of = lambda qg: 4 * (qg + 1)
    step_base = [0]
    for (h, qg) in jobs:
        step_base.append(step_base[-1] + nk_of(qg))
    SH = step_base[8]
    nmask_upto = []
    m = 0
    for (h, qg) in jobs:
        for ki in range(nk_of(qg)):
            if ki >= nk_of(qg) - 4:
                m += 1
            nmask_upto.append(m)

    @blk.gpsimd
    def _(g):
        for hh, nn in gate_tot:
            g.wait_ge(hh, nn)
        for hh, nn in gpair_tot:
            g.wait_ge(hh, nn)
        for kd in range(4):
            smk0.inc(g.dma_start(out=xnb[:, MK0 + kd * 512:MK0 + (kd + 1) * 512],
                                 in_=W[(l, "c_l1m")][kd]), 16)
        for h in range(16):
            if h >= 2:
                g.wait_ge(ssc.h, b[ssc] + SH * (h - 1))
                g.wait_ge(spv.h, b[spv] + SH * (h - 1))
            sld5.inc(g.dma_start(out=xnb[0:64, KH0(h):KH0(h) + T],
                                 in_=kpack[64 * h:64 * h + 64, :]), 16)
            sld5.inc(g.dma_start(out=xnb[64:70, KH0(h):KH0(h) + T],
                                 in_=B["jkpack"][6 * h:6 * h + 6, :]), 16)
            sld5.inc(g.dma_start(out=xnb[0:64, QH0(h):QH0(h) + T],
                                 in_=qpack[64 * h:64 * h + 64, :]), 16)
            sld5.inc(g.dma_start(out=xnb[64:70, QH0(h):QH0(h) + T],
                                 in_=B["jqpack"][6 * h:6 * h + 6, :]), 16)
            sld5.inc(g.dma_start(
                out=xnb[:, VH0(h):VH0(h) + 2080].rearrange("p (k w) -> p k w", k=32),
                in_=vpack[:, :].rearrange("p (k g w) -> p k g w", k=32, g=16)[:, :, h, :]),
                16)

    @blk.scalar
    def _(s_):
        for hh, nn in gate_tot[:8]:
            s_.wait_ge(hh, nn)
        sbb1.inc(s_.activation(out=hr["h2"][0:1, 0:2], in_=hr["h1"][0:1, 0:2],
                               func=AF.Sqrt, scale=1.0))

    @blk.vector
    def _(v):
        v.wait_ge(sbb1.h, b[sbb1] + 1)
        P.ve(v, v.tensor_mul(out=hr["h3"][0:1, 0:1], in0=hr["h2"][0:1, 0:1],
                             in1=hr["h2"][0:1, 1:2]))
        P.ve(v, v.tensor_scalar(out=hr["h4"][0:1, 0:16], in0=hr["h5"][0:1, 0:16],
                                scalar1=-1.0, scalar2=None, op0=OP.mult))
        P.ve(v, v.reduce_max(out=hr["h3"][0:1, 1:2], in_=hr["h5"][0:1, 0:16], axis=AX.X))
        P.ve(v, v.reduce_max(out=hr["h3"][0:1, 2:3], in_=hr["h4"][0:1, 0:16], axis=AX.X))
        P.ve(v, v.tensor_max(out=hr["h3"][0:1, 3:4], in0=hr["h3"][0:1, 1:2],
                             in1=hr["h3"][0:1, 2:3]))
        P.ve(v, v.tensor_add(out=hr["h3"][0:1, 4:5], in0=hr["h3"][0:1, 0:1],
                             in1=hr["h3"][0:1, 3:4]))
        P.ve(v, v.tensor_scalar(out=hr["h3"][0:1, 5:6], in0=hr["h3"][0:1, 4:5],
                                scalar1=-1.0, scalar2=None, op0=OP.mult))
        sbb2.inc(v.tensor_copy(out=B["rwb"][0:1, 32:33], in_=hr["h3"][0:1, 5:6]))

    @blk.tensor
    def _(q):
        q.wait_ge(sbb2.h, b[sbb2] + 1)
        sje_g = P.sem(f"a1_je_{li}")
        q.wait_ge(sje_g.h, sje_g.n)
        sbb1.inc(q.matmul(out=ps[6][:, 0:1], lhsT=oner[0:1, 0:128],
                          rhs=B["rwb"][0:1, 32:33], start=True, stop=True))

    @blk.scalar
    def _(s_):
        s_.wait_ge(sbb1.h, b[sbb1] + 2)
        sbb4.inc(s_.copy(out=B["bcol"][1][:, 0:1], in_=ps[6][:, 0:1]))

    @blk.tensor
    def _(q):
        for ji, (h, qg) in enumerate(jobs):
            nk = nk_of(qg)
            q0 = qg * 512
            N = 512
            q.wait_ge(sld5.h, b[sld5] + 80 * (h + 1))
            if ji == 0:
                q.wait_ge(smk0.h, b[smk0] + 64)

            def emit_pv(ki):
                s_ = step_base[ji] + ki
                q.wait_ge(sex.h, b[sex] + s_ + 1)
                if ki == 0:
                    q.wait_ge(snv.h, b[snv] + ji)
                spv.inc(q.matmul(out=ps[2][0:65, 0:N],
                                 lhsT=xnb[:, VH0(h) + ki * 65:VH0(h) + ki * 65 + 65],
                                 rhs=xnb[:, E0 + (s_ % 2) * 512:E0 + (s_ % 2) * 512 + N],
                                 start=(ki == 0), stop=(ki == nk - 1)))

            for ki in range(nk):
                s = step_base[ji] + ki
                if s >= 2:
                    q.wait_ge(sex.h, b[sex] + s - 1)
                ssc.inc(q.matmul(out=ps[s % 2][:, 0:N],
                                 lhsT=xnb[0:70, KH0(h) + ki * 128:KH0(h) + ki * 128 + 128],
                                 rhs=xnb[0:70, QH0(h) + q0:QH0(h) + q0 + N],
                                 start=True, stop=True))
                if ki >= 1:
                    emit_pv(ki - 1)
            emit_pv(nk - 1)
            q.wait_ge(sdn.h, b[sdn] + ji + 1)
            q.matmul(out=ps[6][0:64, 0:N], lhsT=oner[0:1, 0:64],
                     rhs=rowsb["b5"][0:1, 0:N], start=True, stop=False)
            srb.inc(q.matmul(out=ps[6][0:64, 0:N], lhsT=oner[0:1, 0:64],
                             rhs=rowsb["b6"][0:1, 0:N], start=False, stop=True))

    @blk.vector
    def _(v):
        for ji, (h, qg) in enumerate(jobs):
            nk = nk_of(qg)
            q0 = qg * 512
            N = 512
            for ki in range(nk):
                s = step_base[ji] + ki
                kd = ki - (nk - 4)
                if kd >= 0:
                    v.wait_ge(ssc.h, b[ssc] + s + 1)
                    sms.inc(v.tensor_add(out=ps[s % 2][:, 0:N], in0=ps[s % 2][:, 0:N],
                                         in1=xnb[:, MK0 + kd * 512:MK0 + kd * 512 + N]))
            v.wait_ge(spv.h, b[spv] + step_base[ji + 1])
            P.ve(v, v.tensor_copy(out=rows["r1"][0:1, 0:N], in_=ps[2][64:65, 0:N]))
            sdc.inc(v.engine_nop())
            P.ve(v, v.reciprocal(out=rows["r3"][0:1, 0:N], in_=rows["r1"][0:1, 0:N]))
            if ji >= 1:
                v.wait_ge(srb.h, b[srb] + ji)
            P.ve(v, v.tensor_copy(out=rowsb["b5"][0:1, 0:N], in_=rows["r3"][0:1, 0:N]))
            P.ve(v, v.tensor_sub(out=rows["r5"][0:1, 0:N], in0=rows["r3"][0:1, 0:N],
                                 in1=rowsb["b5"][0:1, 0:N]))
            sdn.inc(v.tensor_copy(out=rowsb["b6"][0:1, 0:N], in_=rows["r5"][0:1, 0:N]))
            v.wait_ge(snv.h, b[snv] + ji + 1)
            v.wait_ge(srb.h, b[srb] + ji + 1)
            r0 = 64 * (h % 2)
            sno.inc(v.tensor_mul(
                out=otb[r0:r0 + 64, (h // 2) * T + q0:(h // 2) * T + q0 + N],
                in0=evb[0:64, 0:N], in1=ps[6][0:64, 0:N]))

    @blk.scalar
    def _(s_):
        s_.wait_ge(sbb4.h, b[sbb4] + 1)
        for ji, (h, qg) in enumerate(jobs):
            nk = nk_of(qg)
            for ki in range(nk):
                s = step_base[ji] + ki
                if ki >= nk - 4:
                    s_.wait_ge(sms.h, b[sms] + nmask_upto[s])
                else:
                    s_.wait_ge(ssc.h, b[ssc] + s + 1)
                if s >= 2:
                    s_.wait_ge(spv.h, b[spv] + s - 1)
                sex.inc(s_.activation(out=xnb[:, E0 + (s % 2) * 512:E0 + (s % 2) * 512 + 512],
                                      in_=ps[s % 2][:, 0:512], func=AF.Exp,
                                      bias=B["bcol"][1][:, 0:1], scale=1.0))
            s_.wait_ge(sdc.h, b[sdc] + ji + 1)
            if ji >= 1:
                s_.wait_ge(sno.h, b[sno] + ji)
            snv.inc(s_.copy(out=evb[0:64, 0:512], in_=ps[2][0:64, 0:512]))



L2_LAMBDA_INIT = 0.8 - 0.6 * math.exp(-0.3 * 11)


def emit_qkv_l2(P, l, li, B, W):
    blk = P.block
    xnb, otb, ps = B["xnb"], B["otb"], B["ps"]
    rows, rowsb, hr = B["rows"], B["rowsb"], B["hr"]
    onec, oner = B["onec"], B["oner"]
    qpack, kpack, vpack = B["qpack"], B["kpack"], B["vpack"]
    sap = P.sem(f"ln_ap_{li}")
    sap_done = sap.n

    sw = P.sem(f"a2_w_{li}", kind="s")
    swq = P.pair(f"a2_wq_{li}", kind="s")
    svw = P.sem(f"a2_vw_{li}", kind="s")
    smm = P.sem(f"a2_mm_{li}")
    sev = P.sem(f"a2_ev_{li}")
    ssq = P.sem(f"a2_sq_{li}")
    srs = P.sem(f"a2_rs_{li}")
    snx = P.sem(f"a2_nx_{li}")
    sst = P.pair(f"a2_st_{li}")
    svm = P.sem(f"a2_vm_{li}")
    sve = P.sem(f"a2_ve_{li}")
    svs = P.pair(f"a2_vs_{li}")
    srow1 = P.sem(f"a2_row1_{li}")
    slm = P.sem(f"a2_lm_{li}")
    b = {s: s.n for s in [sw, svw, smm, sev, ssq, srs, snx, svm, sve, srow1, slm]}
    bwq, bst, bvs = swq.base(), sst.base(), svs.base()

    @blk.gpsimd
    def _(g):
        g.wait_ge(sap.h, sap_done)
        for i, nm in enumerate(["lq1", "lk1", "lq2", "lk2"]):
            sw.inc(g.dma_start(out=rows["r1"][0:1, 64 * i:64 * i + 64],
                               in_=W[(l, nm)]), 16)
        for oc in range(32):
            wsrc = W[(l, "wq")] if oc < 16 else W[(l, "wk")]
            col0 = (oc % 16) * 128
            if oc >= 2:
                g.wait_ge(smm.h, b[smm] + (oc - 1) * NT)
            swq.inc(g.dma_start(
                out=otb[:, 5664 + (oc % 2) * 1024:5664 + (oc % 2) * 1024 + 1024].rearrange(
                    "p (k m) -> p k m", k=8),
                in_=wsrc[:, col0:col0 + 128].rearrange("(k p) m -> p k m", p=128)), oc)
        for vc in range(2):
            svw.inc(g.dma_start(
                out=otb[:, 8192 + vc * 4096:8192 + (vc + 1) * 4096].rearrange(
                    "p (k m) -> p k m", k=8),
                in_=W[(l, "wv")][:, vc * 512:(vc + 1) * 512].rearrange(
                    "(k p) m -> p k m", p=128)), 16)

    @blk.vector
    def _(v):
        v.wait_ge(sap.h, sap_done)
        for i in range(2):
            srow1.inc(v.memset(otb[:, 3584 + i * 1040 + 64:3584 + i * 1040 + 1040:65], 1.0))
        v.memset(hr["h1"][0:1, 0:32], 0.0)
        v.memset(hr["h2"][0:1, 0:32], 0.0)
        v.wait_ge(sw.h, b[sw] + 64)
        P.ve(v, v.tensor_mul(out=rows["r5"][0:1, 0:64], in0=rows["r1"][0:1, 0:64],
                             in1=rows["r1"][0:1, 64:128]))
        P.ve(v, v.reduce_sum(out=rows["r5"][0:1, 256:257], in_=rows["r5"][0:1, 0:64],
                             axis=AX.X))
        P.ve(v, v.tensor_mul(out=rows["r5"][0:1, 64:128], in0=rows["r1"][0:1, 128:192],
                             in1=rows["r1"][0:1, 192:256]))
        slm.inc(v.reduce_sum(out=rows["r5"][0:1, 257:258], in_=rows["r5"][0:1, 64:128],
                             axis=AX.X))

    @blk.scalar
    def _(s):
        s.wait_ge(slm.h, b[slm] + 1)
        slm.inc(s.activation(out=rows["r5"][0:1, 258:260], in_=rows["r5"][0:1, 256:258],
                             func=AF.Exp, bias=B["constc"][0:1, 2:3], scale=1.0))

    @blk.vector
    def _(v):
        v.wait_ge(slm.h, b[slm] + 2)
        P.ve(v, v.tensor_sub(out=rows["r5"][0:1, 260:261], in0=rows["r5"][0:1, 258:259],
                             in1=rows["r5"][0:1, 259:260]))
        P.ve(v, v.tensor_scalar(out=rows["r5"][0:1, 261:262], in0=rows["r5"][0:1, 260:261],
                                scalar1=L2_LAMBDA_INIT, scalar2=None, op0=OP.add))
        slm.inc(v.tensor_copy(out=B["rwb"][0:1, 34:35], in_=rows["r5"][0:1, 261:262]))

    @blk.tensor
    def _(q):
        q.wait_ge(slm.h, b[slm] + 3)
        slm.inc(q.matmul(out=ps[6][:, 0:1], lhsT=oner[0:1, 0:128],
                         rhs=B["rwb"][0:1, 34:35], start=True, stop=True))

    @blk.scalar
    def _(s):
        s.wait_ge(slm.h, b[slm] + 4)
        slm.inc(s.copy(out=B["bcol"][2][:, 0:1], in_=ps[6][:, 0:1]))

    @blk.tensor
    def _(q):
        for oc in range(32):
            swq.wait_done(q, bwq, oc)
            for tt in range(NT):
                step = oc * NT + tt
                pb = step % 2
                q.wait_ge(sap.h, sap_done - NT + tt + 1)
                if step >= 2:
                    q.wait_ge(sev.h, b[sev] + step - 1)
                wb0 = 5664 + (oc % 2) * 1024
                for c in range(NCH):
                    inst = q.matmul(out=ps[pb][:, 0:TT],
                                    lhsT=otb[:, wb0 + c * 128:wb0 + (c + 1) * 128],
                                    rhs=xnb[:, c * T + tt * TT:c * T + tt * TT + TT],
                                    start=(c == 0), stop=(c == NCH - 1))
                smm.inc(inst)
                q.wait_ge(ssq.h, b[ssq] + step + 1)
                if step >= 1:
                    q.wait_ge(snx.h, b[snx] + 2 * step)
                q.matmul(out=ps[2][0:1, 0:TT], lhsT=onec[0:64, 0:1],
                         rhs=otb[0:64, 512:512 + TT], start=True, stop=True,
                         tile_position=(0, 0))
                srs.inc(q.matmul(out=ps[3][0:1, 0:TT], lhsT=onec[64:128, 0:1],
                                 rhs=otb[64:128, 512:512 + TT], start=True, stop=True,
                                 tile_position=(64, 0)))

    @blk.scalar
    def _(s):
        for oc in range(32):
            for tt in range(NT):
                step = oc * NT + tt
                pb = step % 2
                s.wait_ge(smm.h, b[smm] + step + 1)
                if step >= 1:
                    s.wait_ge(ssq.h, b[ssq] + step)
                if step >= 2:
                    sst.wait_done(s, bst, step - 2)
                eb0 = (step % 2) * 2048
                sev.inc(s.activation(out=otb[:, eb0:eb0 + TT], in_=ps[pb][:, 0:TT],
                                     func=AF.Identity, bias=B["constc"][:, 2:3],
                                     scale=(0.125 if oc < 16 else 1.0)))

    @blk.vector
    def _(v):
        for oc in range(32):
            for tt in range(NT):
                step = oc * NT + tt
                v.wait_ge(sev.h, b[sev] + step + 1)
                eb0 = (step % 2) * 2048
                ssq.inc(v.tensor_mul(out=otb[:, 512:512 + TT], in0=otb[:, eb0:eb0 + TT],
                                     in1=otb[:, eb0:eb0 + TT]))
                v.wait_ge(srs.h, b[srs] + step + 1)
                hrow = hr["h1"] if oc < 16 else hr["h2"]
                sl = 2 * (oc % 16)
                P.ve(v, v.reduce_max(out=hr["h4"][0:1, 0:1], in_=ps[2][0:1, 0:TT],
                                     axis=AX.X))
                P.ve(v, v.tensor_max(out=hrow[0:1, sl:sl + 1], in0=hrow[0:1, sl:sl + 1],
                                     in1=hr["h4"][0:1, 0:1]))
                P.ve(v, v.reduce_max(out=hr["h4"][0:1, 1:2], in_=ps[3][0:1, 0:TT],
                                     axis=AX.X))
                snx.inc(v.tensor_max(out=hrow[0:1, sl + 1:sl + 2],
                                     in0=hrow[0:1, sl + 1:sl + 2],
                                     in1=hr["h4"][0:1, 1:2]), 2)

    @blk.sync
    def _(sy):
        for oc in range(32):
            pack = qpack if oc < 16 else kpack
            prow = (oc % 16) * 128
            for tt in range(NT):
                step = oc * NT + tt
                sy.wait_ge(ssq.h, b[ssq] + step + 1)
                eb0 = (step % 2) * 2048
                sst.inc(sy.dma_start(out=pack[prow:prow + 128, tt * TT:tt * TT + TT],
                                     in_=otb[:, eb0:eb0 + TT]), step)

    @blk.tensor
    def _(q):
        q.wait_ge(svw.h, b[svw] + 32)
        q.wait_ge(srow1.h, b[srow1] + 2)
        q.wait_ge(sap.h, sap_done)
        for tk in range(32):
            for vc in range(2):
                step = tk * 2 + vc
                pb = step % 2
                if step >= 2:
                    q.wait_ge(sve.h, b[sve] + step - 1)
                for c in range(NCH):
                    inst = q.matmul(
                        out=ps[pb][:, 0:TT],
                        lhsT=xnb[:, c * T + tk * 128:c * T + tk * 128 + 128],
                        rhs=otb[:, 8192 + vc * 4096 + c * 512:8192 + vc * 4096 + (c + 1) * 512],
                        start=(c == 0), stop=(c == NCH - 1))
                svm.inc(inst)

    @blk.scalar
    def _(s):
        for tk in range(32):
            for vc in range(2):
                step = tk * 2 + vc
                pb = step % 2
                s.wait_ge(svm.h, b[svm] + step + 1)
                if tk >= 2 and vc == 0:
                    svs.wait_done(s, bvs, tk - 2)
                vr0 = 3584 + (tk % 2) * 1040
                sve.inc(s.copy(
                    out=otb[:, vr0 + vc * 520:vr0 + (vc + 1) * 520].rearrange(
                        "p (h d) -> p h d", h=8)[:, :, 0:64],
                    in_=ps[pb][:, 0:TT].rearrange("p (h d) -> p h d", h=8)))

    @blk.sync
    def _(sy):
        for tk in range(32):
            sy.wait_ge(sve.h, b[sve] + 2 * (tk + 1))
            svs.inc(sy.dma_start(out=vpack[:, tk * 1040:(tk + 1) * 1040],
                                 in_=otb[:, 3584 + (tk % 2) * 1040:3584 + (tk % 2) * 1040 + 1040]),
                    tk)


def emit_ffn_l2(P, l, li, B, W):
    blk = P.block
    xnb, otb, ps = B["xnb"], B["otb"], B["ps"]
    evb, evb2 = B["evb"], B["evb2"]
    smallb, smallb2 = B["smallb"], B["smallb2"]
    hbuf, gbuf = B["hbuf"], B["gbuf"]
    sap = P.sem(f"ln_ap_{li}")
    sap_done = sap.n
    a2 = {nm: P.sem(f"a2_{nm}_{li}") for nm in ["mm", "vm", "ev", "ve", "sq"]}
    a2tot = {nm: s.n for nm, s in a2.items()}
    a2st = P.pair(f"a2_st_{li}")
    a2st_tot = a2st.total()
    a2vs = P.pair(f"a2_vs_{li}")
    a2vs_tot = a2vs.total()

    swg = P.pair(f"c2_wg_{li}", kind="s")
    swd = P.sem(f"c2_wd_{li}", kind="s")
    shl = P.pair(f"c2_hl_{li}")
    sAm = P.sem(f"c2_Am_{li}")
    sAe = P.sem(f"c2_Ae_{li}")
    sAh = P.sem(f"c2_Ah_{li}")
    sAs = P.pair(f"c2_As_{li}")
    sBm = P.sem(f"c2_Bm_{li}")
    sBe = P.sem(f"c2_Be_{li}")
    b = {s: s.n for s in [swd, sAm, sAe, sAh, sBm, sBe]}
    bswg, bshl, bsAs = swg.base(), shl.base(), sAs.base()
    sgw = P.pair(f"f_fw_{li}", kind="s")
    bgw = sgw.base()

    @blk.gpsimd
    def _(g):
        g.wait_ge(sap.h, sap_done)
        for hc in range(24):
            if hc >= 2:
                g.wait_ge(sAm.h, b[sAm] + 16 * (hc - 1))
            swg.inc(g.dma_start(
                out=otb[:, 16384 + (hc % 2) * 1024:16384 + (hc % 2) * 1024 + 1024].rearrange(
                    "p (k m) -> p k m", k=8),
                in_=W[(l, "wg_t")][:, hc * 128:(hc + 1) * 128].rearrange(
                    "(k p) m -> p k m", p=128)), 2 * hc)
            swg.inc(g.dma_start(
                out=otb[:, 18432 + (hc % 2) * 1024:18432 + (hc % 2) * 1024 + 1024].rearrange(
                    "p (k m) -> p k m", k=8),
                in_=W[(l, "wu_t")][:, hc * 128:(hc + 1) * 128].rearrange(
                    "(k p) m -> p k m", p=128)), 2 * hc + 1)
        g.wait_ge(a2st.s[0].h, a2st_tot[0])
        g.wait_ge(a2st.s[1].h, a2st_tot[1])
        g.wait_ge(a2vs.s[0].h, a2vs_tot[0])
        g.wait_ge(a2vs.s[1].h, a2vs_tot[1])
        g.wait_ge(sAs.s[0].h, bsAs[0] + 16 * 96)
        g.wait_ge(sAs.s[1].h, bsAs[1] + 16 * 96)
        for mc in range(8):
            swd.inc(g.dma_start(
                out=otb[:, mc * 3072:(mc + 1) * 3072].rearrange("p (k m) -> p k m", k=24),
                in_=W[(l, "wd_t")][:, mc * 128:(mc + 1) * 128].rearrange(
                    "(k p) m -> p k m", p=128)), 16)

    @blk.sync
    def _(sy):
        for hc in range(24):
            for tc in range(8):
                st = hc * 8 + tc
                sy.wait_ge(sAh.h, b[sAh] + st + 1)
                sAs.inc(sy.dma_start(
                    out=hbuf[hc * 128:(hc + 1) * 128, tc * TT:(tc + 1) * TT],
                    in_=otb[:, 20480 + (st % 2) * 512:20480 + (st % 2) * 512 + 512]), st)

    @blk.sync
    def _(sy):
        sy.wait_ge(sAm.h, b[sAm] + 384)
        sy.wait_ge(a2["mm"].h, a2tot["mm"])
        sy.wait_ge(a2["vm"].h, a2tot["vm"])
        for tc in range(8):
            if tc >= 2:
                sy.wait_ge(sBm.h, b[sBm] + 8 * (tc - 1))
            shl.inc(sy.dma_start(
                out=xnb[:, (tc % 2) * 12288:(tc % 2) * 12288 + 12288].rearrange(
                    "p (k m) -> p k m", k=24),
                in_=hbuf[:, tc * 512:(tc + 1) * 512].rearrange("(k p) m -> p k m", p=128)),
                tc)

    @blk.tensor
    def _(q):
        q.wait_ge(a2["ev"].h, a2tot["ev"])
        q.wait_ge(a2["ve"].h, a2tot["ve"])
        q.wait_ge(a2["sq"].h, a2tot["sq"])
        for hc in range(24):
            swg.wait_done(q, bswg, 2 * hc)
            swg.wait_done(q, bswg, 2 * hc + 1)
            for tc in range(8):
                st = hc * 8 + tc
                psg = ps[st % 2]
                psu = ps[2 + st % 2]
                if st >= 2:
                    q.wait_ge(sAe.h, b[sAe] + 2 * (st - 1))
                for c in range(NCH):
                    inst = q.matmul(
                        out=psg[:, 0:TT],
                        lhsT=otb[:, 16384 + (hc % 2) * 1024 + c * 128:
                                 16384 + (hc % 2) * 1024 + c * 128 + 128],
                        rhs=xnb[:, c * T + tc * TT:c * T + tc * TT + TT],
                        start=(c == 0), stop=(c == NCH - 1))
                sAm.inc(inst)
                for c in range(NCH):
                    inst = q.matmul(
                        out=psu[:, 0:TT],
                        lhsT=otb[:, 18432 + (hc % 2) * 1024 + c * 128:
                                 18432 + (hc % 2) * 1024 + c * 128 + 128],
                        rhs=xnb[:, c * T + tc * TT:c * T + tc * TT + TT],
                        start=(c == 0), stop=(c == NCH - 1))
                sAm.inc(inst)

        q.wait_ge(swd.h, b[swd] + 128)
        for tc in range(8):
            shl.wait_done(q, bshl, tc)
            for mc in range(8):
                st = tc * 8 + mc
                psb = ps[4 + st % 2]
                if st >= 2:
                    q.wait_ge(sBe.h, b[sBe] + st - 1)
                for kc in range(24):
                    inst = q.matmul(
                        out=psb[:, 0:TT],
                        lhsT=otb[:, mc * 3072 + kc * 128:mc * 3072 + kc * 128 + 128],
                        rhs=xnb[:, (tc % 2) * 12288 + kc * 512:(tc % 2) * 12288 + kc * 512 + 512],
                        start=(kc == 0), stop=(kc == 23))
                sBm.inc(inst)

    @blk.scalar
    def _(s):
        for hc in range(24):
            for tc in range(8):
                st = hc * 8 + tc
                s.wait_ge(sAm.h, b[sAm] + 2 * (st + 1))
                if st >= 1:
                    s.wait_ge(sAh.h, b[sAh] + st)
                s.activation(out=smallb[:, 0:TT], in_=ps[st % 2][:, 0:TT],
                             func=AF.Silu, bias=B["constc"][:, 2:3], scale=1.0 / 127.0)
                sAe.inc(s.activation(out=smallb2[:, 0:TT], in_=ps[2 + st % 2][:, 0:TT],
                                     func=AF.Identity, bias=B["constc"][:, 2:3],
                                     scale=1.0 / 127.0), 2)
        for tc in range(8):
            for mc in range(8):
                st = tc * 8 + mc
                s.wait_ge(sBm.h, b[sBm] + st + 1)
                if st >= 2:
                    sgw.wait_done(s, bgw, st - 2)
                sBe.inc(s.activation(out=(evb if st % 2 == 0 else evb2)[:, 0:TT],
                                     in_=ps[4 + st % 2][:, 0:TT], func=AF.Silu,
                                     bias=B["constc"][:, 2:3], scale=1.0 / 127.0))

    @blk.vector
    def _(v):
        for hc in range(24):
            for tc in range(8):
                st = hc * 8 + tc
                v.wait_ge(sAe.h, b[sAe] + 2 * (st + 1))
                if st >= 2:
                    sAs.wait_done(v, bsAs, st - 2)
                sAh.inc(v.tensor_mul(out=otb[:, 20480 + (st % 2) * 512:
                                             20480 + (st % 2) * 512 + 512],
                                     in0=smallb[:, 0:TT], in1=smallb2[:, 0:TT]))

    @blk.gpsimd
    def _(g):
        for tc in range(8):
            for mc in range(8):
                st = tc * 8 + mc
                g.wait_ge(sBe.h, b[sBe] + st + 1)
                sgw.inc(g.dma_start(
                    out=gbuf[mc * 128:(mc + 1) * 128, tc * TT:(tc + 1) * TT],
                    in_=(evb if st % 2 == 0 else evb2)[:, 0:TT]), st)


def emit_attn_l2(P, l, li, B, W):
    blk = P.block
    xnb, otb, ps = B["xnb"], B["otb"], B["ps"]
    evb, evb2 = B["evb"], B["evb2"]
    rows, rowsb, hr = B["rows"], B["rowsb"], B["hr"]
    oner = B["oner"]
    qpack, kpack, vpack = B["qpack"], B["kpack"], B["vpack"]
    ar3 = B["ar3"]
    sBm_g = P.sem(f"c2_Bm_{li}")
    sBm_tot = sBm_g.n
    snx_g = P.sem(f"a2_nx_{li}")
    snx_tot = snx_g.n
    slm_g = P.sem(f"a2_lm_{li}")
    slm_tot = slm_g.n
    gpair_tot = []
    for nm in ["st", "vs"]:
        pr = P.pair(f"a2_{nm}_{li}")
        gpair_tot += [(pr.s[0].h, pr.s[0].n), (pr.s[1].h, pr.s[1].n)]

    sld3 = P.sem(f"b2_ld_{li}", kind="s")
    ssc = P.sem(f"b2_sc_{li}")
    sex = P.sem(f"b2_ex_{li}")
    spv = P.sem(f"b2_pv_{li}")
    sdc = P.sem(f"b2_dc_{li}")
    sdn = P.sem(f"b2_dn_{li}")
    srb = P.sem(f"b2_rb_{li}")
    snv = P.sem(f"b2_nv_{li}")
    sno = P.sem(f"b2_no_{li}")
    sb1 = P.sem(f"b2_b1_{li}")
    sb2 = P.sem(f"b2_b2_{li}")
    sb3 = P.sem(f"b2_b3_{li}")
    sb4 = P.sem(f"b2_b4_{li}")
    sb5 = P.sem(f"b2_b5_{li}")
    b = {s: s.n for s in [sld3, ssc, sex, spv, sdc, sdn, srb, snv, sno,
                          sb1, sb2, sb3, sb4, sb5]}

    KH0 = lambda h: (h % 2) * 8192
    QH0 = lambda h: (h % 2) * 8192 + 4096
    VH0 = lambda h: 16384 + (h % 2) * 2080
    E1 = 20544
    E2 = 21568

    jobs = [(h, qg) for h in range(16) for qg in range(8)]
    NKT = 32
    SH = NKT * 8

    @blk.gpsimd
    def _(g):
        g.wait_ge(sBm_g.h, sBm_tot)
        for hh, nn in gpair_tot:
            g.wait_ge(hh, nn)
        for h in range(16):
            if h >= 2:
                g.wait_ge(ssc.h, b[ssc] + 2 * SH * (h - 1))
                g.wait_ge(spv.h, b[spv] + 2 * SH * (h - 1))
            sld3.inc(g.dma_start(out=xnb[:, KH0(h):KH0(h) + T],
                                 in_=kpack[128 * h:128 * h + 128, :]), 16)
            sld3.inc(g.dma_start(out=xnb[:, QH0(h):QH0(h) + T],
                                 in_=qpack[128 * h:128 * h + 128, :]), 16)
            sld3.inc(g.dma_start(
                out=xnb[:, VH0(h):VH0(h) + 2080].rearrange("p (k w) -> p k w", k=32),
                in_=vpack[:, :].rearrange("p (k g w) -> p k g w", k=32, g=16)[:, :, h, :]),
                16)

    @blk.vector
    def _(v):
        v.wait_ge(snx_g.h, snx_tot)
        sb1.inc(v.tensor_mul(out=hr["h4"][0:1, 0:32], in0=hr["h1"][0:1, 0:32],
                             in1=hr["h2"][0:1, 0:32]))

    @blk.scalar
    def _(s_):
        s_.wait_ge(sb1.h, b[sb1] + 1)
        sb2.inc(s_.activation(out=hr["h3"][0:1, 0:32], in_=hr["h4"][0:1, 0:32],
                              func=AF.Sqrt, scale=1.0))

    @blk.vector
    def _(v):
        v.wait_ge(sb2.h, b[sb2] + 1)
        P.ve(v, v.tensor_scalar(out=hr["h6"][0:1, 0:32], in0=hr["h3"][0:1, 0:32],
                                scalar1=-1.0, scalar2=None, op0=OP.mult))
        sb3.inc(v.tensor_copy(out=B["rwb"][0:1, 64:96], in_=hr["h6"][0:1, 0:32]))

    @blk.tensor
    def _(q):
        q.wait_ge(sb3.h, b[sb3] + 1)
        q.wait_ge(slm_g.h, slm_tot)
        sb4.inc(q.matmul(out=ps[6][:, 0:32], lhsT=oner[0:1, 0:128],
                         rhs=B["rwb"][0:1, 64:96], start=True, stop=True))

    @blk.scalar
    def _(s_):
        s_.wait_ge(sb4.h, b[sb4] + 1)
        sb5.inc(s_.copy(out=ar3[:, 2368:2400], in_=ps[6][:, 0:32]))

    @blk.tensor
    def _(q):
        for ji, (h, qg) in enumerate(jobs):
            q0 = qg * 512
            N = 512
            q.wait_ge(sld3.h, b[sld3] + 48 * (h + 1))

            def emit_pv(ki):
                s_ = ji * NKT + ki
                q.wait_ge(sex.h, b[sex] + 2 * (s_ + 1))
                if ki == 0:
                    q.wait_ge(snv.h, b[snv] + 2 * ji)
                first = (ki == 0)
                last = (ki == NKT - 1)
                q.matmul(out=ps[2][0:65, 0:N],
                         lhsT=xnb[:, VH0(h) + ki * 65:VH0(h) + ki * 65 + 65],
                         rhs=xnb[:, E1 + (s_ % 2) * 512:E1 + (s_ % 2) * 512 + N],
                         start=first, stop=last)
                spv.inc(q.matmul(out=ps[5][0:65, 0:N],
                                 lhsT=xnb[:, VH0(h) + ki * 65:VH0(h) + ki * 65 + 65],
                                 rhs=xnb[:, E2 + (s_ % 2) * 512:E2 + (s_ % 2) * 512 + N],
                                 start=first, stop=last), 2)

            for ki in range(NKT):
                s = ji * NKT + ki
                bA = ps[0 + 3 * (s % 2)]
                bB = ps[1 + 3 * (s % 2)]
                if s >= 2:
                    q.wait_ge(sex.h, b[sex] + 2 * (s - 1))
                q.matmul(out=bA[:, 0:N],
                         lhsT=xnb[0:64, KH0(h) + ki * 128:KH0(h) + ki * 128 + 128],
                         rhs=xnb[0:64, QH0(h) + q0:QH0(h) + q0 + N], start=True,
                         stop=True, tile_position=(0, 0))
                ssc.inc(q.matmul(out=bB[:, 0:N],
                                 lhsT=xnb[64:128, KH0(h) + ki * 128:KH0(h) + ki * 128 + 128],
                                 rhs=xnb[64:128, QH0(h) + q0:QH0(h) + q0 + N], start=True,
                                 stop=True, tile_position=(64, 0)), 2)
                if ki >= 1:
                    emit_pv(ki - 1)
            emit_pv(NKT - 1)
            q.wait_ge(sdn.h, b[sdn] + ji + 1)
            q.matmul(out=ps[6][0:64, 0:N], lhsT=oner[0:1, 0:64],
                     rhs=rowsb["b5"][0:1, 0:N], start=True, stop=False)
            q.matmul(out=ps[6][0:64, 0:N], lhsT=oner[0:1, 0:64],
                     rhs=rowsb["b6"][0:1, 0:N], start=False, stop=True)
            q.matmul(out=ps[6][64:128, 0:N], lhsT=oner[0:1, 0:64],
                     rhs=rowsb["b5"][0:1, 512:512 + N], start=True, stop=False)
            srb.inc(q.matmul(out=ps[6][64:128, 0:N], lhsT=oner[0:1, 0:64],
                             rhs=rowsb["b6"][0:1, 512:512 + N], start=False, stop=True))

    @blk.vector
    def _(v):
        for ji, (h, qg) in enumerate(jobs):
            q0 = qg * 512
            N = 512
            v.wait_ge(spv.h, b[spv] + 2 * (ji + 1) * NKT)
            P.ve(v, v.tensor_copy(out=rows["r1"][0:1, 0:N], in_=ps[2][64:65, 0:N]))
            sdc.inc(v.tensor_copy(out=rows["r2"][0:1, 0:N], in_=ps[5][64:65, 0:N]))
            P.ve(v, v.reciprocal(out=rows["r3"][0:1, 0:N], in_=rows["r1"][0:1, 0:N]))
            P.ve(v, v.reciprocal(out=rows["r4"][0:1, 0:N], in_=rows["r2"][0:1, 0:N]))
            if ji >= 1:
                v.wait_ge(srb.h, b[srb] + ji)
            P.ve(v, v.tensor_copy(out=rowsb["b5"][0:1, 0:N], in_=rows["r3"][0:1, 0:N]))
            P.ve(v, v.tensor_sub(out=rows["r5"][0:1, 0:N], in0=rows["r3"][0:1, 0:N],
                                 in1=rowsb["b5"][0:1, 0:N]))
            P.ve(v, v.tensor_copy(out=rowsb["b6"][0:1, 0:N], in_=rows["r5"][0:1, 0:N]))
            P.ve(v, v.tensor_copy(out=rowsb["b5"][0:1, 512:512 + N], in_=rows["r4"][0:1, 0:N]))
            P.ve(v, v.tensor_sub(out=rows["r5"][0:1, 0:N], in0=rows["r4"][0:1, 0:N],
                                 in1=rowsb["b5"][0:1, 512:512 + N]))
            sdn.inc(v.tensor_copy(out=rowsb["b6"][0:1, 512:512 + N],
                                  in_=rows["r5"][0:1, 0:N]))
            v.wait_ge(snv.h, b[snv] + 2 * (ji + 1))
            v.wait_ge(srb.h, b[srb] + ji + 1)
            P.ve(v, v.tensor_mul(out=evb[64:128, 0:N], in0=evb[0:64, 0:N],
                                 in1=ps[6][0:64, 0:N]))
            P.ve(v, v.tensor_mul(out=evb2[64:128, 0:N], in0=evb2[0:64, 0:N],
                                 in1=ps[6][64:128, 0:N]))
            P.ve(v, v.tensor_scalar(out=evb2[64:128, 0:N], in0=evb2[64:128, 0:N],
                                    scalar1=B["bcol"][2][64:128, 0:1], scalar2=None,
                                    op0=OP.mult))
            r0 = 64 * (h % 2)
            sno.inc(v.tensor_sub(
                out=otb[r0:r0 + 64, (h // 2) * T + q0:(h // 2) * T + q0 + N],
                in0=evb[64:128, 0:N], in1=evb2[64:128, 0:N]))

    @blk.scalar
    def _(s_):
        s_.wait_ge(sb5.h, b[sb5] + 1)
        for ji, (h, qg) in enumerate(jobs):
            for ki in range(NKT):
                s = ji * NKT + ki
                s_.wait_ge(ssc.h, b[ssc] + 2 * (s + 1))
                if s >= 2:
                    s_.wait_ge(spv.h, b[spv] + 2 * (s - 1))
                bA = ps[0 + 3 * (s % 2)]
                bB = ps[1 + 3 * (s % 2)]
                s_.activation(out=xnb[:, E1 + (s % 2) * 512:E1 + (s % 2) * 512 + 512],
                              in_=bA[:, 0:512], func=AF.Exp,
                              bias=ar3[:, 2368 + 2 * h:2368 + 2 * h + 1], scale=1.0)
                sex.inc(s_.activation(out=xnb[:, E2 + (s % 2) * 512:E2 + (s % 2) * 512 + 512],
                                      in_=bB[:, 0:512], func=AF.Exp,
                                      bias=ar3[:, 2368 + 2 * h + 1:2368 + 2 * h + 2],
                                      scale=1.0), 2)
            s_.wait_ge(sdc.h, b[sdc] + ji + 1)
            if ji >= 1:
                s_.wait_ge(sno.h, b[sno] + ji)
            s_.copy(out=evb[0:64, 0:512], in_=ps[2][0:64, 0:512])
            snv.inc(s_.copy(out=evb2[0:64, 0:512], in_=ps[5][0:64, 0:512]), 2)


def emit_proj_update_l2(P, l, li, xi, xo, B, W):
    blk = P.block
    xnb, otb, ps, ar3 = B["xnb"], B["otb"], B["ps"], B["ar3"]
    evb = B["evb"]
    rows, rowsb = B["rows"], B["rowsb"]
    onec, oner = B["onec"], B["oner"]
    cstb = B["cstb"]
    biasA = B["biasA"]
    sno = P.sem(f"b2_no_{li}")
    sno_done = sno.n
    for nm in ["sc", "ex", "pv"]:
        pass
    ssc_g = P.sem(f"b2_sc_{li}")
    ssc_tot = ssc_g.n
    sex_g = P.sem(f"b2_ex_{li}")
    sex_tot = sex_g.n
    sg = P.pair(f"f_fw_{li}")
    sg_tot = sg.total()

    sw = P.sem(f"p4_w_{li}", kind="s")
    sgl = P.pair(f"p4_g_{li}")
    sxl = P.pair(f"p4_x_{li}")
    smm = P.sem(f"p4_mm_{li}")
    sev = P.sem(f"p4_ev_{li}")
    ssl = P.sem(f"p4_sl_{li}")
    sln = P.sem(f"p4_ln_{li}")
    slr = P.sem(f"p4_lr_{li}")
    srs2 = P.sem(f"p4_rs_{li}")
    srw2 = P.sem(f"p4_rw_{li}")
    slb = P.sem(f"p4_lb_{li}")
    sdv = P.sem(f"p4_dv_{li}")
    supd = P.pair(f"upd_done_{li}", kind="s")
    b = {s: s.n for s in [sw, smm, sev, ssl, sln, slr, srs2, srw2, slb, sdv]}
    bg, bx2, bupd = sgl.base(), sxl.base(), supd.base()

    @blk.gpsimd
    def _(g):
        g.wait_ge(sno.h, sno_done)
        g.wait_ge(ssc_g.h, ssc_tot)
        g.wait_ge(sex_g.h, sex_tot)
        sw.inc(g.dma_start(
            out=xnb[:, 0:8192].rearrange("p (k m) -> p k m", k=8),
            in_=W[(l, "wproj")].rearrange("(k p) m -> p k m", p=128)), 16)
        sw.inc(g.dma_start(out=biasA[:, 0:8], in_=W[(l, "gamma")]), 16)
        sw.inc(g.dma_start(out=biasA[:, 8:16], in_=W[(l, "beta")]), 16)

    @blk.sync
    def _(sy):
        sy.wait_ge(sg.s[0].h, sg_tot[0])
        sy.wait_ge(sg.s[1].h, sg_tot[1])
        for tc in range(NT):
            for mc in range(NCH):
                st = tc * NCH + mc
                pb = st % 2
                if st >= 2:
                    sy.wait_ge(sdv.h, b[sdv] + 2 * (st - 1))
                sgl.inc(sy.dma_start(out=ar3[:, 4096 + pb * 512:4096 + pb * 512 + 512],
                             in_=B["gbuf"][mc * 128:(mc + 1) * 128, tc * TT:(tc + 1) * TT]),
                        st)
                sxl.inc(sy.dma_start(out=ar3[:, 5120 + pb * 512:5120 + pb * 512 + 512],
                                     in_=xi[mc * 128:(mc + 1) * 128, tc * TT:(tc + 1) * TT]),
                        st)

    @blk.tensor
    def _(q):
        q.wait_ge(sw.h, b[sw] + 48)
        for tc in range(NT):
            for mc in range(NCH):
                st = tc * NCH + mc
                pb = st % 2
                if st >= 2:
                    q.wait_ge(sev.h, b[sev] + st - 1)
                for fc in range(NCH):
                    inst = q.matmul(
                        out=ps[pb][:, 0:TT],
                        lhsT=xnb[:, fc * 1024 + mc * 128:fc * 1024 + mc * 128 + 128],
                        rhs=otb[:, fc * T + tc * TT:fc * T + tc * TT + TT],
                        start=(fc == 0), stop=(fc == NCH - 1))
                smm.inc(inst)
            q.wait_ge(ssl.h, b[ssl] + 8 * (tc + 1))
            if tc >= 1:
                q.wait_ge(slr.h, b[slr] + tc)
            for mc in range(NCH):
                q.matmul(out=ps[4][0:1, 0:TT], lhsT=onec[:, 0:1],
                         rhs=cstb[:, mc * 512:mc * 512 + TT],
                         start=(mc == 0), stop=(mc == NCH - 1))
            for mc in range(NCH):
                inst = q.matmul(out=ps[5][0:1, 0:TT], lhsT=onec[:, 0:1],
                                rhs=cstb[:, 4096 + mc * 512:4096 + mc * 512 + TT],
                                start=(mc == 0), stop=(mc == NCH - 1))
            sln.inc(inst)
            q.wait_ge(srw2.h, b[srw2] + tc + 1)
            if tc >= 1:
                q.wait_ge(sdv.h, b[sdv] + 2 * 8 * tc)
            q.matmul(out=ps[2][:, 0:TT], lhsT=oner[0:1, 0:128],
                     rhs=rowsb["b1"][0:1, 0:TT], start=True, stop=False)
            q.matmul(out=ps[2][:, 0:TT], lhsT=oner[0:1, 0:128],
                     rhs=rowsb["b2"][0:1, 0:TT], start=False, stop=True)
            q.matmul(out=ps[3][:, 0:TT], lhsT=oner[0:1, 0:128],
                     rhs=rowsb["b3"][0:1, 0:TT], start=True, stop=False)
            slb.inc(q.matmul(out=ps[3][:, 0:TT], lhsT=oner[0:1, 0:128],
                             rhs=rowsb["b4"][0:1, 0:TT], start=False, stop=True))

    @blk.scalar
    def _(s):
        for tc in range(NT):
            for mc in range(NCH):
                st = tc * NCH + mc
                pb = st % 2
                s.wait_ge(smm.h, b[smm] + st + 1)
                if tc >= 1:
                    s.wait_ge(sdv.h, b[sdv] + 2 * (8 * (tc - 1) + mc + 1))
                sev.inc(s.activation(out=cstb[:, mc * 512:mc * 512 + TT],
                                     in_=ps[pb][:, 0:TT], func=AF.Identity,
                                     bias=B["constc"][:, 2:3], scale=1.0))
            s.wait_ge(slr.h, b[slr] + tc + 1)
            srs2.inc(s.activation(out=rows["r3"][0:1, 0:TT], in_=rows["r2"][0:1, 0:TT],
                                  func=AF.Sqrt, bias=B["constc"][0:1, 0:1], scale=1.0))

    @blk.vector
    def _(v):
        for tc in range(NT):
            for mc in range(NCH):
                st = tc * NCH + mc
                v.wait_ge(sev.h, b[sev] + st + 1)
                ssl.inc(v.tensor_mul(out=cstb[:, 4096 + mc * 512:4096 + mc * 512 + TT],
                                     in0=cstb[:, mc * 512:mc * 512 + TT],
                                     in1=cstb[:, mc * 512:mc * 512 + TT]))
            v.wait_ge(sln.h, b[sln] + tc + 1)
            P.ve(v, v.tensor_scalar(out=rows["r1"][0:1, 0:TT], in0=ps[4][0:1, 0:TT],
                                    scalar1=1.0 / 1024.0, scalar2=None, op0=OP.mult))
            P.ve(v, v.tensor_mul(out=rows["r5"][0:1, 0:TT], in0=rows["r1"][0:1, 0:TT],
                                 in1=rows["r1"][0:1, 0:TT]))
            P.ve(v, v.tensor_scalar(out=rows["r2"][0:1, 0:TT], in0=ps[5][0:1, 0:TT],
                                    scalar1=1.0 / 1024.0, scalar2=None, op0=OP.mult))
            P.ve(v, v.tensor_sub(out=rows["r2"][0:1, 0:TT], in0=rows["r2"][0:1, 0:TT],
                                 in1=rows["r5"][0:1, 0:TT]))
            slr.inc(v.engine_nop())
            if tc >= 1:
                v.wait_ge(slb.h, b[slb] + tc)
            P.ve(v, v.tensor_copy(out=rowsb["b1"][0:1, 0:TT], in_=rows["r1"][0:1, 0:TT]))
            P.ve(v, v.tensor_sub(out=rows["r5"][0:1, 0:TT], in0=rows["r1"][0:1, 0:TT],
                                 in1=rowsb["b1"][0:1, 0:TT]))
            P.ve(v, v.tensor_copy(out=rowsb["b2"][0:1, 0:TT], in_=rows["r5"][0:1, 0:TT]))
            v.wait_ge(srs2.h, b[srs2] + tc + 1)
            P.ve(v, v.reciprocal(out=rows["r4"][0:1, 0:TT], in_=rows["r3"][0:1, 0:TT]))
            P.ve(v, v.tensor_copy(out=rowsb["b3"][0:1, 0:TT], in_=rows["r4"][0:1, 0:TT]))
            P.ve(v, v.tensor_sub(out=rows["r5"][0:1, 0:TT], in0=rows["r4"][0:1, 0:TT],
                                 in1=rowsb["b3"][0:1, 0:TT]))
            srw2.inc(v.tensor_copy(out=rowsb["b4"][0:1, 0:TT], in_=rows["r5"][0:1, 0:TT]))
            v.wait_ge(slb.h, b[slb] + tc + 1)
            for mc in range(NCH):
                st = tc * NCH + mc
                pb = st % 2
                sgl.wait_done(v, bg, st)
                sxl.wait_done(v, bx2, st)
                if st >= 2:
                    supd.wait_done(v, bupd, st - 2)
                P.ve(v, v.tensor_sub(out=evb[:, 0:TT], in0=cstb[:, mc * 512:mc * 512 + TT],
                                     in1=ps[2][:, 0:TT]))
                P.ve(v, v.tensor_mul(out=evb[:, 0:TT], in0=evb[:, 0:TT],
                                     in1=ps[3][:, 0:TT]))
                P.ve(v, v.tensor_scalar(out=evb[:, 0:TT], in0=evb[:, 0:TT],
                                        scalar1=biasA[:, mc:mc + 1],
                                        scalar2=biasA[:, 8 + mc:9 + mc],
                                        op0=OP.mult, op1=OP.add))
                P.ve(v, v.tensor_mul(out=evb[:, 0:TT], in0=evb[:, 0:TT],
                                     in1=ar3[:, 4096 + pb * 512:4096 + pb * 512 + 512]))
                sdv.inc(v.tensor_add(out=ar3[:, pb * 512:pb * 512 + 512],
                                     in0=evb[:, 0:TT],
                                     in1=ar3[:, 5120 + pb * 512:5120 + pb * 512 + 512]), 2)

    @blk.gpsimd
    def _(g):
        for tc in range(NT):
            for mc in range(NCH):
                st = tc * NCH + mc
                pb = st % 2
                g.wait_ge(sdv.h, b[sdv] + 2 * (st + 1))
                supd.inc(g.dma_start(
                    out=xo[mc * 128:(mc + 1) * 128, tc * TT:(tc + 1) * TT],
                    in_=ar3[:, pb * 512:pb * 512 + 512]), st)



def emit_qkv_l3(P, l, li, B, W):
    blk = P.block
    xnb, otb, ps = B["xnb"], B["otb"], B["ps"]
    onec, oner = B["onec"], B["oner"]
    qpack, kpack, vpack = B["qpack"], B["kpack"], B["vpack"]
    sap = P.sem(f"ln_ap_{li}")
    sap_done = sap.n
    p2_sym = P.sem(f"f_ym_{li}")
    p2_done_mm = p2_sym.n
    p2_fw = P.pair(f"f_fw_{li}")
    p2_fw_tot = p2_fw.total()
    p2_hd = P.sem(f"f_hd_{li}")
    p2_done_hd = p2_hd.n

    swq = P.pair(f"a3_wq_{li}", kind="s")
    svw = P.sem(f"a3_vw_{li}", kind="s")
    smm = P.sem(f"a3_mm_{li}")
    sev = P.sem(f"a3_ev_{li}")
    sst = P.pair(f"a3_st_{li}")
    svm = P.sem(f"a3_vm_{li}")
    sve = P.sem(f"a3_ve_{li}")
    svs = P.pair(f"a3_vs_{li}")
    srow1 = P.sem(f"a3_row1_{li}")
    b = {s: s.n for s in [svw, smm, sev, svm, sve, srow1]}
    bwq, bst, bvs = swq.base(), sst.base(), svs.base()

    @blk.gpsimd
    def _(g):
        g.wait_ge(p2_sym.h, p2_done_mm)
        g.wait_ge(p2_hd.h, p2_done_hd)
        g.wait_ge(p2_fw.s[0].h, p2_fw_tot[0])
        g.wait_ge(p2_fw.s[1].h, p2_fw_tot[1])
        for oc in range(16):
            col0 = oc * 128
            if oc >= 2:
                g.wait_ge(smm.h, b[smm] + (oc - 1) * NT)
            swq.inc(g.dma_start(
                out=otb[:, 5664 + (oc % 2) * 1024:5664 + (oc % 2) * 1024 + 1024].rearrange(
                    "p (k m) -> p k m", k=8),
                in_=W[(l, "wqkv")][:, col0:col0 + 128].rearrange("(k p) m -> p k m", p=128)),
                oc)
        for vc in range(2):
            svw.inc(g.dma_start(
                out=otb[:, 8192 + vc * 4096:8192 + (vc + 1) * 4096].rearrange(
                    "p (k m) -> p k m", k=8),
                in_=W[(l, "wqkv")][:, 2048 + vc * 512:2048 + (vc + 1) * 512].rearrange(
                    "(k p) m -> p k m", p=128)), 16)

    @blk.vector
    def _(v):
        v.wait_ge(sap.h, sap_done)
        v.wait_ge(p2_sym.h, p2_done_mm)
        v.wait_ge(p2_hd.h, p2_done_hd)
        for i in range(2):
            srow1.inc(v.memset(otb[:, 3584 + i * 1040 + 64:3584 + i * 1040 + 1040:65], 1.0))

    @blk.tensor
    def _(q):
        for oc in range(16):
            swq.wait_done(q, bwq, oc)
            for tt in range(NT):
                step = oc * NT + tt
                pb = step % 2
                q.wait_ge(sap.h, sap_done - NT + tt + 1)
                if step >= 2:
                    q.wait_ge(sev.h, b[sev] + step - 1)
                wb0 = 5664 + (oc % 2) * 1024
                for c in range(NCH):
                    inst = q.matmul(out=ps[pb][:, 0:TT],
                                    lhsT=otb[:, wb0 + c * 128:wb0 + (c + 1) * 128],
                                    rhs=xnb[:, c * T + tt * TT:c * T + tt * TT + TT],
                                    start=(c == 0), stop=(c == NCH - 1))
                smm.inc(inst)

    @blk.scalar
    def _(s):
        for oc in range(16):
            for tt in range(NT):
                step = oc * NT + tt
                pb = step % 2
                s.wait_ge(smm.h, b[smm] + step + 1)
                if step >= 2:
                    sst.wait_done(s, bst, step - 2)
                eb0 = (step % 2) * 2048
                sev.inc(s.copy(out=otb[:, eb0:eb0 + TT], in_=ps[pb][:, 0:TT]))

    @blk.sync
    def _(sy):
        for oc in range(16):
            pack = qpack if oc < 8 else kpack
            prow = (oc % 8) * 128
            for tt in range(NT):
                step = oc * NT + tt
                sy.wait_ge(sev.h, b[sev] + step + 1)
                eb0 = (step % 2) * 2048
                sst.inc(sy.dma_start(out=pack[prow:prow + 128, tt * TT:tt * TT + TT],
                                     in_=otb[:, eb0:eb0 + TT]), step)

    @blk.tensor
    def _(q):
        q.wait_ge(svw.h, b[svw] + 32)
        q.wait_ge(srow1.h, b[srow1] + 2)
        q.wait_ge(sap.h, sap_done)
        for tk in range(32):
            for vc in range(2):
                step = tk * 2 + vc
                pb = step % 2
                if step >= 2:
                    q.wait_ge(sve.h, b[sve] + step - 1)
                for c in range(NCH):
                    inst = q.matmul(
                        out=ps[pb][:, 0:TT],
                        lhsT=xnb[:, c * T + tk * 128:c * T + tk * 128 + 128],
                        rhs=otb[:, 8192 + vc * 4096 + c * 512:8192 + vc * 4096 + (c + 1) * 512],
                        start=(c == 0), stop=(c == NCH - 1))
                svm.inc(inst)

    @blk.scalar
    def _(s):
        for tk in range(32):
            for vc in range(2):
                step = tk * 2 + vc
                pb = step % 2
                s.wait_ge(svm.h, b[svm] + step + 1)
                if tk >= 2 and vc == 0:
                    svs.wait_done(s, bvs, tk - 2)
                vr0 = 3584 + (tk % 2) * 1040
                sve.inc(s.copy(
                    out=otb[:, vr0 + vc * 520:vr0 + (vc + 1) * 520].rearrange(
                        "p (h d) -> p h d", h=8)[:, :, 0:64],
                    in_=ps[pb][:, 0:TT].rearrange("p (h d) -> p h d", h=8)))

    @blk.sync
    def _(sy):
        for tk in range(32):
            sy.wait_ge(sve.h, b[sve] + 2 * (tk + 1))
            svs.inc(sy.dma_start(out=vpack[:, tk * 1040:(tk + 1) * 1040],
                                 in_=otb[:, 3584 + (tk % 2) * 1040:3584 + (tk % 2) * 1040 + 1040]),
                    tk)


def emit_attn_l3(P, l, li, B, W):
    blk = P.block
    xnb, otb, ps, evb = B["xnb"], B["otb"], B["ps"], B["evb"]
    rows, rowsb = B["rows"], B["rowsb"]
    oner = B["oner"]
    qpack, kpack, vpack = B["qpack"], B["kpack"], B["vpack"]
    identb = B["identb"]
    gates = []
    for nm in ["mm", "vm", "ev", "ve"]:
        s = P.sem(f"a3_{nm}_{li}")
        gates.append((s.h, s.n))
    for nm in ["st", "vs"]:
        pr = P.pair(f"a3_{nm}_{li}")
        gates += [(pr.s[0].h, pr.s[0].n), (pr.s[1].h, pr.s[1].n)]

    sld = P.sem(f"b3_ld_{li}", kind="s")
    szk = P.sem(f"b3_zk_{li}")
    snm3 = P.sem(f"b3_nm_{li}")
    sek = P.sem(f"b3_ek_{li}")
    skv = P.sem(f"b3_kv_{li}")
    skve = P.sem(f"b3_kve_{li}")
    szq = P.sem(f"b3_zq_{li}")
    snmq = P.sem(f"b3_nq_{li}")
    seq = P.sem(f"b3_eq_{li}")
    stp = P.sem(f"b3_tp_{li}")
    stpe = P.sem(f"b3_te_{li}")
    snd = P.sem(f"b3_nd_{li}")
    sdn3 = P.sem(f"b3_dn_{li}")
    srb3 = P.sem(f"b3_rb_{li}")
    snv3 = P.sem(f"b3_nv_{li}")
    sno = P.sem(f"b3_no_{li}")
    b = {s: s.n for s in [sld, szk, snm3, sek, skv, skve, szq, snmq, seq,
                          stp, stpe, snd, sdn3, srb3, snv3, sno]}

    KH = lambda h: (h % 2) * 8192
    QH = lambda h: (h % 2) * 8192 + 4096
    VH = lambda h: 16384 + (h % 2) * 2080
    PJ = lambda h: 20544 + (h % 2) * 256
    KP = lambda par: 21056 + par * 256
    QP = lambda par: 21568 + par * 256
    QT = lambda par: 22080 + par * 256
    KV = lambda h: 22592 + (h % 2) * 130

    @blk.gpsimd
    def _(g):
        for hh, nn in gates:
            g.wait_ge(hh, nn)
        for h in range(16):
            if h >= 2:
                g.wait_ge(szk.h, b[szk] + 32 * (h - 1))
                g.wait_ge(szq.h, b[szq] + 32 * (h - 1))
            sld.inc(g.dma_start(out=xnb[0:64, KH(h):KH(h) + T],
                                in_=kpack[64 * h:64 * h + 64, :]), 16)
            sld.inc(g.dma_start(out=xnb[0:64, QH(h):QH(h) + T],
                                in_=qpack[64 * h:64 * h + 64, :]), 16)
            sld.inc(g.dma_start(
                out=xnb[:, VH(h):VH(h) + 2080].rearrange("p (k w) -> p k w", k=32),
                in_=vpack[:, :].rearrange("p (k g w) -> p k g w", k=32, g=16)[:, :, h, :]),
                16)
            sld.inc(g.dma_start(out=xnb[0:64, PJ(h):PJ(h) + 256],
                                in_=W[(l, "proj_f")][h]), 16)

    @blk.tensor
    def _(q):
        for h in range(16):
            q.wait_ge(sld.h, b[sld] + 64 * (h + 1))

            def emit_kv(tk):
                cz = h * 32 + tk
                q.wait_ge(sek.h, b[sek] + cz + 1)
                first = (tk == 0)
                last = (tk == 31)
                q.matmul(out=ps[2][:, 0:65],
                         lhsT=xnb[:, KP(tk % 2):KP(tk % 2) + 128],
                         rhs=xnb[:, VH(h) + tk * 65:VH(h) + tk * 65 + 65],
                         start=first, stop=last)
                skv.inc(q.matmul(out=ps[3][:, 0:65],
                                 lhsT=xnb[:, KP(tk % 2) + 128:KP(tk % 2) + 256],
                                 rhs=xnb[:, VH(h) + tk * 65:VH(h) + tk * 65 + 65],
                                 start=first, stop=last), 2)

            for tk in range(32):
                cz = h * 32 + tk
                if cz >= 2:
                    q.wait_ge(sek.h, b[sek] + cz - 1)
                szk.inc(q.matmul(out=ps[cz % 2][:, 0:256],
                                 lhsT=xnb[0:64, KH(h) + tk * 128:KH(h) + tk * 128 + 128],
                                 rhs=xnb[0:64, PJ(h):PJ(h) + 256],
                                 start=True, stop=True))
                if tk >= 1:
                    emit_kv(tk - 1)
            emit_kv(31)
            q.wait_ge(skve.h, b[skve] + 2 * (h + 1))
            for tq in range(32):
                cq = h * 32 + tq
                q.wait_ge(seq.h, b[seq] + cq - 1) if cq >= 2 else None
                if tq == 0:
                    q.wait_ge(sek.h, b[sek] + 32 * (h + 1))
                szq.inc(q.matmul(out=ps[cq % 2][:, 0:256],
                                 lhsT=xnb[0:64, QH(h) + tq * 128:QH(h) + tq * 128 + 128],
                                 rhs=xnb[0:64, PJ(h):PJ(h) + 256],
                                 start=True, stop=True))
                q.wait_ge(seq.h, b[seq] + cq + 1)
                if cq >= 1:
                    q.wait_ge(srb3.h, b[srb3] + cq)
                q.matmul(out=ps[4][:, 0:128], lhsT=xnb[:, QP(tq % 2):QP(tq % 2) + 128],
                         rhs=identb[:, 0:128], start=True, stop=True)
                stp.inc(q.matmul(out=ps[5][:, 0:128],
                                 lhsT=xnb[:, QP(tq % 2) + 128:QP(tq % 2) + 256],
                                 rhs=identb[:, 0:128], start=True, stop=True), 2)
                q.wait_ge(stpe.h, b[stpe] + 2 * (cq + 1))
                if cq >= 1:
                    q.wait_ge(snv3.h, b[snv3] + cq)
                q.matmul(out=ps[6][0:65, 0:128], lhsT=xnb[:, KV(h):KV(h) + 65],
                         rhs=xnb[:, QT(cq % 2):QT(cq % 2) + 128], start=True, stop=False)
                snd.inc(q.matmul(out=ps[6][0:65, 0:128], lhsT=xnb[:, KV(h) + 65:KV(h) + 130],
                                 rhs=xnb[:, QT(cq % 2) + 128:QT(cq % 2) + 256],
                                 start=False, stop=True))
                q.wait_ge(sdn3.h, b[sdn3] + cq + 1)
                q.matmul(out=ps[4][0:64, 0:128], lhsT=oner[0:1, 0:64],
                         rhs=rowsb["b5"][0:1, 0:128], start=True, stop=False)
                srb3.inc(q.matmul(out=ps[4][0:64, 0:128], lhsT=oner[0:1, 0:64],
                                  rhs=rowsb["b6"][0:1, 0:128], start=False, stop=True))

    @blk.vector
    def _(v):
        for h in range(16):
            for tk in range(32):
                cz = h * 32 + tk
                v.wait_ge(szk.h, b[szk] + cz + 1)
                if cz >= 1:
                    v.wait_ge(sek.h, b[sek] + cz)
                P.ve(v, v.reduce_max(out=B["bcol"][3][:, 0:1], in_=ps[cz % 2][:, 0:256],
                                     axis=AX.X))
                snm3.inc(v.tensor_scalar(out=B["bcol"][3][:, 0:1], in0=B["bcol"][3][:, 0:1],
                                         scalar1=-1.0, scalar2=None, op0=OP.mult))
            for tq in range(32):
                cq = h * 32 + tq
                v.wait_ge(szq.h, b[szq] + cq + 1)
                v.wait_ge(seq.h, b[seq] + cq)
                if cq == 0:
                    v.wait_ge(sek.h, b[sek] + 32 * 16) if h == 15 else None
                P.ve(v, v.reduce_max(out=B["bcol"][3][:, 0:1], in_=ps[cq % 2][:, 0:256],
                                     axis=AX.X))
                snmq.inc(v.tensor_scalar(out=B["bcol"][3][:, 0:1], in0=B["bcol"][3][:, 0:1],
                                         scalar1=-1.0, scalar2=None, op0=OP.mult))
                v.wait_ge(snd.h, b[snd] + cq + 1)
                P.ve(v, v.tensor_copy(out=rows["r1"][0:1, 0:128], in_=ps[6][64:65, 0:128]))
                P.ve(v, v.tensor_scalar(out=rows["r1"][0:1, 0:128], in0=rows["r1"][0:1, 0:128],
                                        scalar1=1e-6, scalar2=None, op0=OP.add))
                P.ve(v, v.reciprocal(out=rows["r3"][0:1, 0:128], in_=rows["r1"][0:1, 0:128]))
                if cq >= 1:
                    v.wait_ge(srb3.h, b[srb3] + cq)
                P.ve(v, v.tensor_copy(out=rowsb["b5"][0:1, 0:128], in_=rows["r3"][0:1, 0:128]))
                P.ve(v, v.tensor_sub(out=rows["r5"][0:1, 0:128], in0=rows["r3"][0:1, 0:128],
                                     in1=rowsb["b5"][0:1, 0:128]))
                sdn3.inc(v.tensor_copy(out=rowsb["b6"][0:1, 0:128], in_=rows["r5"][0:1, 0:128]))
                v.wait_ge(snv3.h, b[snv3] + cq + 1)
                v.wait_ge(srb3.h, b[srb3] + cq + 1)
                r0 = 64 * (h % 2)
                t0 = (h // 2) * T + tq * 128
                sno.inc(v.tensor_mul(out=otb[r0:r0 + 64, t0:t0 + 128],
                                     in0=evb[0:64, 0:128], in1=ps[4][0:64, 0:128]))

    @blk.scalar
    def _(s_):
        for h in range(16):
            for tk in range(32):
                cz = h * 32 + tk
                s_.wait_ge(snm3.h, b[snm3] + cz + 1)
                if tk >= 2:
                    s_.wait_ge(skv.h, b[skv] + 2 * (cz - 1))
                sek.inc(s_.activation(out=xnb[:, KP(tk % 2):KP(tk % 2) + 256],
                                      in_=ps[cz % 2][:, 0:256], func=AF.Exp,
                                      bias=B["bcol"][3][:, 0:1], scale=1.0))
            s_.wait_ge(skv.h, b[skv] + 64 * (h + 1))
            if h >= 2:
                s_.wait_ge(snd.h, b[snd] + 32 * (h - 1))
            skve.inc(s_.copy(out=xnb[:, KV(h):KV(h) + 65], in_=ps[2][:, 0:65]))
            skve.inc(s_.copy(out=xnb[:, KV(h) + 65:KV(h) + 130], in_=ps[3][:, 0:65]))
            for tq in range(32):
                cq = h * 32 + tq
                s_.wait_ge(snmq.h, b[snmq] + cq + 1)
                if tq >= 2:
                    s_.wait_ge(stp.h, b[stp] + 2 * (cq - 1))
                seq.inc(s_.activation(out=xnb[:, QP(tq % 2):QP(tq % 2) + 256],
                                      in_=ps[cq % 2][:, 0:256], func=AF.Exp,
                                      bias=B["bcol"][3][:, 0:1], scale=1.0))
                s_.wait_ge(stp.h, b[stp] + 2 * (cq + 1))
                if cq >= 2:
                    s_.wait_ge(snd.h, b[snd] + cq - 1)
                s_.copy(out=xnb[:, QT(cq % 2):QT(cq % 2) + 128], in_=ps[4][:, 0:128])
                stpe.inc(s_.copy(out=xnb[:, QT(cq % 2) + 128:QT(cq % 2) + 256],
                                 in_=ps[5][:, 0:128]), 2)
                s_.wait_ge(snd.h, b[snd] + cq + 1)
                if cq >= 1:
                    s_.wait_ge(sno.h, b[sno] + cq)
                snv3.inc(s_.copy(out=evb[0:64, 0:128], in_=ps[6][0:64, 0:128]))
